# version 04
# baseline (speedup 1.0000x reference)
import contextlib

import numpy as np
import concourse.bass as bass
import concourse.mybir as mybir
from concourse.bass_utils import run_bass_kernel_spmd

F32 = mybir.dt.float32
BF16 = mybir.dt.bfloat16
AF = mybir.ActivationFunctionType
ALU = mybir.AluOpType
AX = mybir.AxisListType

T = 1024
D = 2048
NKC = 16
NE = 8
H = 16
NZC = 32
EPS_LN = 1e-5
EPS_RMS = 1.1920929e-07

WSLOT = (0, 8192, 16384)


class _Dummy:
    def __getattr__(self, name):
        def f(*a, **k):
            return None
        return f


_DUM = _Dummy()


class Prog:

    def __init__(self, emit, eng, sems):
        self.emit = emit
        self.eng = eng
        self.sems = sems
        self.cnt = {k: 0 for k in sems}
        self.ms = {}
        self.uid = 0

    def E(self, name):
        return self.eng if self.emit == name else _DUM

    @property
    def g(self):
        return self.E('g')

    @property
    def v(self):
        return self.E('v')

    @property
    def s(self):
        return self.E('s')

    @property
    def t(self):
        return self.E('t')

    @property
    def y(self):
        return self.E('y')

    def inc(self, inst, sem, k=1):
        if inst is not None:
            inst.then_inc(self.sems[sem], k)
        self.cnt[sem] += k

    def mark(self, name, sem):
        assert name not in self.ms, name
        self.ms[name] = (sem, self.cnt[sem])
        return name

    def mkid(self, tag):
        self.uid += 1
        return f"{tag}_{self.uid}"

    def mk(self, tag, sem):
        return self.mark(self.mkid(tag), sem)

    def wait(self, eng_name, msname):
        if msname is None:
            return
        sem, val = self.ms[msname]
        if self.emit == eng_name and val > 0:
            self.eng.wait_ge(self.sems[sem], val)


def build_program(debug=False):
    nc = bass.Bass("TRN2", target_bir_lowering=False, debug=False)

    def inp(name, shape):
        return nc.dram_tensor(name, list(shape), F32, kind="ExternalInput")

    xin = inp("xin", (D, T))
    wg = inp("wg", (128, 128))
    wih = inp("wih", (NE * 4, 128, 16 * 512))
    woh = inp("woh", (NE * 2, 128, 8 * 1024))
    wint = inp("wint", (69, 128, 16 * 128))
    wott = inp("wott", (32, 128, 2048))
    wh2it = inp("wh2it", (16, 4096))
    wct = inp("wct", (128, 128))
    wnt = inp("wnt", (128, 16))
    dtb = inp("dtb", (16, 1))
    alog = inp("alog", (16, 1))
    indt = inp("indt", (128, 512))
    ind2 = inp("ind2", (128, 32))
    rrep = inp("rrep", (16, 256))
    ones_f = inp("ones_f", (1, 128))
    ones_b = inp("ones_b", (1, 128))
    ones_c = inp("ones_c", (128, 1))
    id_f = inp("id_f", (128, 128))
    id_b = inp("id_b", (128, 128))
    selm = inp("selm", (128, 8))
    invselm = inp("invselm", (128, 8))
    selprev = inp("selprev", (128, 8))

    oxt = nc.dram_tensor("oxt", [D, T], F32, kind="ExternalOutput")

    dbg = {}
    if debug:
        dbg["d_xn"] = nc.dram_tensor("d_xn", [128, 16 * T], BF16, kind="ExternalOutput")
        dbg["d_g8"] = nc.dram_tensor("d_g8", [8, 1024], BF16, kind="ExternalOutput")
        dbg["d_r1"] = nc.dram_tensor("d_r1", [128, 16 * T], F32, kind="ExternalOutput")
        dbg["d_dt"] = nc.dram_tensor("d_dt", [16, T], F32, kind="ExternalOutput")
        dbg["d_xs"] = nc.dram_tensor("d_xs", [16, T], F32, kind="ExternalOutput")
        dbg["d_scan"] = nc.dram_tensor("d_scan", [128, 8192], BF16, kind="ExternalOutput")
        dbg["d_yb"] = nc.dram_tensor("d_yb", [16, T], BF16, kind="ExternalOutput")
        dbg["d_r2"] = nc.dram_tensor("d_r2", [128, 16 * T], F32, kind="ExternalOutput")
        dbg["d_hc"] = nc.dram_tensor("d_hc", [128, 2], F32, kind="ExternalOutput")
        dbg["d_xn2"] = nc.dram_tensor("d_xn2", [128, 16 * T], BF16, kind="ExternalOutput")
        dbg["d_gw"] = nc.dram_tensor("d_gw", [128, 64], F32, kind="ExternalOutput")
        dbg["d_gc"] = nc.dram_tensor("d_gc", [128, 56], F32, kind="ExternalOutput")
        dbg["d_msk"] = nc.dram_tensor("d_msk", [128, 64], F32, kind="ExternalOutput")
        dbg["d_exp"] = nc.dram_tensor("d_exp", [128, 64], F32, kind="ExternalOutput")
        dbg["d_lgf"] = nc.dram_tensor("d_lgf", [8, 1024], F32, kind="ExternalOutput")
        dbg["d_int"] = nc.dram_tensor("d_int", [128, 8192], BF16, kind="ExternalOutput")
    cc1_in = nc.dram_tensor("cc1_in", [1, 32 * 128 * 3], BF16)
    cc1_out = nc.dram_tensor("cc1_out", [8, 32 * 128 * 3], BF16)
    cc2_in = nc.dram_tensor("cc2_in", [1, 272], F32)
    cc2_out = nc.dram_tensor("cc2_out", [8, 272], F32)

    es = contextlib.ExitStack()
    sbuf = lambda name, shape, dt: es.enter_context(nc.sbuf_tensor(name, list(shape), dt))
    psum = lambda name, shape, dt: es.enter_context(nc.psum_tensor(name, list(shape), dt))

    resid = sbuf("resid", (128, 16 * T), F32)
    xnr = sbuf("xnr", (128, 16 * T), BF16)
    wreg = sbuf("wreg", (128, 3 * 8192), BF16)
    inter = sbuf("inter", (128, 8192), BF16)
    ygr = sbuf("ygr", (128, 4096), BF16)
    poolf = sbuf("poolf", (128, 4096), F32)
    c_wg = sbuf("c_wg", (128, 128), BF16)
    c_wc = sbuf("c_wc", (128, 128), F32)
    c_wn = sbuf("c_wn", (128, 16), F32)
    c_dtb = sbuf("c_dtb", (16, 1), F32)
    c_alog = sbuf("c_alog", (16, 1), F32)
    c_acol = sbuf("c_acol", (16, 1), F32)
    c_ind = sbuf("c_ind", (128, 512), BF16)
    c_ind2 = sbuf("c_ind2", (128, 32), BF16)
    c_r = sbuf("c_r", (16, 256), F32)
    c_1f = sbuf("c_1f", (1, 128), F32)
    c_1b = sbuf("c_1b", (1, 128), BF16)
    c_1bc = sbuf("c_1bc", (128, 1), BF16)
    c_idf = sbuf("c_idf", (128, 128), F32)
    c_idb = sbuf("c_idb", (128, 128), BF16)
    c_sel = sbuf("c_sel", (128, 8), F32)
    c_isel = sbuf("c_isel", (128, 8), F32)
    c_sprev = sbuf("c_sprev", (128, 8), F32)
    gw = sbuf("gw", (128, 64), F32)
    ge1 = sbuf("ge1", (128, 64), F32)
    glm = sbuf("glm", (128, 64), F32)
    gsub = sbuf("gsub", (128, 64), F32)
    gmsk = sbuf("gmsk", (128, 64), F32)
    gexp = sbuf("gexp", (128, 64), F32)
    gpp = sbuf("gpp", (128, 64), F32)
    gcols = sbuf("gcols", (128, 56), F32)
    gwb = sbuf("gwb", (128, 64), BF16)
    grow = sbuf("grow", (1, 1024), BF16)
    yb16 = sbuf("yb16", (16, T), BF16)
    halo_o = sbuf("halo_o", (128, 96), BF16)
    halo_i = sbuf("halo_i", (128, 96), BF16)
    xi_f3 = sbuf("xi_f3", (128, 96), BF16)
    comb6 = sbuf("comb6", (128, 192), BF16)
    xcfixf = sbuf("xcfixf", (128, 96), F32)
    xcfixb = sbuf("xcfixb", (128, 96), BF16)
    ccr = sbuf("ccr", (128, 192), BF16)
    fcol = sbuf("fcol", (128, 2), F32)
    hcol = sbuf("hcol", (128, 2), F32)
    hb = sbuf("hb", (128, 2), BF16)
    fjall = sbuf("fjall", (128, 16), F32)
    djall = sbuf("djall", (16, 8), F32)
    djrep = sbuf("djrep", (128, 2), F32)
    djeff = sbuf("djeff", (128, 2), F32)
    fjeff = sbuf("fjeff", (128, 2), F32)
    d16 = sbuf("d16", (16, 1), F32)
    ls16 = sbuf("ls16", (16, 1), F32)

    pb = [psum(f"pb{i}", (128, 512), F32) for i in range(7)]
    pbf = psum("pbf", (128, 1024), BF16)

    sems = {k: es.enter_context(nc.semaphore(k)) for k in
            ("wd", "sd", "mm", "ve", "se", "cc")}

    IT_XI = 0
    IT_WIN = (1024, 3072)
    IT_WH2I = (5120, 5632)
    IT_XCS = (6144, 7168)
    YG_GSB = (0, 1024)
    YG_G8 = 2048
    YG_H1T = 3072
    PF_XC = 0
    PF_DT = 1024
    PF_SH = 2048
    PF_LGF = 3072
    SP_B = (0, 1024)
    SP_C = (2048, 3072)
    SP_H = (4096, 5120)
    SP_DA = (6144, 7168)

    R_MEAN = poolf[0:1, 0:1024]
    R_VAR = poolf[0:1, 1024:2048]
    R_SCR = poolf[0:1, 2048:3072]
    R_A = poolf[0:1, 3072:4096]
    R_RMS = poolf[0:1, 2048:3072]

    wih_ap = wih.ap()
    woh_ap = woh.ap()
    wint_ap = wint.ap()
    wott_ap = wott.ap()

    def program(p: Prog):
        def tap(name, src_ap, ms, eng='v'):
            if not debug:
                return
            p.wait('y', ms)
            i = p.y.dma_start(dbg[name].ap(), src_ap)
            p.inc(i, 'sd', 16)
        bank_free = {i: None for i in range(8)}
        slot_ms = {0: None, 1: None, 2: None}
        wload = {}

        def mm_group(bank, mms, tag):
            p.wait('t', bank_free[bank])
            n = len(mms)
            last = None
            for i, (o, l, r) in enumerate(mms):
                last = p.t.matmul(o, l, r, start=(i == 0), stop=(i == n - 1))
            p.inc(last, 'mm')
            return p.mk(tag, 'mm')

        def evac_v(bank, grp, fn, tag):
            p.wait('v', grp)
            inst = fn(p.v)
            p.inc(inst, 've')
            name = p.mk(tag, 've')
            bank_free[bank] = name
            return name

        def evac_s(bank, grp, fn, tag):
            p.wait('s', grp)
            inst = fn(p.s)
            p.inc(inst, 'se')
            name = p.mk(tag, 'se')
            bank_free[bank] = name
            return name

        i = p.y.dma_start(resid[:, :].rearrange("q (c t) -> q c t", t=T),
                          xin.ap().rearrange("(c q) t -> q c t", q=128))
        p.inc(i, 'sd', 16)
        p.mark("x_loaded", 'sd')
        for dst, src in ((c_wc, wct), (c_wn, wnt), (c_dtb, dtb), (c_alog, alog),
                         (c_r, rrep), (c_1f, ones_f), (c_idf, id_f), (c_sel, selm),
                         (c_isel, invselm), (c_sprev, selprev)):
            i = p.y.dma_start(dst[:, :], src[:, :])
            p.inc(i, 'sd', 16)
        p.mark("consts_f", 'sd')
        for dst, src in ((c_wg, wg), (c_ind, indt), (c_ind2, ind2),
                         (c_1b, ones_b), (c_1bc, ones_c), (c_idb, id_b)):
            i = p.g.dma_start(dst[:, :], src[:, :])
            p.inc(i, 'wd', 16)
        p.mark("consts_b", 'wd')

        def load_weight(w):
            slot = w % 3
            e = (w % 48) // 6
            j = w % 6
            p.wait('g', slot_ms[slot])
            if j < 4:
                src = wih_ap[e * 4 + j]
            else:
                src = woh_ap[e * 2 + (j - 4)]
            i = p.g.dma_start(wreg[:, WSLOT[slot]:WSLOT[slot] + 8192], src)
            p.inc(i, 'wd', 16)
            wload[w] = (p.mk(f"wl{w}", 'wd'), slot)

        for w in (0, 1, 2):
            load_weight(w)

        def emit_ln(tag, with_wn, resid_ms, resid_sem_eng, xn_free_ms):
            p.wait('s', resid_ms)
            chunk_mm = [None, None]
            for c in range(NKC):
                xb = inter[:, (c % 2) * 1024:(c % 2) * 1024 + 1024]
                xq = inter[:, 2048 + (c % 2) * 1024:2048 + (c % 2) * 1024 + 1024]
                p.wait('s', chunk_mm[c % 2])
                i = p.s.activation(xb, resid[:, c * T:(c + 1) * T], AF.Copy)
                p.inc(i, 'se')
                cast_ms = p.mk(f"{tag}cb", 'se')
                p.wait('v', cast_ms)
                i = p.v.tensor_mul(xq, xb, xb)
                p.inc(i, 've')
                sq_ms = p.mk(f"{tag}sq", 've')
                p.wait('t', sq_ms)
                if c == 0:
                    for b in range(4):
                        p.wait('t', bank_free[b])
                last = None
                for piece in range(2):
                    last = p.t.matmul(pb[piece][0:1, :], c_1bc[:, 0:1],
                                      xb[:, piece * 512:piece * 512 + 512],
                                      start=(c == 0), stop=(c == NKC - 1))
                    last = p.t.matmul(pb[2 + piece][0:1, :], c_1bc[:, 0:1],
                                      xq[:, piece * 512:piece * 512 + 512],
                                      start=(c == 0), stop=(c == NKC - 1))
                p.inc(last, 'mm')
                chunk_mm[c % 2] = p.mk(f"{tag}cm", 'mm')
            stat_grp = chunk_mm[(NKC - 1) % 2]
            p.wait('v', stat_grp)
            p.v.tensor_copy(R_MEAN[:, 0:512], pb[0][0:1, :])
            p.v.tensor_copy(R_MEAN[:, 512:1024], pb[1][0:1, :])
            p.v.tensor_copy(R_VAR[:, 0:512], pb[2][0:1, :])
            i = p.v.tensor_copy(R_VAR[:, 512:1024], pb[3][0:1, :])
            p.inc(i, 've')
            ev = p.mk(f"{tag}sev", 've')
            for b in range(4):
                bank_free[b] = ev
            p.wait('v', ev)
            p.v.tensor_scalar(R_MEAN, R_MEAN, 1.0 / D, None, ALU.mult)
            p.v.tensor_mul(R_SCR, R_MEAN, R_MEAN)
            p.v.tensor_scalar(R_VAR, R_VAR, 1.0 / D, None, ALU.mult)
            i = p.v.tensor_sub(R_VAR, R_VAR, R_SCR)
            p.inc(i, 've')
            var_ms = p.mk(f"{tag}var", 've')
            i = p.v.tensor_scalar(R_A, R_VAR, EPS_LN, None, ALU.add)
            p.inc(i, 've')
            t1_ms = p.mk(f"{tag}t1", 've')
            p.wait('s', t1_ms)
            i = p.s.activation(R_A, R_A, AF.Ln)
            p.inc(i, 'se')
            t2_ms = p.mk(f"{tag}t2", 'se')
            p.wait('v', t2_ms)
            i = p.v.tensor_scalar(R_A, R_A, -0.5, None, ALU.mult)
            p.inc(i, 've')
            t3_ms = p.mk(f"{tag}t3", 've')
            p.wait('s', t3_ms)
            i = p.s.activation(R_A, R_A, AF.Exp)
            p.inc(i, 'se')
            rstd_ms = p.mk(f"{tag}rstd", 'se')
            if with_wn:
                p.v.tensor_scalar(R_SCR, R_VAR, EPS_LN, None, ALU.add)
                i = p.v.reciprocal(R_SCR, R_SCR)
                p.inc(i, 've')
                rc_ms = p.mk(f"{tag}rc", 've')
                p.wait('v', rc_ms)
                i = p.v.tensor_mul(R_SCR, R_VAR, R_SCR)
                p.inc(i, 've')
                i = p.v.tensor_scalar(R_SCR, R_SCR, EPS_RMS, None, ALU.add)
                p.inc(i, 've')
                u2_ms = p.mk(f"{tag}u2", 've')
                p.wait('s', u2_ms)
                i = p.s.activation(R_SCR, R_SCR, AF.Ln)
                p.inc(i, 'se')
                u3_ms = p.mk(f"{tag}u3", 'se')
                p.wait('v', u3_ms)
                i = p.v.tensor_scalar(R_SCR, R_SCR, -0.5, None, ALU.mult)
                p.inc(i, 've')
                u4_ms = p.mk(f"{tag}u4", 've')
                p.wait('s', u4_ms)
                i = p.s.activation(R_SCR, R_SCR, AF.Exp)
                p.inc(i, 'se')
                rms_ms = p.mk(f"{tag}rms", 'se')
                p.wait('v', rms_ms)
                a_ms = rstd_ms
            else:
                p.wait('v', rstd_ms)
                a_ms = rstd_ms
            p.v.tensor_mul(R_VAR, R_MEAN, R_A)
            i = p.v.tensor_scalar(R_VAR, R_VAR, -1.0, None, ALU.mult)
            p.inc(i, 've')
            ab_ms = p.mk(f"{tag}ab", 've')
            p.wait('t', ab_ms)
            rows_to_bc = [(4, R_A), (0, R_VAR)] + ([(2, R_RMS)] if with_wn else [])
            for piece in range(2):
                for bi0, row in rows_to_bc:
                    bi = bi0 + piece
                    p.wait('t', bank_free[bi])
                    i = p.t.matmul(pb[bi][:, :], c_1f[0:1, :],
                                   row[:, piece * 512:piece * 512 + 512],
                                   start=True, stop=True)
                    p.inc(i, 'mm')
            bc_ms = p.mk(f"{tag}bc", 'mm')
            p.wait('v', bc_ms)
            p.wait('v', xn_free_ms)
            scr = inter[:, 4096:4608]
            last = None
            for c in range(NKC):
                for piece in range(2):
                    sl = slice(c * T + piece * 512, c * T + piece * 512 + 512)
                    p.v.tensor_mul(scr, resid[:, sl], pb[4 + piece][:, :])
                    if with_wn:
                        p.v.tensor_add(scr, scr, pb[piece][:, :])
                        p.v.tensor_add(resid[:, sl], resid[:, sl], scr)
                        p.v.tensor_scalar(scr, scr, c_wn[:, c:c + 1], None, ALU.mult)
                        last = p.v.tensor_mul(xnr[:, sl], scr, pb[2 + piece][:, :])
                    else:
                        last = p.v.tensor_add(xnr[:, sl], scr, pb[piece][:, :])
            p.inc(last, 've')
            xn_ms = p.mk(f"{tag}xn", 've')
            for b in ((0, 1, 2, 3, 4, 5) if with_wn else (0, 1, 4, 5)):
                bank_free[b] = xn_ms
            if tag == "ln0":
                tap("d_xn", xnr[:, :], xn_ms)
            if tag == "ln1m":
                tap("d_xn2", xnr[:, :], xn_ms)
            return xn_ms

        def emit_gate(call, xn_ms):
            tag = f"g{call}"
            grps = []
            p.wait('t', xn_ms)
            for piece in range(2):
                mms = [(pb[2 + piece][0:8, :], c_wg[:, kc * 8:kc * 8 + 8],
                        xnr[:, kc * T + piece * 512:kc * T + piece * 512 + 512])
                       for kc in range(NKC)]
                grps.append(mm_group(2 + piece, mms, f"{tag}lg"))
            lgf = poolf[0:8, PF_LGF:PF_LGF + 1024]
            p.wait('v', grps[1])
            p.v.tensor_copy(lgf[:, 0:512], pb[2][0:8, :])
            i = p.v.tensor_copy(lgf[:, 512:1024], pb[3][0:8, :])
            p.inc(i, 've')
            lg_ms = p.mk(f"{tag}lgf", 've')
            bank_free[2] = lg_ms
            bank_free[3] = lg_ms
            p.wait('t', lg_ms)
            p.wait('t', bank_free[6])
            last = None
            for tc in range(8):
                last = p.t.transpose(pb[6][:, tc * 8:tc * 8 + 8],
                                     lgf[:, tc * 128:(tc + 1) * 128], c_idf[0:8, 0:8])
            p.inc(last, 'mm')
            tr_ms = p.mk(f"{tag}tr", 'mm')
            p.wait('v', tr_ms)
            i = p.v.tensor_copy(gw[:, :], pb[6][:, 0:64])
            p.inc(i, 've')
            bank_free[6] = p.mk(f"{tag}gw", 've')
            m1 = gcols[:, 0:8]
            m2 = gcols[:, 8:16]
            c1 = gcols[:, 16:24]
            tie = gcols[:, 24:32]
            m2e = gcols[:, 32:40]
            zz = gcols[:, 40:48]
            rz = gcols[:, 48:56]
            last = None
            for tc in range(8):
                last = p.v.reduce_max(m1[:, tc:tc + 1], gw[:, tc * 8:tc * 8 + 8], AX.X)
            p.inc(last, 've')
            p.wait('v', p.mk(f"{tag}s1", 've'))
            last = None
            for tc in range(8):
                p.v.tensor_scalar(ge1[:, tc * 8:tc * 8 + 8], gw[:, tc * 8:tc * 8 + 8],
                                  m1[:, tc:tc + 1], None, ALU.is_ge)
                last = p.v.tensor_scalar(gsub[:, tc * 8:tc * 8 + 8], gw[:, tc * 8:tc * 8 + 8],
                                         m1[:, tc:tc + 1], None, ALU.subtract)
            p.inc(last, 've')
            p.wait('v', p.mk(f"{tag}s2a", 've'))
            i = p.v.scalar_tensor_tensor(glm[:, :], ge1[:, :], -1.0e30, gw[:, :],
                                         ALU.mult, ALU.add)
            p.inc(i, 've')
            p.wait('v', p.mk(f"{tag}s2b", 've'))
            last = None
            for tc in range(8):
                p.v.reduce_max(m2[:, tc:tc + 1], glm[:, tc * 8:tc * 8 + 8], AX.X)
                last = p.v.reduce_sum(c1[:, tc:tc + 1], ge1[:, tc * 8:tc * 8 + 8], AX.X)
            p.inc(last, 've')
            sub_ms = p.mk(f"{tag}s2", 've')
            p.wait('v', sub_ms)
            i = p.v.tensor_scalar(tie, c1, 2.0, None, ALU.is_ge)
            p.inc(i, 've')
            p.wait('v', p.mk(f"{tag}s3a", 've'))
            i = p.v.tensor_sub(m2e, m1, m2)
            p.inc(i, 've')
            p.wait('v', p.mk(f"{tag}s3b", 've'))
            i = p.v.tensor_mul(m2e, m2e, tie)
            p.inc(i, 've')
            p.wait('v', p.mk(f"{tag}s3c", 've'))
            i = p.v.tensor_add(m2e, m2e, m2)
            p.inc(i, 've')
            p.wait('v', p.mk(f"{tag}s3", 've'))
            last = None
            for tc in range(8):
                last = p.v.tensor_scalar(gmsk[:, tc * 8:tc * 8 + 8],
                                         gw[:, tc * 8:tc * 8 + 8],
                                         m2e[:, tc:tc + 1], None, ALU.is_ge)
            p.inc(last, 've')
            msk_ms = p.mk(f"{tag}s4", 've')
            p.wait('v', msk_ms)
            p.wait('s', sub_ms)
            i = p.s.activation(gexp[:, :], gsub[:, :], AF.Exp)
            p.inc(i, 'se')
            exp_ms = p.mk(f"{tag}exp", 'se')
            p.wait('v', exp_ms)
            i = p.v.tensor_mul(gpp[:, :], gexp[:, :], gmsk[:, :])
            p.inc(i, 've')
            p.wait('v', p.mk(f"{tag}s5a", 've'))
            last = None
            for tc in range(8):
                last = p.v.reduce_sum(zz[:, tc:tc + 1], gpp[:, tc * 8:tc * 8 + 8], AX.X)
            p.inc(last, 've')
            p.wait('v', p.mk(f"{tag}s5", 've'))
            i = p.v.reciprocal(rz, zz)
            p.inc(i, 've')
            p.wait('v', p.mk(f"{tag}s6", 've'))
            last = None
            for tc in range(8):
                last = p.v.tensor_scalar(gwb[:, tc * 8:tc * 8 + 8],
                                         gpp[:, tc * 8:tc * 8 + 8],
                                         rz[:, tc:tc + 1], 0.5, ALU.mult, ALU.mult)
            p.inc(last, 've')
            w_ms = p.mk(f"{tag}w", 've')
            p.wait('t', w_ms)
            p.wait('t', bank_free[7])
            last = None
            for tc in range(8):
                last = p.t.transpose(pbf[0:8, tc * 128:(tc + 1) * 128],
                                     gwb[:, tc * 8:tc * 8 + 8], c_idb[:, :])
            p.inc(last, 'mm')
            gtr_ms = p.mk(f"{tag}gtr", 'mm')
            p.wait('v', gtr_ms)
            i = p.v.tensor_copy(ygr[0:8, YG_G8:YG_G8 + 1024], pbf[0:8, :])
            p.inc(i, 've')
            g8_ms = p.mk(f"{tag}g8", 've')
            bank_free[7] = g8_ms
            if call == 0:
                tap("d_g8", ygr[0:8, YG_G8:YG_G8 + 1024], g8_ms)
                tap("d_gw", gw[:, :], g8_ms)
                tap("d_gc", gcols[:, :], g8_ms)
                tap("d_msk", gmsk[:, :], g8_ms)
                tap("d_exp", gexp[:, :], g8_ms)
                tap("d_lgf", lgf[:, :], g8_ms)
            return g8_ms

        state = {"grow_pe": None, "h1t_last": [None, None], "hping": [0],
                 "woping": [0]}

        def gsb_make(call, e, g8_ms):
            p.wait('y', g8_ms)
            p.wait('y', state["grow_pe"])
            i = p.y.dma_start(grow[0:1, :], ygr[e:e + 1, YG_G8:YG_G8 + 1024])
            p.inc(i, 'sd', 16)
            grow_ms = p.mk(f"c{call}e{e}grow", 'sd')
            p.wait('t', grow_ms)
            last_pe = None
            for piece in range(2):
                p.wait('t', bank_free[6])
                i = p.t.matmul(pb[6][:, :], c_1b[0:1, :],
                               grow[0:1, piece * 512:piece * 512 + 512],
                               start=True, stop=True)
                p.inc(i, 'mm')
                grp = p.mk(f"c{call}e{e}gb", 'mm')
                if piece == 1:
                    last_pe = grp
                evac_v(6, grp, lambda v, piece=piece, e=e:
                       v.tensor_copy(ygr[:, YG_GSB[e % 2] + piece * 512:
                                          YG_GSB[e % 2] + piece * 512 + 512],
                                     pb[6][:, :]), f"c{call}e{e}ge")
            state["grow_pe"] = last_pe

        def ffn_h(call, e, w, which):
            wms, slot = wload[w]
            base = WSLOT[slot]
            p.wait('t', wms)
            gsb_off = YG_GSB[e % 2]
            last_grp = None
            for ml in range(4):
                m = (which % 2) * 4 + ml
                for piece in range(2):
                    bank = state["hping"][0]
                    state["hping"][0] ^= 1
                    mms = [(pb[bank][:, :],
                            wreg[:, base + kc * 512 + ml * 128:
                                 base + kc * 512 + ml * 128 + 128],
                            xnr[:, kc * T + piece * 512:kc * T + piece * 512 + 512])
                           for kc in range(NKC)]
                    grp = mm_group(bank, mms, f"c{call}e{e}h{which}")
                    last_grp = grp
                    s_sl = inter[:, m * 1024 + piece * 512:m * 1024 + piece * 512 + 512]
                    if which < 2:
                        evac_s(bank, grp, lambda s, s_sl=s_sl, bank=bank:
                               s.activation(s_sl, pb[bank][:, :], AF.Silu),
                               f"c{call}e{e}sil")
                    else:
                        h1t = ygr[:, YG_H1T + piece * 512:YG_H1T + piece * 512 + 512]
                        p.wait('s', state["h1t_last"][piece])
                        ems = evac_s(bank, grp, lambda s, h1t=h1t, bank=bank:
                                     s.activation(h1t, pb[bank][:, :], AF.Copy),
                                     f"c{call}e{e}h1c")
                        p.wait('v', ems)
                        gsb = ygr[:, gsb_off + piece * 512:gsb_off + piece * 512 + 512]
                        p.v.tensor_mul(s_sl, s_sl, gsb)
                        i = p.v.tensor_mul(s_sl, s_sl, h1t)
                        p.inc(i, 've')
                        state["h1t_last"][piece] = p.mk(f"c{call}e{e}im", 've')
            return last_grp

        def ffn_wo(call, e, w, half, inter_ms):
            wms, slot = wload[w]
            base = WSLOT[slot]
            p.wait('t', wms)
            p.wait('t', inter_ms)
            last_add = None
            for ocl in range(8):
                oc = half * 8 + ocl
                for piece in range(2):
                    bank = 2 + state["woping"][0]
                    state["woping"][0] ^= 1
                    mms = [(pb[bank][:, :],
                            wreg[:, base + ic * 1024 + ocl * 128:
                                 base + ic * 1024 + ocl * 128 + 128],
                            inter[:, ic * 1024 + piece * 512:ic * 1024 + piece * 512 + 512])
                           for ic in range(8)]
                    grp = mm_group(bank, mms, f"c{call}e{e}wo")
                    sl = slice(oc * T + piece * 512, oc * T + piece * 512 + 512)
                    last_add = evac_v(bank, grp, lambda v, sl=sl, bank=bank:
                                      v.tensor_add(resid[:, sl], resid[:, sl],
                                                   pb[bank][:, :]),
                                      f"c{call}e{e}ra")
            return last_add

        def emit_ffn(call, resid_ms, resid_eng, xn_free_ms):
            xn_ms = emit_ln(f"ln{call * 2}", False, resid_ms, resid_eng, xn_free_ms)
            g8_ms = emit_gate(call, xn_ms)
            wbase = call * 48
            last_resid = None
            for e in range(NE):
                gsb_make(call, e, g8_ms)
                for ph in range(6):
                    w = wbase + e * 6 + ph
                    if ph < 4:
                        done = ffn_h(call, e, w, ph)
                        if ph == 3:
                            inter_ms = p.mk(f"c{call}e{e}int", 've')
                            if call == 0 and e == 0:
                                tap("d_int", inter[:, :], inter_ms)
                    else:
                        done = ffn_wo(call, e, w, ph - 4, inter_ms)
                        if ph == 5:
                            last_resid = done
                    slot_ms[w % 3] = done
                    if (w % 48) < 45:
                        load_weight(w + 3)
            return last_resid

        resid_ms = emit_ffn(0, "x_loaded", 'sd', None)
        tap("d_r1", resid[:, :], resid_ms)

        xn2_ms = emit_ln("ln1m", True, resid_ms, 've', p.mk("xnfree0", 'mm'))

        win_last = [xn2_ms, xn2_ms]

        def load_wintile(oc, idx):
            sl = idx % 2
            p.wait('g', win_last[sl])
            i = p.g.dma_start(inter[:, IT_WIN[sl]:IT_WIN[sl] + 2048], wint_ap[oc])
            p.inc(i, 'wd', 16)
            return p.mk(f"win{oc}", 'wd'), sl

        def proj_groups(oc, idx, tag):
            wms, sl = win_tiles[idx]
            p.wait('t', wms)
            grps = []
            for piece in range(2):
                bank = state["hping"][0]
                state["hping"][0] ^= 1
                mms = [(pb[bank][:, :],
                        inter[:, IT_WIN[sl] + kc * 128:IT_WIN[sl] + kc * 128 + 128],
                        xnr[:, kc * T + piece * 512:kc * T + piece * 512 + 512])
                       for kc in range(NKC)]
                grps.append((bank, mm_group(bank, mms, tag)))
            win_last[sl] = grps[1][1]
            return grps

        p.wait('t', xn2_ms)
        proj_ocs = [64, 65, 66, 67, 68] + list(range(32, 64))
        win_tiles = {}
        for idx, oc in enumerate(proj_ocs[:2]):
            win_tiles[idx] = load_wintile(oc, idx)

        grps = proj_groups(64, 0, "pdt")
        win_tiles[2] = load_wintile(proj_ocs[2], 2)
        for piece, (bank, grp) in enumerate(grps):
            evac_v(bank, grp, lambda v, piece=piece, bank=bank:
                   v.tensor_copy(poolf[0:16, PF_DT + piece * 512:PF_DT + piece * 512 + 512],
                                 pb[bank][0:16, :]), "dtev")
        dtraw_ms = p.mk("dtraw", 've')
        dt = poolf[0:16, PF_DT:PF_DT + 1024]
        sh = poolf[0:16, PF_SH:PF_SH + 1024]
        i = p.v.tensor_scalar(dt, dt, c_dtb[:, 0:1], None, ALU.add)
        p.inc(i, 've')
        dtx_ms = p.mk("dtx", 've')
        p.wait('s', dtx_ms)
        p.s.activation(sh, dt, AF.Abs)
        p.s.activation(sh, sh, AF.Exp, scale=-1.0)
        i = p.s.activation(sh, sh, AF.Ln, bias=1.0)
        p.inc(i, 'se')
        sp_ms = p.mk("sp", 'se')
        p.wait('v', sp_ms)
        p.v.tensor_scalar_max(dt, dt, 0.0)
        i = p.v.tensor_add(dt, dt, sh)
        p.inc(i, 've')
        dt_ms = p.mk("dt", 've')
        tap("d_dt", poolf[0:16, PF_DT:PF_DT + 1024], dt_ms)

        scan_s0 = WSLOT[0]
        p.wait('v', slot_ms[0])
        for idx, oc in enumerate(proj_ocs[1:5], start=1):
            grps = proj_groups(oc, idx, "pbc")
            if idx + 2 < len(proj_ocs):
                win_tiles[idx + 2] = load_wintile(proj_ocs[idx + 2], idx + 2)
            dsts = {65: SP_B[0], 66: SP_B[1], 67: SP_C[0], 68: SP_C[1]}
            off = dsts[oc]
            for piece, (bank, grp) in enumerate(grps):
                evac_v(bank, grp, lambda v, off=off, piece=piece, bank=bank:
                       v.tensor_copy(wreg[:, scan_s0 + off + piece * 512:
                                          scan_s0 + off + piece * 512 + 512],
                                     pb[bank][:, :]), "bcev")

        xsum_started = [False]
        xcs_last = [None, None]
        xc_sil_last = [None]
        for idx0, oc in enumerate(proj_ocs[5:]):
            idx = idx0 + 5
            c = oc - 32
            grps = proj_groups(oc, idx, "pxi")
            if idx + 2 < len(proj_ocs):
                win_tiles[idx + 2] = load_wintile(proj_ocs[idx + 2], idx + 2)
            xi = inter[:, IT_XI:IT_XI + 1024]
            for piece, (bank, grp) in enumerate(grps):
                evac_v(bank, grp, lambda v, piece=piece, bank=bank:
                       v.tensor_copy(xi[:, piece * 512:piece * 512 + 512],
                                     pb[bank][:, :]), "xiev")
            p.v.tensor_copy(xi_f3[:, c * 3:c * 3 + 3], xi[:, 0:3])
            i = p.v.tensor_copy(halo_o[:, c * 3:c * 3 + 3], xi[:, 1021:1024])
            p.inc(i, 've')
            p.mk(f"xis{c}", 've')
            p.wait('v', xc_sil_last[0])
            xc = poolf[:, PF_XC:PF_XC + 1024]
            p.v.tensor_scalar(xc[:, 3:1024], xi[:, 0:1021],
                              c_wc[:, c * 4:c * 4 + 1], None, ALU.mult)
            for k in (1, 2, 3):
                p.v.scalar_tensor_tensor(xc[:, 3:1024], xi[:, k:1021 + k],
                                         c_wc[:, c * 4 + k:c * 4 + k + 1],
                                         xc[:, 3:1024], ALU.mult, ALU.add)
            i = p.v.memset(xc[:, 0:3], 0.0)
            p.inc(i, 've')
            xc_ms = p.mk(f"xc{c}", 've')
            xcs = inter[:, IT_XCS[c % 2]:IT_XCS[c % 2] + 1024]
            p.wait('s', xc_ms)
            p.wait('s', xcs_last[c % 2])
            i = p.s.activation(xcs, xc, AF.Silu)
            p.inc(i, 'se')
            xc_sil_last[0] = p.mk(f"xsl{c}", 'se')
            p.wait('t', xc_sil_last[0])
            if not xsum_started[0]:
                p.wait('t', bank_free[4])
                p.wait('t', bank_free[5])
            last = None
            for piece in range(2):
                last = p.t.matmul(pb[4 + piece][0:16, :], c_ind[:, c * 16:c * 16 + 16],
                                  xcs[:, piece * 512:piece * 512 + 512],
                                  start=(not xsum_started[0] and piece == 0) or
                                        (not xsum_started[0] and piece == 1),
                                  stop=False)
            xsum_started[0] = True
            p.inc(last, 'mm')
            xcs_last[c % 2] = p.mk(f"xsm{c}", 'mm')

        halo_ready = p.mk("haloo", 've')
        p.wait('y', halo_ready)
        i = p.y.dma_start(cc1_in.ap().rearrange("o (q x) -> (o q) x", q=128),
                          halo_o[:, :])
        p.inc(i, 'sd', 16)
        cc1s_ms = p.mk("cc1s", 'sd')
        p.wait('g', cc1s_ms)
        i = p.g.collective_compute("AllGather", ALU.bypass,
                                   replica_groups=[list(range(8))],
                                   ins=[cc1_in.ap()], outs=[cc1_out.ap()])
        p.inc(i, 'cc')
        cc1_ms = p.mk("cc1", 'cc')
        prev_use = [None, None]
        chain_ms = None
        for j in range(8):
            sl = j % 2
            p.wait('y', cc1_ms if j < 2 else prev_use[sl])
            i = p.y.dma_start(ccr[:, sl * 96:sl * 96 + 96],
                              cc1_out.ap()[j].rearrange("(q x) -> q x", q=128))
            p.inc(i, 'sd', 16)
            row_ms = p.mk(f"ccr{j}", 'sd')
            p.wait('v', row_ms)
            p.wait('v', chain_ms)
            if j == 0:
                i = p.v.tensor_scalar(halo_i[:, :], ccr[:, sl * 96:sl * 96 + 96],
                                      c_sprev[:, j:j + 1], None, ALU.mult)
            else:
                i = p.v.scalar_tensor_tensor(halo_i[:, :], ccr[:, sl * 96:sl * 96 + 96],
                                             c_sprev[:, j:j + 1], halo_i[:, :],
                                             ALU.mult, ALU.add)
            p.inc(i, 've')
            prev_use[sl] = p.mk(f"hacc{j}", 've')
            chain_ms = prev_use[sl]
        p.wait('v', chain_ms)
        comb3 = comb6[:, :].rearrange("q (c k) -> q c k", k=6)
        p.v.tensor_copy(comb3[:, :, 0:3],
                        halo_i[:, :].rearrange("q (c k) -> q c k", k=3))
        i = p.v.tensor_copy(comb3[:, :, 3:6],
                            xi_f3[:, :].rearrange("q (c k) -> q c k", k=3))
        p.inc(i, 've')
        p.wait('v', p.mk("comb", 've'))
        last = None
        for k in (0, 1, 2, 3):
            for c in range(NZC):
                if k == 0:
                    last = p.v.tensor_scalar(xcfixf[:, c * 3:c * 3 + 3],
                                             comb6[:, c * 6:c * 6 + 3],
                                             c_wc[:, c * 4:c * 4 + 1], None, ALU.mult)
                else:
                    last = p.v.scalar_tensor_tensor(xcfixf[:, c * 3:c * 3 + 3],
                                                    comb6[:, c * 6 + k:c * 6 + k + 3],
                                                    c_wc[:, c * 4 + k:c * 4 + k + 1],
                                                    xcfixf[:, c * 3:c * 3 + 3],
                                                    ALU.mult, ALU.add)
        p.inc(last, 've')
        fix_ms = p.mk("fixf", 've')
        p.wait('s', fix_ms)
        i = p.s.activation(xcfixb[:, :], xcfixf[:, :], AF.Silu)
        p.inc(i, 'se')
        fixb_ms = p.mk("fixb", 'se')
        p.wait('t', fixb_ms)
        last = None
        for c in range(NZC):
            last = p.t.matmul(pb[4][0:16, 0:3], c_ind[:, c * 16:c * 16 + 16],
                              xcfixb[:, c * 3:c * 3 + 3],
                              start=False, stop=(c == NZC - 1))
        p.inc(last, 'mm')
        xsum_ms = p.mk("xsum", 'mm')
        p.wait('v', xsum_ms)
        xs = poolf[0:16, PF_SH:PF_SH + 1024]
        p.v.tensor_copy(xs[:, 0:512], pb[4][0:16, :])
        i = p.v.tensor_copy(xs[:, 512:1024], pb[5][0:16, :])
        p.inc(i, 've')
        xs_ms = p.mk("xs", 've')
        tap("d_xs", xs, xs_ms)
        bank_free[4] = xs_ms
        bank_free[5] = xs_ms
        p.wait('v', xs_ms)
        p.wait('v', dt_ms)
        i = p.v.scalar_tensor_tensor(xs, dt, 1.0 / 256.0, xs, ALU.mult, ALU.mult)
        p.inc(i, 've')
        v_ms = p.mk("v", 've')

        p.wait('t', v_ms)
        for tile in range(2):
            for piece in range(2):
                bank = 6 if (tile * 2 + piece) % 2 == 0 else 2
                p.wait('t', bank_free[bank])
                i = p.t.matmul(pb[bank][:, :], c_r[0:16, tile * 128:tile * 128 + 128],
                               xs[:, piece * 512:piece * 512 + 512],
                               start=True, stop=True)
                p.inc(i, 'mm')
                grp = p.mk(f"vr{tile}{piece}", 'mm')
                tmp = inter[:, IT_XCS[0] + piece * 512:IT_XCS[0] + piece * 512 + 512]
                ems = evac_s(bank, grp, lambda s, tmp=tmp, bank=bank:
                             s.activation(tmp, pb[bank][:, :], AF.Copy), f"vre{tile}{piece}")
                p.wait('v', ems)
                boff = scan_s0 + SP_B[tile] + piece * 512
                i = p.v.tensor_mul(wreg[:, boff:boff + 512], wreg[:, boff:boff + 512], tmp)
                p.inc(i, 've')
                p.mk(f"u{tile}{piece}", 've')
        u_ms = p.mk("u", 've')
        p.wait('s', "consts_f")
        i = p.s.activation(c_acol[:, :], c_alog[:, :], AF.Exp)
        p.inc(i, 'se')
        acol_ms = p.mk("acol", 'se')
        p.wait('v', acol_ms)
        i = p.v.tensor_scalar(c_acol[:, :], c_acol[:, :], -1.0, None, ALU.mult)
        p.inc(i, 've')
        p.wait('v', p.mk("acoln", 've'))
        i = p.v.tensor_scalar(xs, dt, c_acol[:, 0:1], None, ALU.mult)
        p.inc(i, 've')
        loga_ms = p.mk("loga", 've')
        i = p.v.reduce_sum(ls16[:, :], xs, AX.X)
        p.inc(i, 've')
        lsum_ms = p.mk("lsum", 've')
        p.wait('s', lsum_ms)
        i = p.s.activation(d16[:, :], ls16[:, :], AF.Exp)
        p.inc(i, 'se')
        d16_ms = p.mk("d16", 'se')
        p.wait('s', slot_ms[0])
        p.wait('t', loga_ms)
        for tile in range(2):
            for piece in range(2):
                bank = 6 if (tile * 2 + piece) % 2 == 0 else 2
                p.wait('t', bank_free[bank])
                i = p.t.matmul(pb[bank][:, :], c_r[0:16, tile * 128:tile * 128 + 128],
                               xs[:, piece * 512:piece * 512 + 512],
                               start=True, stop=True)
                p.inc(i, 'mm')
                grp = p.mk(f"lr{tile}{piece}", 'mm')
                daoff = scan_s0 + SP_DA[tile] + piece * 512
                evac_s(bank, grp, lambda s, daoff=daoff, bank=bank:
                       s.activation(wreg[:, daoff:daoff + 512], pb[bank][:, :], AF.Exp),
                       f"dae{tile}{piece}")
        da_ms = p.mk("da", 'se')

        p.wait('v', da_ms)
        p.wait('v', u_ms)
        for tile in range(2):
            i = p.v.tensor_tensor_scan(
                wreg[:, scan_s0 + SP_H[tile]:scan_s0 + SP_H[tile] + 1024],
                wreg[:, scan_s0 + SP_DA[tile]:scan_s0 + SP_DA[tile] + 1024],
                wreg[:, scan_s0 + SP_B[tile]:scan_s0 + SP_B[tile] + 1024],
                0.0, ALU.mult, ALU.add)
            p.inc(i, 've')
        for tile in range(2):
            i = p.v.tensor_copy(fcol[:, tile:tile + 1],
                                wreg[:, scan_s0 + SP_H[tile] + 1023:
                                     scan_s0 + SP_H[tile] + 1024])
            p.inc(i, 've')
        f_ms = p.mk("f", 've')
        tap("d_scan", wreg[:, 0:8192], f_ms)
        p.wait('y', f_ms)
        p.wait('y', d16_ms)
        i = p.y.dma_start(cc2_in.ap()[0:1, 0:128].rearrange("o q -> q o"), fcol[:, 0:1])
        p.inc(i, 'sd', 16)
        i = p.y.dma_start(cc2_in.ap()[0:1, 128:256].rearrange("o q -> q o"), fcol[:, 1:2])
        p.inc(i, 'sd', 16)
        i = p.y.dma_start(cc2_in.ap()[0:1, 256:272].rearrange("o q -> q o"), d16[:, :])
        p.inc(i, 'sd', 16)
        cc2s_ms = p.mk("cc2s", 'sd')
        p.wait('g', cc2s_ms)
        i = p.g.collective_compute("AllGather", ALU.bypass,
                                   replica_groups=[list(range(8))],
                                   ins=[cc2_in.ap()], outs=[cc2_out.ap()])
        p.inc(i, 'cc')
        cc2_ms = p.mk("cc2", 'cc')
        p.wait('y', cc2_ms)
        for j in range(8):
            i = p.y.dma_start(fjall[:, 2 * j:2 * j + 1],
                              cc2_out.ap()[j:j + 1, 0:128].rearrange("o q -> q o"))
            p.inc(i, 'sd', 16)
            i = p.y.dma_start(fjall[:, 2 * j + 1:2 * j + 2],
                              cc2_out.ap()[j:j + 1, 128:256].rearrange("o q -> q o"))
            p.inc(i, 'sd', 16)
            i = p.y.dma_start(djall[:, j:j + 1],
                              cc2_out.ap()[j:j + 1, 256:272].rearrange("o q -> q o"))
            p.inc(i, 'sd', 16)
        gath_ms = p.mk("gath", 'sd')
        p.wait('v', gath_ms)
        i = p.v.memset(hcol[:, :], 0.0)
        p.inc(i, 've')
        p.mk("hz", 've')
        for j in range(8):
            p.wait('t', gath_ms)
            for tile in range(2):
                bank = 6 if tile == 0 else 2
                p.wait('t', bank_free[bank])
                i = p.t.matmul(pb[bank][:, 0:1], c_r[0:16, tile * 128:tile * 128 + 128],
                               djall[:, j:j + 1], start=True, stop=True)
                p.inc(i, 'mm')
                grp = p.mk(f"dr{j}{tile}", 'mm')
                evac_v(bank, grp, lambda v, tile=tile, bank=bank:
                       v.tensor_copy(djrep[:, tile:tile + 1], pb[bank][:, 0:1]),
                       f"dre{j}{tile}")
            p.wait('v', p.mk(f"drr{j}", 've'))
            for tile in range(2):
                p.v.scalar_tensor_tensor(djeff[:, tile:tile + 1], djrep[:, tile:tile + 1],
                                         c_sel[:, j:j + 1], c_isel[:, j:j + 1],
                                         ALU.mult, ALU.add)
                i = p.v.tensor_scalar(fjeff[:, tile:tile + 1], fjall[:, 2 * j + tile:2 * j + tile + 1],
                                      c_sel[:, j:j + 1], None, ALU.mult)
            p.inc(i, 've')
            p.wait('v', p.mk(f"cmb{j}", 've'))
            for tile in range(2):
                i = p.v.scalar_tensor_tensor(hcol[:, tile:tile + 1], hcol[:, tile:tile + 1],
                                             djeff[:, tile:tile + 1], fjeff[:, tile:tile + 1],
                                             ALU.mult, ALU.add)
            p.inc(i, 've')
            p.wait('v', p.mk(f"cmh{j}", 've'))
        tap("d_hc", hcol[:, :], p.mk("hcfin", 've'))
        i = p.v.tensor_copy(hb[:, :], hcol[:, :])
        p.inc(i, 've')
        p.wait('v', p.mk("hb", 've'))
        for tile in range(2):
            i = p.v.tensor_tensor_scan(
                wreg[:, scan_s0 + SP_H[tile]:scan_s0 + SP_H[tile] + 1024],
                wreg[:, scan_s0 + SP_DA[tile]:scan_s0 + SP_DA[tile] + 1024],
                wreg[:, scan_s0 + SP_B[tile]:scan_s0 + SP_B[tile] + 1024],
                hb[:, tile:tile + 1], ALU.mult, ALU.add)
            p.inc(i, 've')
        for tile in range(2):
            i = p.v.tensor_mul(wreg[:, scan_s0 + SP_C[tile]:scan_s0 + SP_C[tile] + 1024],
                               wreg[:, scan_s0 + SP_C[tile]:scan_s0 + SP_C[tile] + 1024],
                               wreg[:, scan_s0 + SP_H[tile]:scan_s0 + SP_H[tile] + 1024])
            p.inc(i, 've')
        ch_ms = p.mk("ch", 've')
        p.wait('t', ch_ms)
        ylast = None
        for piece in range(2):
            bank = 4 + piece
            p.wait('t', bank_free[bank])
            for tile in range(2):
                i = p.t.matmul(pb[bank][0:16, :], c_ind2[:, tile * 16:tile * 16 + 16],
                               wreg[:, scan_s0 + SP_C[tile] + piece * 512:
                                    scan_s0 + SP_C[tile] + piece * 512 + 512],
                               start=(tile == 0), stop=(tile == 1))
            p.inc(i, 'mm')
            ylast = p.mk(f"y{piece}", 'mm')
        p.wait('v', ylast)
        p.v.tensor_copy(yb16[:, 0:512], pb[4][0:16, :])
        i = p.v.tensor_copy(yb16[:, 512:1024], pb[5][0:16, :])
        p.inc(i, 've')
        yb_ms = p.mk("yb", 've')
        tap("d_yb", yb16[:, :], yb_ms)
        bank_free[4] = yb_ms
        bank_free[5] = yb_ms
        slot_ms[0] = ylast
        load_weight(48)

        wh2i_last = [None, None]
        yproj_last = [None, None]
        zsil_last = [None, None]
        group_add_ms = {}
        zidx = 0
        for gidx in range(8):
            slot = 1 + (gidx % 2)
            base = WSLOT[slot]
            dep = slot_ms[slot] if gidx < 2 else group_add_ms[gidx - 2]
            p.wait('g', dep)
            for q in range(4):
                ic = gidx * 4 + q
                i = p.g.dma_start(wreg[:, base + q * 2048:base + q * 2048 + 2048],
                                  wott_ap[ic])
                p.inc(i, 'wd', 16)
            wout_ms = p.mk(f"woutg{gidx}", 'wd')
            wsl = gidx % 2
            p.wait('g', wh2i_last[wsl])
            i = p.g.dma_start(inter[0:16, IT_WH2I[wsl]:IT_WH2I[wsl] + 512],
                              wh2it.ap()[0:16, gidx * 512:gidx * 512 + 512])
            p.inc(i, 'wd', 16)
            wh2i_ms = p.mk(f"wh2i{gidx}", 'wd')
            for q in range(4):
                zc = gidx * 4 + q
                widx = 37 + zidx
                win_tiles[widx] = load_wintile(zc, widx)
                zidx += 1
                grps = proj_groups(zc, widx, f"pz{zc}")
                zsl = zc % 2
                zbuf = inter[:, IT_XCS[zsl]:IT_XCS[zsl] + 1024]
                for piece, (bank, grp) in enumerate(grps):
                    p.wait('s', zsil_last[zsl])
                    evac_s(bank, grp, lambda s, zbuf=zbuf, piece=piece, bank=bank:
                           s.activation(zbuf[:, piece * 512:piece * 512 + 512],
                                        pb[bank][:, :], AF.Silu), f"zsil{zc}")
                zsil_ms = p.mk(f"zs{zc}", 'se')
                p.wait('t', wh2i_ms)
                p.wait('t', yb_ms)
                for piece in range(2):
                    bank = 4 + piece
                    p.wait('t', bank_free[bank])
                    i = p.t.matmul(pb[bank][:, :],
                                   inter[0:16, IT_WH2I[wsl] + q * 128:IT_WH2I[wsl] + q * 128 + 128],
                                   yb16[:, piece * 512:piece * 512 + 512],
                                   start=True, stop=True)
                    p.inc(i, 'mm')
                    grp = p.mk(f"yp{zc}{piece}", 'mm')
                    ypt = inter[:, IT_XI + piece * 512:IT_XI + piece * 512 + 512]
                    p.wait('s', yproj_last[piece])
                    ems = evac_s(bank, grp, lambda s, ypt=ypt, bank=bank:
                                 s.activation(ypt, pb[bank][:, :], AF.Copy),
                                 f"ype{zc}{piece}")
                    p.wait('v', ems)
                    p.wait('v', zsil_ms)
                    i = p.v.tensor_mul(ygr[:, q * 1024 + piece * 512:q * 1024 + piece * 512 + 512],
                                       zbuf[:, piece * 512:piece * 512 + 512], ypt)
                    p.inc(i, 've')
                    yproj_last[piece] = p.mk(f"ygm{zc}{piece}", 've')
                zsil_last[zsl] = yproj_last[1]
            yg_ms = yproj_last[1]
            p.wait('t', wout_ms)
            p.wait('t', yg_ms)
            last_add = None
            for piece in range(2):
                for oc in range(16):
                    bank = 2 + state["woping"][0]
                    state["woping"][0] ^= 1
                    mms = [(pb[bank][:, :],
                            wreg[:, base + q * 2048 + oc * 128:base + q * 2048 + oc * 128 + 128],
                            ygr[:, q * 1024 + piece * 512:q * 1024 + piece * 512 + 512])
                           for q in range(4)]
                    grp = mm_group(bank, mms, f"zwo{gidx}")
                    sl = slice(oc * T + piece * 512, oc * T + piece * 512 + 512)
                    last_add = evac_v(bank, grp, lambda v, sl=sl, bank=bank:
                                      v.tensor_add(resid[:, sl], resid[:, sl],
                                                   pb[bank][:, :]),
                                      f"zra{gidx}")
            group_add_ms[gidx] = last_add
            if gidx == 6:
                slot_ms[1] = last_add
                load_weight(49)
            if gidx == 7:
                slot_ms[2] = last_add
                load_weight(50)
        residB_ms = group_add_ms[7]
        tap("d_r2", resid[:, :], residB_ms)

        xnfree = p.mk("xnfree1", 'mm')
        resid_final = emit_ffn(1, residB_ms, 've', xnfree)

        p.wait('y', resid_final)
        i = p.y.dma_start(oxt.ap().rearrange("(c q) t -> q c t", q=128),
                          resid[:, :].rearrange("q (c t) -> q c t", t=T))
        p.inc(i, 'sd', 16)
        if p.emit == 'y':
            p.eng.wait_ge(p.sems['sd'], p.cnt['sd'])

    with nc.Block() as block:
        @block.sync
        def _(sync):
            program(Prog('y', sync, sems))

        @block.gpsimd
        def _(gp):
            program(Prog('g', gp, sems))

        @block.tensor
        def _(tn):
            program(Prog('t', tn, sems))

        @block.vector
        def _(vc):
            program(Prog('v', vc, sems))

        @block.scalar
        def _(sc):
            program(Prog('s', sc, sems))

    return nc


_NC_CACHE = {}


def _get_nc():
    if "nc" not in _NC_CACHE:
        _NC_CACHE["nc"] = build_program()
    return _NC_CACHE["nc"]


def kernel(L0_attn_x, L0_attn_w_norm, L0_attn_W_in, L0_attn_w_conv, L0_attn_dt_bias,
           L0_attn_A_log, L0_attn_W_h2i, L0_attn_W_out, L0_ffn_w_gate, L0_ffn_wi_0,
           L0_ffn_wi_1, L0_ffn_wo):
    f32 = np.float32
    x = np.asarray(L0_attn_x, f32)
    W_in = np.asarray(L0_attn_W_in, f32)
    w_conv = np.asarray(L0_attn_w_conv, f32)
    W_h2i = np.asarray(L0_attn_W_h2i, f32)
    W_out = np.asarray(L0_attn_W_out, f32)
    w_gate = np.asarray(L0_ffn_w_gate, f32)
    wi_0 = np.asarray(L0_ffn_wi_0, f32)
    wi_1 = np.asarray(L0_ffn_wi_1, f32)
    wo = np.asarray(L0_ffn_wo, f32)

    wi0h = wi_0.reshape(8, 16, 128, 2, 512).transpose(0, 3, 2, 1, 4)
    wi1h = wi_1.reshape(8, 16, 128, 2, 512).transpose(0, 3, 2, 1, 4)
    wih = np.ascontiguousarray(
        np.stack([wi0h[:, 0], wi0h[:, 1], wi1h[:, 0], wi1h[:, 1]], axis=1)
    ).reshape(32, 128, 16 * 512)
    woh = np.ascontiguousarray(
        wo.reshape(8, 8, 128, 2, 1024).transpose(0, 3, 2, 1, 4)
    ).reshape(16, 128, 8 * 1024)
    Wp = np.zeros((8832, 2048), f32)
    Wp[0:8192] = W_in[0:8192]
    Wp[8192:8208] = W_in[8192:8208]
    Wp[8320:8576] = W_in[8208:8464]
    Wp[8576:8832] = W_in[8464:8720]
    wint = np.ascontiguousarray(
        Wp.T.reshape(16, 128, 69, 128).transpose(2, 1, 0, 3)
    ).reshape(69, 128, 16 * 128)
    wott = np.ascontiguousarray(W_out.T).reshape(32, 128, 2048)
    wh2it = np.ascontiguousarray(W_h2i.T)
    wct = np.ascontiguousarray(
        w_conv.reshape(4096, 4).reshape(32, 128, 4).transpose(1, 0, 2)
    ).reshape(128, 128)
    wnt = np.ascontiguousarray(np.asarray(L0_attn_w_norm, f32).reshape(16, 128).T)
    wgt = np.ascontiguousarray(
        w_gate.reshape(16, 128, 8).transpose(1, 0, 2)
    ).reshape(128, 128)
    dtbv = np.asarray(L0_attn_dt_bias, f32).reshape(16, 1)
    alogv = np.asarray(L0_attn_A_log, f32).reshape(16, 1)
    f = np.arange(4096)
    indt = np.ascontiguousarray(
        (f[:, None] // 256 == np.arange(16)[None, :]).astype(f32)
        .reshape(32, 128, 16).transpose(1, 0, 2)
    ).reshape(128, 512)
    pidx = np.arange(128)
    ind2 = np.zeros((128, 32), f32)
    ind2[:, 0:16] = (pidx[:, None] // 16 == np.arange(16)[None, :])
    ind2[:, 16:32] = (pidx[:, None] // 16 + 8 == np.arange(16)[None, :])
    rrep = np.zeros((16, 256), f32)
    for t in range(2):
        rrep[:, t * 128:(t + 1) * 128] = (
            np.arange(16)[:, None] == (t * 8 + pidx[None, :] // 16))
    ones = np.ones((1, 128), f32)
    idm = np.eye(128, dtype=f32)

    shared = dict(wg=wgt, wih=wih, woh=woh, wint=wint, wott=wott, wh2it=wh2it,
                  wct=wct, wnt=wnt, dtb=dtbv, alog=alogv, indt=indt, ind2=ind2,
                  rrep=rrep, ones_f=ones, ones_b=ones, ones_c=np.ones((128, 1), f32),
                  id_f=idm, id_b=idm)

    in_maps = []
    for k in range(8):
        beta, pos = k // 4, k % 4
        xk = np.ascontiguousarray(x[beta, pos * T:(pos + 1) * T, :].T)
        selm = np.zeros((128, 8), f32)
        selprev = np.zeros((128, 8), f32)
        for j in range(8):
            if j // 4 == beta and j % 4 < pos:
                selm[:, j] = 1.0
        if pos > 0:
            selprev[:, k - 1] = 1.0
        m = dict(shared)
        m.update(xin=xk, selm=selm, invselm=np.ascontiguousarray(1.0 - selm),
                 selprev=selprev)
        in_maps.append(m)

    nc = _get_nc()
    res = run_bass_kernel_spmd(nc, in_maps, core_ids=list(range(8)))
    out = np.empty((2, 4096, 2048), f32)
    for k in range(8):
        beta, pos = k // 4, k % 4
        out[beta, pos * T:(pos + 1) * T, :] = res.results[k]["oxt"].T
    return out


# version 05
# speedup vs baseline: 1.0393x; 1.0393x over previous
import contextlib

import numpy as np
import concourse.bass as bass
import concourse.mybir as mybir
from concourse.bass_utils import run_bass_kernel_spmd

F32 = mybir.dt.float32
BF16 = mybir.dt.bfloat16
AF = mybir.ActivationFunctionType
ALU = mybir.AluOpType
AX = mybir.AxisListType

T = 1024
D = 2048
NKC = 16
NE = 8
H = 16
NZC = 32
EPS_LN = 1e-5
EPS_RMS = 1.1920929e-07

WSLOT = (0, 8192, 16384)


class _Dummy:
    def __getattr__(self, name):
        def f(*a, **k):
            return None
        return f


_DUM = _Dummy()


class Prog:

    def __init__(self, emit, eng, sems):
        self.emit = emit
        self.eng = eng
        self.sems = sems
        self.cnt = {k: 0 for k in sems}
        self.ms = {}
        self.uid = 0

    def E(self, name):
        return self.eng if self.emit == name else _DUM

    @property
    def g(self):
        return self.E('g')

    @property
    def v(self):
        return self.E('v')

    @property
    def s(self):
        return self.E('s')

    @property
    def t(self):
        return self.E('t')

    @property
    def y(self):
        return self.E('y')

    def inc(self, inst, sem, k=1):
        if inst is not None:
            inst.then_inc(self.sems[sem], k)
        self.cnt[sem] += k

    def mark(self, name, sem):
        assert name not in self.ms, name
        self.ms[name] = (sem, self.cnt[sem])
        return name

    def mkid(self, tag):
        self.uid += 1
        return f"{tag}_{self.uid}"

    def mk(self, tag, sem):
        return self.mark(self.mkid(tag), sem)

    def wait(self, eng_name, msname):
        if msname is None:
            return
        sem, val = self.ms[msname]
        if self.emit == eng_name and val > 0:
            self.eng.wait_ge(self.sems[sem], val)


def build_program(debug=False):
    nc = bass.Bass("TRN2", target_bir_lowering=False, debug=False)

    def inp(name, shape):
        return nc.dram_tensor(name, list(shape), F32, kind="ExternalInput")

    xin = inp("xin", (D, T))
    wg = inp("wg", (128, 128))
    wih = inp("wih", (NE * 4, 128, 16 * 512))
    woh = inp("woh", (NE * 2, 128, 8 * 1024))
    wint = inp("wint", (69, 128, 16 * 128))
    wott = inp("wott", (32, 128, 2048))
    wh2it = inp("wh2it", (16, 4096))
    wct = inp("wct", (128, 128))
    wnt = inp("wnt", (128, 16))
    dtb = inp("dtb", (16, 1))
    alog = inp("alog", (16, 1))
    indt = inp("indt", (128, 512))
    ind2 = inp("ind2", (128, 32))
    rrep = inp("rrep", (16, 256))
    ones_f = inp("ones_f", (1, 128))
    ones_b = inp("ones_b", (1, 128))
    ones_c = inp("ones_c", (128, 1))
    epsr = inp("epsr", (1, 8))
    id_f = inp("id_f", (128, 128))
    id_b = inp("id_b", (128, 128))
    selm = inp("selm", (128, 8))
    invselm = inp("invselm", (128, 8))
    selprev = inp("selprev", (128, 8))

    oxt = nc.dram_tensor("oxt", [D, T], F32, kind="ExternalOutput")

    dbg = {}
    if debug:
        dbg["d_xn"] = nc.dram_tensor("d_xn", [128, 16 * T], BF16, kind="ExternalOutput")
        dbg["d_g8"] = nc.dram_tensor("d_g8", [8, 1024], BF16, kind="ExternalOutput")
        dbg["d_r1"] = nc.dram_tensor("d_r1", [128, 16 * T], F32, kind="ExternalOutput")
        dbg["d_dt"] = nc.dram_tensor("d_dt", [16, T], F32, kind="ExternalOutput")
        dbg["d_xs"] = nc.dram_tensor("d_xs", [16, T], F32, kind="ExternalOutput")
        dbg["d_scan"] = nc.dram_tensor("d_scan", [128, 8192], BF16, kind="ExternalOutput")
        dbg["d_yb"] = nc.dram_tensor("d_yb", [16, T], BF16, kind="ExternalOutput")
        dbg["d_r2"] = nc.dram_tensor("d_r2", [128, 16 * T], F32, kind="ExternalOutput")
        dbg["d_hc"] = nc.dram_tensor("d_hc", [128, 2], F32, kind="ExternalOutput")
        dbg["d_xn2"] = nc.dram_tensor("d_xn2", [128, 16 * T], BF16, kind="ExternalOutput")
        dbg["d_gw"] = nc.dram_tensor("d_gw", [128, 64], F32, kind="ExternalOutput")
        dbg["d_gc"] = nc.dram_tensor("d_gc", [128, 56], F32, kind="ExternalOutput")
        dbg["d_msk"] = nc.dram_tensor("d_msk", [128, 64], F32, kind="ExternalOutput")
        dbg["d_exp"] = nc.dram_tensor("d_exp", [128, 64], F32, kind="ExternalOutput")
        dbg["d_lgf"] = nc.dram_tensor("d_lgf", [8, 1024], F32, kind="ExternalOutput")
        dbg["d_int"] = nc.dram_tensor("d_int", [128, 8192], BF16, kind="ExternalOutput")
    cc1_in = nc.dram_tensor("cc1_in", [1, 32 * 128 * 3], BF16)
    cc1_out = nc.dram_tensor("cc1_out", [8, 32 * 128 * 3], BF16)
    cc2_in = nc.dram_tensor("cc2_in", [1, 272], F32)
    cc2_out = nc.dram_tensor("cc2_out", [8, 272], F32)

    es = contextlib.ExitStack()
    sbuf = lambda name, shape, dt: es.enter_context(nc.sbuf_tensor(name, list(shape), dt))
    psum = lambda name, shape, dt: es.enter_context(nc.psum_tensor(name, list(shape), dt))

    resid = sbuf("resid", (128, 16 * T), F32)
    xnr = sbuf("xnr", (128, 16 * T), BF16)
    wreg = sbuf("wreg", (128, 3 * 8192), BF16)
    inter = sbuf("inter", (128, 8192), BF16)
    ygr = sbuf("ygr", (128, 4096), BF16)
    poolf = sbuf("poolf", (128, 4096), F32)
    c_wg = sbuf("c_wg", (128, 128), BF16)
    c_wc = sbuf("c_wc", (128, 128), F32)
    c_wn = sbuf("c_wn", (128, 16), F32)
    c_dtb = sbuf("c_dtb", (16, 1), F32)
    c_alog = sbuf("c_alog", (16, 1), F32)
    c_acol = sbuf("c_acol", (16, 1), F32)
    c_ind = sbuf("c_ind", (128, 512), BF16)
    c_ind2 = sbuf("c_ind2", (128, 32), BF16)
    c_r = sbuf("c_r", (16, 256), F32)
    c_1f = sbuf("c_1f", (1, 128), F32)
    c_1b = sbuf("c_1b", (1, 128), BF16)
    c_1bc = sbuf("c_1bc", (128, 1), BF16)
    c_eps = sbuf("c_eps", (1, 8), F32)
    c_idf = sbuf("c_idf", (128, 128), F32)
    c_idb = sbuf("c_idb", (128, 128), BF16)
    c_sel = sbuf("c_sel", (128, 8), F32)
    c_isel = sbuf("c_isel", (128, 8), F32)
    c_sprev = sbuf("c_sprev", (128, 8), F32)
    gw = sbuf("gw", (128, 64), F32)
    ge1 = sbuf("ge1", (128, 64), F32)
    glm = sbuf("glm", (128, 64), F32)
    gsub = sbuf("gsub", (128, 64), F32)
    gmsk = sbuf("gmsk", (128, 64), F32)
    gexp = sbuf("gexp", (128, 64), F32)
    gpp = sbuf("gpp", (128, 64), F32)
    gcols = sbuf("gcols", (128, 56), F32)
    gwb = sbuf("gwb", (128, 64), BF16)
    grow = sbuf("grow", (1, 1024), BF16)
    yb16 = sbuf("yb16", (16, T), BF16)
    halo_o = sbuf("halo_o", (128, 96), BF16)
    halo_i = sbuf("halo_i", (128, 96), BF16)
    xi_f3 = sbuf("xi_f3", (128, 96), BF16)
    comb6 = sbuf("comb6", (128, 192), BF16)
    xcfixf = sbuf("xcfixf", (128, 96), F32)
    xcfixb = sbuf("xcfixb", (128, 96), BF16)
    ccr = sbuf("ccr", (128, 192), BF16)
    fcol = sbuf("fcol", (128, 2), F32)
    hcol = sbuf("hcol", (128, 2), F32)
    hb = sbuf("hb", (128, 2), BF16)
    fjall = sbuf("fjall", (128, 16), F32)
    djall = sbuf("djall", (16, 8), F32)
    djrep = sbuf("djrep", (128, 2), F32)
    djeff = sbuf("djeff", (128, 2), F32)
    fjeff = sbuf("fjeff", (128, 2), F32)
    d16 = sbuf("d16", (16, 1), F32)
    ls16 = sbuf("ls16", (16, 1), F32)

    pb = [psum(f"pb{i}", (128, 512), F32) for i in range(7)]
    pbf = psum("pbf", (128, 1024), BF16)

    sems = {k: es.enter_context(nc.semaphore(k)) for k in
            ("wd", "sd", "mm", "ve", "se", "cc")}

    IT_XI = 0
    IT_WIN = (1024, 3072)
    IT_WH2I = (5120, 5632)
    IT_XCS = (6144, 7168)
    YG_GSB = (0, 1024)
    YG_G8 = 2048
    YG_H1T = 3072
    PF_XC = 0
    PF_DT = 1024
    PF_SH = 2048
    PF_LGF = 3072
    SP_B = (0, 1024)
    SP_C = (2048, 3072)
    SP_H = (4096, 5120)
    SP_DA = (6144, 7168)

    R_MEAN = poolf[0:1, 0:1024]
    R_VAR = poolf[0:1, 1024:2048]
    R_SCR = poolf[0:1, 2048:3072]
    R_A = poolf[0:1, 3072:4096]
    R_RMS = poolf[0:1, 2048:3072]

    wih_ap = wih.ap()
    woh_ap = woh.ap()
    wint_ap = wint.ap()
    wott_ap = wott.ap()

    def program(p: Prog):
        def tap(name, src_ap, ms, eng='v'):
            if not debug:
                return
            p.wait('y', ms)
            i = p.y.dma_start(dbg[name].ap(), src_ap)
            p.inc(i, 'sd', 16)
        bank_free = {i: None for i in range(8)}
        slot_ms = {0: None, 1: None, 2: None}
        wload = {}

        def mm_group(bank, mms, tag):
            p.wait('t', bank_free[bank])
            n = len(mms)
            last = None
            for i, (o, l, r) in enumerate(mms):
                last = p.t.matmul(o, l, r, start=(i == 0), stop=(i == n - 1))
            p.inc(last, 'mm')
            return p.mk(tag, 'mm')

        def evac_v(bank, grp, fn, tag):
            p.wait('v', grp)
            inst = fn(p.v)
            p.inc(inst, 've')
            name = p.mk(tag, 've')
            bank_free[bank] = name
            return name

        def evac_s(bank, grp, fn, tag):
            p.wait('s', grp)
            inst = fn(p.s)
            p.inc(inst, 'se')
            name = p.mk(tag, 'se')
            bank_free[bank] = name
            return name

        i = p.y.dma_start(resid[:, :].rearrange("q (c t) -> q c t", t=T),
                          xin.ap().rearrange("(c q) t -> q c t", q=128))
        p.inc(i, 'sd', 16)
        p.mark("x_loaded", 'sd')
        for dst, src in ((c_wc, wct), (c_wn, wnt), (c_dtb, dtb), (c_alog, alog),
                         (c_r, rrep), (c_1f, ones_f), (c_idf, id_f), (c_sel, selm),
                         (c_isel, invselm), (c_sprev, selprev), (c_eps, epsr)):
            i = p.y.dma_start(dst[:, :], src[:, :])
            p.inc(i, 'sd', 16)
        p.mark("consts_f", 'sd')
        for dst, src in ((c_wg, wg), (c_ind, indt), (c_ind2, ind2),
                         (c_1b, ones_b), (c_1bc, ones_c), (c_idb, id_b)):
            i = p.g.dma_start(dst[:, :], src[:, :])
            p.inc(i, 'wd', 16)
        p.mark("consts_b", 'wd')

        def load_weight(w):
            slot = w % 3
            e = (w % 48) // 6
            j = w % 6
            p.wait('g', slot_ms[slot])
            if j < 4:
                src = wih_ap[e * 4 + j]
            else:
                src = woh_ap[e * 2 + (j - 4)]
            i = p.g.dma_start(wreg[:, WSLOT[slot]:WSLOT[slot] + 8192], src)
            p.inc(i, 'wd', 16)
            wload[w] = (p.mk(f"wl{w}", 'wd'), slot)

        for w in (0, 1, 2):
            load_weight(w)

        def emit_ln(tag, with_wn, resid_ms, resid_sem_eng, xn_free_ms):
            XB = (5120, 6144)
            XQ = (7168, 0)
            p.wait('s', resid_ms)
            chunk_mm = [None, None]
            for c in range(NKC):
                xb = inter[:, XB[c % 2]:XB[c % 2] + 1024]
                xq = inter[:, XQ[c % 2]:XQ[c % 2] + 1024]
                p.wait('s', chunk_mm[c % 2])
                i = p.s.activation(xb, resid[:, c * T:(c + 1) * T], AF.Copy)
                p.inc(i, 'se')
                cast_ms = p.mk(f"{tag}cb", 'se')
                p.wait('v', cast_ms)
                i = p.v.tensor_mul(xq, xb, xb)
                p.inc(i, 've')
                sq_ms = p.mk(f"{tag}sq", 've')
                p.wait('t', sq_ms)
                if c == 0:
                    for b in range(4):
                        p.wait('t', bank_free[b])
                last = None
                for piece in range(2):
                    last = p.t.matmul(pb[piece][0:1, :], c_1bc[:, 0:1],
                                      xb[:, piece * 512:piece * 512 + 512],
                                      start=(c == 0), stop=(c == NKC - 1))
                    last = p.t.matmul(pb[2 + piece][0:1, :], c_1bc[:, 0:1],
                                      xq[:, piece * 512:piece * 512 + 512],
                                      start=(c == 0), stop=(c == NKC - 1))
                p.inc(last, 'mm')
                chunk_mm[c % 2] = p.mk(f"{tag}cm", 'mm')
            stat_grp = chunk_mm[(NKC - 1) % 2]
            p.wait('v', stat_grp)
            p.v.tensor_copy(R_MEAN[:, 0:512], pb[0][0:1, :])
            p.v.tensor_copy(R_MEAN[:, 512:1024], pb[1][0:1, :])
            p.v.tensor_copy(R_VAR[:, 0:512], pb[2][0:1, :])
            i = p.v.tensor_copy(R_VAR[:, 512:1024], pb[3][0:1, :])
            p.inc(i, 've')
            ev = p.mk(f"{tag}sev", 've')
            for b in range(4):
                bank_free[b] = ev
            p.wait('v', ev)
            p.v.tensor_scalar(R_MEAN, R_MEAN, 1.0 / D, None, ALU.mult)
            p.v.tensor_mul(R_SCR, R_MEAN, R_MEAN)
            p.v.tensor_scalar(R_VAR, R_VAR, 1.0 / D, None, ALU.mult)
            i = p.v.tensor_sub(R_VAR, R_VAR, R_SCR)
            p.inc(i, 've')
            var_ms = p.mk(f"{tag}var", 've')
            p.wait('s', var_ms)
            if with_wn:
                p.s.activation(R_A, R_VAR, AF.Ln,
                               scale=c_eps[0:1, 2:3], bias=c_eps[0:1, 3:4])
                p.s.activation(R_A, R_A, AF.Exp, scale=c_eps[0:1, 1:2])
                p.s.activation(R_SCR, R_VAR, AF.Ln, bias=c_eps[0:1, 0:1])
                i = p.s.activation(R_SCR, R_SCR, AF.Exp, scale=c_eps[0:1, 1:2])
            else:
                p.s.activation(R_A, R_VAR, AF.Ln, bias=c_eps[0:1, 0:1])
                i = p.s.activation(R_A, R_A, AF.Exp, scale=c_eps[0:1, 1:2])
            p.inc(i, 'se')
            rstd_ms = p.mk(f"{tag}rstd", 'se')
            p.wait('v', rstd_ms)
            if with_wn:
                p.v.tensor_mul(R_VAR, R_MEAN, R_A)
                p.v.tensor_scalar(R_VAR, R_VAR, -1.0, None, ALU.mult)
                p.v.tensor_mul(R_MEAN, R_MEAN, R_SCR)
                i = p.v.tensor_scalar(R_MEAN, R_MEAN, -1.0, None, ALU.mult)
            else:
                p.v.tensor_mul(R_VAR, R_MEAN, R_A)
                i = p.v.tensor_scalar(R_VAR, R_VAR, -1.0, None, ALU.mult)
            p.inc(i, 've')
            ab_ms = p.mk(f"{tag}ab", 've')
            p.wait('t', ab_ms)
            for piece in range(2):
                for bi0, row in ((4, R_A), (0, R_VAR)):
                    bi = bi0 + piece
                    p.wait('t', bank_free[bi])
                    i = p.t.matmul(pb[bi][:, :], c_1f[0:1, :],
                                   row[:, piece * 512:piece * 512 + 512],
                                   start=True, stop=True)
                    p.inc(i, 'mm')
            if with_wn:
                for piece in range(2):
                    p.wait('t', bank_free[2 + piece])
                    i = p.t.matmul(pb[2 + piece][:, :], c_1f[0:1, :],
                                   R_SCR[:, piece * 512:piece * 512 + 512],
                                   start=True, stop=True)
                    p.inc(i, 'mm')
                p.wait('t', bank_free[6])
                i = p.t.matmul(pb[6][:, :], c_1f[0:1, :], R_MEAN[:, 0:512],
                               start=True, stop=True)
                p.inc(i, 'mm')
            bc_ms = p.mk(f"{tag}bc", 'mm')
            p.wait('v', bc_ms)
            p.wait('v', xn_free_ms)
            scr = inter[:, 5120:5632]
            last = None
            for c in range(NKC):
                for piece in range(2):
                    sl = slice(c * T + piece * 512, c * T + piece * 512 + 512)
                    p.v.tensor_mul(scr, resid[:, sl], pb[4 + piece][:, :])
                    if with_wn:
                        p.v.tensor_add(scr, scr, pb[piece][:, :])
                        last = p.v.tensor_scalar(xnr[:, sl], scr,
                                                 c_wn[:, c:c + 1], None, ALU.mult)
                    else:
                        last = p.v.tensor_add(xnr[:, sl], scr, pb[piece][:, :])
            p.inc(last, 've')
            xn_ms = p.mk(f"{tag}xn", 've')
            for b in (0, 1, 4, 5):
                bank_free[b] = xn_ms
            if tag == "ln0":
                tap("d_xn", xnr[:, :], xn_ms)
            if tag == "ln1m":
                tap("d_xn2", xnr[:, :], xn_ms)
            if not with_wn:
                return xn_ms, None

            lag = {"items": [(piece, c) for piece in range(2) for c in range(NKC)],
                   "idx": 0, "scr2": inter[:, 5632:6144], "last": None}

            def lag_pump(n):
                while n > 0 and lag["idx"] < 32:
                    piece, c = lag["items"][lag["idx"]]
                    if lag["idx"] == 16:
                        pm = p.mk(f"{tag}lagp0", 've')
                        p.wait('t', pm)
                        i = p.t.matmul(pb[6][:, :], c_1f[0:1, :],
                                       R_MEAN[:, 512:1024], start=True, stop=True)
                        p.inc(i, 'mm')
                        rb = p.mk(f"{tag}lagrb", 'mm')
                        p.wait('v', rb)
                    sl = slice(c * T + piece * 512, c * T + piece * 512 + 512)
                    p.v.tensor_mul(lag["scr2"], resid[:, sl], pb[2 + piece][:, :])
                    p.v.tensor_add(lag["scr2"], lag["scr2"], pb[6][:, :])
                    i = p.v.tensor_add(resid[:, sl], resid[:, sl], lag["scr2"])
                    p.inc(i, 've')
                    lag["idx"] += 1
                    n -= 1
                if lag["idx"] == 32 and lag["last"] is None:
                    lag["last"] = p.mk(f"{tag}lagend", 've')
                    for b in (2, 3, 6):
                        bank_free[b] = lag["last"]

            return xn_ms, lag_pump

        def emit_gate(call, xn_ms):
            tag = f"g{call}"
            grps = []
            p.wait('t', xn_ms)
            for piece in range(2):
                mms = [(pb[2 + piece][0:8, :], c_wg[:, kc * 8:kc * 8 + 8],
                        xnr[:, kc * T + piece * 512:kc * T + piece * 512 + 512])
                       for kc in range(NKC)]
                grps.append(mm_group(2 + piece, mms, f"{tag}lg"))
            lgf = poolf[0:8, PF_LGF:PF_LGF + 1024]
            p.wait('v', grps[1])
            p.v.tensor_copy(lgf[:, 0:512], pb[2][0:8, :])
            i = p.v.tensor_copy(lgf[:, 512:1024], pb[3][0:8, :])
            p.inc(i, 've')
            lg_ms = p.mk(f"{tag}lgf", 've')
            bank_free[2] = lg_ms
            bank_free[3] = lg_ms
            p.wait('t', lg_ms)
            p.wait('t', bank_free[6])
            last = None
            for tc in range(8):
                last = p.t.transpose(pb[6][:, tc * 8:tc * 8 + 8],
                                     lgf[:, tc * 128:(tc + 1) * 128], c_idf[0:8, 0:8])
            p.inc(last, 'mm')
            tr_ms = p.mk(f"{tag}tr", 'mm')
            p.wait('v', tr_ms)
            i = p.v.tensor_copy(gw[:, :], pb[6][:, 0:64])
            p.inc(i, 've')
            bank_free[6] = p.mk(f"{tag}gw", 've')
            m1 = gcols[:, 0:8]
            m2 = gcols[:, 8:16]
            c1 = gcols[:, 16:24]
            tie = gcols[:, 24:32]
            m2e = gcols[:, 32:40]
            zz = gcols[:, 40:48]
            rz = gcols[:, 48:56]
            last = None
            for tc in range(8):
                last = p.v.reduce_max(m1[:, tc:tc + 1], gw[:, tc * 8:tc * 8 + 8], AX.X)
            p.inc(last, 've')
            p.wait('v', p.mk(f"{tag}s1", 've'))
            last = None
            for tc in range(8):
                p.v.tensor_scalar(ge1[:, tc * 8:tc * 8 + 8], gw[:, tc * 8:tc * 8 + 8],
                                  m1[:, tc:tc + 1], None, ALU.is_ge)
                last = p.v.tensor_scalar(gsub[:, tc * 8:tc * 8 + 8], gw[:, tc * 8:tc * 8 + 8],
                                         m1[:, tc:tc + 1], None, ALU.subtract)
            p.inc(last, 've')
            p.wait('v', p.mk(f"{tag}s2a", 've'))
            i = p.v.scalar_tensor_tensor(glm[:, :], ge1[:, :], -1.0e30, gw[:, :],
                                         ALU.mult, ALU.add)
            p.inc(i, 've')
            p.wait('v', p.mk(f"{tag}s2b", 've'))
            last = None
            for tc in range(8):
                p.v.reduce_max(m2[:, tc:tc + 1], glm[:, tc * 8:tc * 8 + 8], AX.X)
                last = p.v.reduce_sum(c1[:, tc:tc + 1], ge1[:, tc * 8:tc * 8 + 8], AX.X)
            p.inc(last, 've')
            sub_ms = p.mk(f"{tag}s2", 've')
            p.wait('v', sub_ms)
            i = p.v.tensor_scalar(tie, c1, 2.0, None, ALU.is_ge)
            p.inc(i, 've')
            p.wait('v', p.mk(f"{tag}s3a", 've'))
            i = p.v.tensor_sub(m2e, m1, m2)
            p.inc(i, 've')
            p.wait('v', p.mk(f"{tag}s3b", 've'))
            i = p.v.tensor_mul(m2e, m2e, tie)
            p.inc(i, 've')
            p.wait('v', p.mk(f"{tag}s3c", 've'))
            i = p.v.tensor_add(m2e, m2e, m2)
            p.inc(i, 've')
            p.wait('v', p.mk(f"{tag}s3", 've'))
            last = None
            for tc in range(8):
                last = p.v.tensor_scalar(gmsk[:, tc * 8:tc * 8 + 8],
                                         gw[:, tc * 8:tc * 8 + 8],
                                         m2e[:, tc:tc + 1], None, ALU.is_ge)
            p.inc(last, 've')
            msk_ms = p.mk(f"{tag}s4", 've')
            p.wait('v', msk_ms)
            p.wait('s', sub_ms)
            i = p.s.activation(gexp[:, :], gsub[:, :], AF.Exp)
            p.inc(i, 'se')
            exp_ms = p.mk(f"{tag}exp", 'se')
            p.wait('v', exp_ms)
            i = p.v.tensor_mul(gpp[:, :], gexp[:, :], gmsk[:, :])
            p.inc(i, 've')
            p.wait('v', p.mk(f"{tag}s5a", 've'))
            last = None
            for tc in range(8):
                last = p.v.reduce_sum(zz[:, tc:tc + 1], gpp[:, tc * 8:tc * 8 + 8], AX.X)
            p.inc(last, 've')
            p.wait('v', p.mk(f"{tag}s5", 've'))
            i = p.v.reciprocal(rz, zz)
            p.inc(i, 've')
            p.wait('v', p.mk(f"{tag}s6", 've'))
            last = None
            for tc in range(8):
                last = p.v.tensor_scalar(gwb[:, tc * 8:tc * 8 + 8],
                                         gpp[:, tc * 8:tc * 8 + 8],
                                         rz[:, tc:tc + 1], 0.5, ALU.mult, ALU.mult)
            p.inc(last, 've')
            w_ms = p.mk(f"{tag}w", 've')
            p.wait('t', w_ms)
            p.wait('t', bank_free[7])
            last = None
            for tc in range(8):
                last = p.t.transpose(pbf[0:8, tc * 128:(tc + 1) * 128],
                                     gwb[:, tc * 8:tc * 8 + 8], c_idb[:, :])
            p.inc(last, 'mm')
            gtr_ms = p.mk(f"{tag}gtr", 'mm')
            p.wait('v', gtr_ms)
            i = p.v.tensor_copy(ygr[0:8, YG_G8:YG_G8 + 1024], pbf[0:8, :])
            p.inc(i, 've')
            g8_ms = p.mk(f"{tag}g8", 've')
            bank_free[7] = g8_ms
            if call == 0:
                tap("d_g8", ygr[0:8, YG_G8:YG_G8 + 1024], g8_ms)
                tap("d_gw", gw[:, :], g8_ms)
                tap("d_gc", gcols[:, :], g8_ms)
                tap("d_msk", gmsk[:, :], g8_ms)
                tap("d_exp", gexp[:, :], g8_ms)
                tap("d_lgf", lgf[:, :], g8_ms)
            return g8_ms

        state = {"grow_pe": None, "h1t_last": [None, None], "hping": [0],
                 "woping": [0]}

        def gsb_make(call, e, g8_ms):
            p.wait('y', g8_ms)
            p.wait('y', state["grow_pe"])
            i = p.y.dma_start(grow[0:1, :], ygr[e:e + 1, YG_G8:YG_G8 + 1024])
            p.inc(i, 'sd', 16)
            grow_ms = p.mk(f"c{call}e{e}grow", 'sd')
            p.wait('t', grow_ms)
            last_pe = None
            for piece in range(2):
                p.wait('t', bank_free[6])
                i = p.t.matmul(pb[6][:, :], c_1b[0:1, :],
                               grow[0:1, piece * 512:piece * 512 + 512],
                               start=True, stop=True)
                p.inc(i, 'mm')
                grp = p.mk(f"c{call}e{e}gb", 'mm')
                if piece == 1:
                    last_pe = grp
                evac_v(6, grp, lambda v, piece=piece, e=e:
                       v.tensor_copy(ygr[:, YG_GSB[e % 2] + piece * 512:
                                          YG_GSB[e % 2] + piece * 512 + 512],
                                     pb[6][:, :]), f"c{call}e{e}ge")
            state["grow_pe"] = last_pe

        def ffn_h(call, e, w, which):
            wms, slot = wload[w]
            base = WSLOT[slot]
            p.wait('t', wms)
            gsb_off = YG_GSB[e % 2]
            last_grp = None
            for ml in range(4):
                m = (which % 2) * 4 + ml
                for piece in range(2):
                    bank = state["hping"][0]
                    state["hping"][0] ^= 1
                    mms = [(pb[bank][:, :],
                            wreg[:, base + kc * 512 + ml * 128:
                                 base + kc * 512 + ml * 128 + 128],
                            xnr[:, kc * T + piece * 512:kc * T + piece * 512 + 512])
                           for kc in range(NKC)]
                    grp = mm_group(bank, mms, f"c{call}e{e}h{which}")
                    last_grp = grp
                    s_sl = inter[:, m * 1024 + piece * 512:m * 1024 + piece * 512 + 512]
                    if which < 2:
                        evac_s(bank, grp, lambda s, s_sl=s_sl, bank=bank:
                               s.activation(s_sl, pb[bank][:, :], AF.Silu),
                               f"c{call}e{e}sil")
                    else:
                        h1t = ygr[:, YG_H1T + piece * 512:YG_H1T + piece * 512 + 512]
                        p.wait('s', state["h1t_last"][piece])
                        ems = evac_s(bank, grp, lambda s, h1t=h1t, bank=bank:
                                     s.activation(h1t, pb[bank][:, :], AF.Copy),
                                     f"c{call}e{e}h1c")
                        p.wait('v', ems)
                        gsb = ygr[:, gsb_off + piece * 512:gsb_off + piece * 512 + 512]
                        p.v.tensor_mul(s_sl, s_sl, gsb)
                        i = p.v.tensor_mul(s_sl, s_sl, h1t)
                        p.inc(i, 've')
                        state["h1t_last"][piece] = p.mk(f"c{call}e{e}im", 've')
            return last_grp

        def ffn_wo(call, e, w, half, inter_ms):
            wms, slot = wload[w]
            base = WSLOT[slot]
            p.wait('t', wms)
            p.wait('t', inter_ms)
            last_add = None
            for ocl in range(8):
                oc = half * 8 + ocl
                for piece in range(2):
                    bank = 2 + state["woping"][0]
                    state["woping"][0] ^= 1
                    mms = [(pb[bank][:, :],
                            wreg[:, base + ic * 1024 + ocl * 128:
                                 base + ic * 1024 + ocl * 128 + 128],
                            inter[:, ic * 1024 + piece * 512:ic * 1024 + piece * 512 + 512])
                           for ic in range(8)]
                    grp = mm_group(bank, mms, f"c{call}e{e}wo")
                    sl = slice(oc * T + piece * 512, oc * T + piece * 512 + 512)
                    last_add = evac_v(bank, grp, lambda v, sl=sl, bank=bank:
                                      v.tensor_add(resid[:, sl], resid[:, sl],
                                                   pb[bank][:, :]),
                                      f"c{call}e{e}ra")
            return last_add

        def emit_ffn(call, resid_ms, resid_eng, xn_free_ms):
            xn_ms, _ = emit_ln(f"ln{call * 2}", False, resid_ms, resid_eng, xn_free_ms)
            g8_ms = emit_gate(call, xn_ms)
            wbase = call * 48
            last_resid = None
            for e in range(NE):
                for ph in range(6):
                    w = wbase + e * 6 + ph
                    if ph < 4:
                        done = ffn_h(call, e, w, ph)
                        if ph == 0:
                            gsb_make(call, e, g8_ms)
                        if ph == 3:
                            inter_ms = p.mk(f"c{call}e{e}int", 've')
                            if call == 0 and e == 0:
                                tap("d_int", inter[:, :], inter_ms)
                    else:
                        done = ffn_wo(call, e, w, ph - 4, inter_ms)
                        if ph == 5:
                            last_resid = done
                    slot_ms[w % 3] = done
                    if (w % 48) < 45:
                        load_weight(w + 3)
            return last_resid

        resid_ms = emit_ffn(0, "x_loaded", 'sd', None)
        tap("d_r1", resid[:, :], resid_ms)

        xnfree0 = p.mk("xnfree0", 'mm')
        xn2_ms, lag_pump = emit_ln("ln1m", True, resid_ms, 've', xnfree0)

        win_last = [xnfree0, xnfree0]

        def load_wintile(oc, idx):
            sl = idx % 2
            p.wait('g', win_last[sl])
            i = p.g.dma_start(inter[:, IT_WIN[sl]:IT_WIN[sl] + 2048], wint_ap[oc])
            p.inc(i, 'wd', 16)
            return p.mk(f"win{oc}", 'wd'), sl

        def proj_groups(oc, idx, tag):
            wms, sl = win_tiles[idx]
            p.wait('t', wms)
            grps = []
            for piece in range(2):
                bank = state["hping"][0]
                state["hping"][0] ^= 1
                mms = [(pb[bank][:, :],
                        inter[:, IT_WIN[sl] + kc * 128:IT_WIN[sl] + kc * 128 + 128],
                        xnr[:, kc * T + piece * 512:kc * T + piece * 512 + 512])
                       for kc in range(NKC)]
                grps.append((bank, mm_group(bank, mms, tag)))
            win_last[sl] = grps[1][1]
            return grps

        p.wait('t', xn2_ms)
        proj_ocs = [64, 65, 66, 67, 68] + list(range(32, 64))
        win_tiles = {}
        for idx, oc in enumerate(proj_ocs[:2]):
            win_tiles[idx] = load_wintile(oc, idx)

        grps = proj_groups(64, 0, "pdt")
        lag_pump(7)
        win_tiles[2] = load_wintile(proj_ocs[2], 2)
        for piece, (bank, grp) in enumerate(grps):
            evac_v(bank, grp, lambda v, piece=piece, bank=bank:
                   v.tensor_copy(poolf[0:16, PF_DT + piece * 512:PF_DT + piece * 512 + 512],
                                 pb[bank][0:16, :]), "dtev")
        dtraw_ms = p.mk("dtraw", 've')
        dt = poolf[0:16, PF_DT:PF_DT + 1024]
        sh = poolf[0:16, PF_SH:PF_SH + 1024]
        i = p.v.tensor_scalar(dt, dt, c_dtb[:, 0:1], None, ALU.add)
        p.inc(i, 've')
        dtx_ms = p.mk("dtx", 've')
        p.wait('s', dtx_ms)
        p.s.activation(sh, dt, AF.Abs)
        p.s.activation(sh, sh, AF.Exp, scale=-1.0)
        i = p.s.activation(sh, sh, AF.Ln, bias=1.0)
        p.inc(i, 'se')
        sp_ms = p.mk("sp", 'se')
        p.wait('v', sp_ms)
        p.v.tensor_scalar_max(dt, dt, 0.0)
        i = p.v.tensor_add(dt, dt, sh)
        p.inc(i, 've')
        dt_ms = p.mk("dt", 've')
        tap("d_dt", poolf[0:16, PF_DT:PF_DT + 1024], dt_ms)

        scan_s0 = WSLOT[0]
        p.wait('v', slot_ms[0])
        for idx, oc in enumerate(proj_ocs[1:5], start=1):
            grps = proj_groups(oc, idx, "pbc")
            lag_pump(7)
            if idx + 2 < len(proj_ocs):
                win_tiles[idx + 2] = load_wintile(proj_ocs[idx + 2], idx + 2)
            dsts = {65: SP_B[0], 66: SP_B[1], 67: SP_C[0], 68: SP_C[1]}
            off = dsts[oc]
            for piece, (bank, grp) in enumerate(grps):
                evac_v(bank, grp, lambda v, off=off, piece=piece, bank=bank:
                       v.tensor_copy(wreg[:, scan_s0 + off + piece * 512:
                                          scan_s0 + off + piece * 512 + 512],
                                     pb[bank][:, :]), "bcev")

        xsum_started = [False]
        xcs_last = [None, None]
        xc_sil_last = [None]
        pending_xsum = []

        def emit_xsum(c, sil_ms):
            p.wait('t', sil_ms)
            if not xsum_started[0]:
                p.wait('t', bank_free[4])
                p.wait('t', bank_free[5])
            xcs = inter[:, IT_XCS[c % 2]:IT_XCS[c % 2] + 1024]
            last = None
            for piece in range(2):
                last = p.t.matmul(pb[4 + piece][0:16, :], c_ind[:, c * 16:c * 16 + 16],
                                  xcs[:, piece * 512:piece * 512 + 512],
                                  start=not xsum_started[0], stop=False)
            xsum_started[0] = True
            p.inc(last, 'mm')
            xcs_last[c % 2] = p.mk(f"xsm{c}", 'mm')

        for idx0, oc in enumerate(proj_ocs[5:]):
            idx = idx0 + 5
            c = oc - 32
            if len(pending_xsum) >= 2:
                emit_xsum(*pending_xsum.pop(0))
            grps = proj_groups(oc, idx, "pxi")
            if idx + 2 < len(proj_ocs):
                win_tiles[idx + 2] = load_wintile(proj_ocs[idx + 2], idx + 2)
            xi = inter[:, IT_XI:IT_XI + 1024]
            for piece, (bank, grp) in enumerate(grps):
                evac_v(bank, grp, lambda v, piece=piece, bank=bank:
                       v.tensor_copy(xi[:, piece * 512:piece * 512 + 512],
                                     pb[bank][:, :]), "xiev")
            p.v.tensor_copy(xi_f3[:, c * 3:c * 3 + 3], xi[:, 0:3])
            i = p.v.tensor_copy(halo_o[:, c * 3:c * 3 + 3], xi[:, 1021:1024])
            p.inc(i, 've')
            p.mk(f"xis{c}", 've')
            p.wait('v', xc_sil_last[0])
            xc = poolf[:, PF_XC:PF_XC + 1024]
            p.v.tensor_scalar(xc[:, 3:1024], xi[:, 0:1021],
                              c_wc[:, c * 4:c * 4 + 1], None, ALU.mult)
            for k in (1, 2, 3):
                p.v.scalar_tensor_tensor(xc[:, 3:1024], xi[:, k:1021 + k],
                                         c_wc[:, c * 4 + k:c * 4 + k + 1],
                                         xc[:, 3:1024], ALU.mult, ALU.add)
            i = p.v.memset(xc[:, 0:3], 0.0)
            p.inc(i, 've')
            xc_ms = p.mk(f"xc{c}", 've')
            xcs = inter[:, IT_XCS[c % 2]:IT_XCS[c % 2] + 1024]
            p.wait('s', xc_ms)
            p.wait('s', xcs_last[c % 2])
            i = p.s.activation(xcs, xc, AF.Silu)
            p.inc(i, 'se')
            xc_sil_last[0] = p.mk(f"xsl{c}", 'se')
            pending_xsum.append((c, xc_sil_last[0]))
        while pending_xsum:
            emit_xsum(*pending_xsum.pop(0))

        halo_ready = p.mk("haloo", 've')
        p.wait('y', halo_ready)
        i = p.y.dma_start(cc1_in.ap().rearrange("o (q x) -> (o q) x", q=128),
                          halo_o[:, :])
        p.inc(i, 'sd', 16)
        cc1s_ms = p.mk("cc1s", 'sd')
        p.wait('g', cc1s_ms)
        i = p.g.collective_compute("AllGather", ALU.bypass,
                                   replica_groups=[list(range(8))],
                                   ins=[cc1_in.ap()], outs=[cc1_out.ap()])
        p.inc(i, 'cc')
        cc1_ms = p.mk("cc1", 'cc')
        prev_use = [None, None]
        chain_ms = None
        for j in range(8):
            sl = j % 2
            p.wait('y', cc1_ms if j < 2 else prev_use[sl])
            i = p.y.dma_start(ccr[:, sl * 96:sl * 96 + 96],
                              cc1_out.ap()[j].rearrange("(q x) -> q x", q=128))
            p.inc(i, 'sd', 16)
            row_ms = p.mk(f"ccr{j}", 'sd')
            p.wait('v', row_ms)
            p.wait('v', chain_ms)
            if j == 0:
                i = p.v.tensor_scalar(halo_i[:, :], ccr[:, sl * 96:sl * 96 + 96],
                                      c_sprev[:, j:j + 1], None, ALU.mult)
            else:
                i = p.v.scalar_tensor_tensor(halo_i[:, :], ccr[:, sl * 96:sl * 96 + 96],
                                             c_sprev[:, j:j + 1], halo_i[:, :],
                                             ALU.mult, ALU.add)
            p.inc(i, 've')
            prev_use[sl] = p.mk(f"hacc{j}", 've')
            chain_ms = prev_use[sl]
        p.wait('v', chain_ms)
        comb3 = comb6[:, :].rearrange("q (c k) -> q c k", k=6)
        p.v.tensor_copy(comb3[:, :, 0:3],
                        halo_i[:, :].rearrange("q (c k) -> q c k", k=3))
        i = p.v.tensor_copy(comb3[:, :, 3:6],
                            xi_f3[:, :].rearrange("q (c k) -> q c k", k=3))
        p.inc(i, 've')
        p.wait('v', p.mk("comb", 've'))
        last = None
        for k in (0, 1, 2, 3):
            for c in range(NZC):
                if k == 0:
                    last = p.v.tensor_scalar(xcfixf[:, c * 3:c * 3 + 3],
                                             comb6[:, c * 6:c * 6 + 3],
                                             c_wc[:, c * 4:c * 4 + 1], None, ALU.mult)
                else:
                    last = p.v.scalar_tensor_tensor(xcfixf[:, c * 3:c * 3 + 3],
                                                    comb6[:, c * 6 + k:c * 6 + k + 3],
                                                    c_wc[:, c * 4 + k:c * 4 + k + 1],
                                                    xcfixf[:, c * 3:c * 3 + 3],
                                                    ALU.mult, ALU.add)
        p.inc(last, 've')
        fix_ms = p.mk("fixf", 've')
        p.wait('s', fix_ms)
        i = p.s.activation(xcfixb[:, :], xcfixf[:, :], AF.Silu)
        p.inc(i, 'se')
        fixb_ms = p.mk("fixb", 'se')
        p.wait('t', fixb_ms)
        last = None
        for c in range(NZC):
            last = p.t.matmul(pb[4][0:16, 0:3], c_ind[:, c * 16:c * 16 + 16],
                              xcfixb[:, c * 3:c * 3 + 3],
                              start=False, stop=(c == NZC - 1))
        p.inc(last, 'mm')
        xsum_ms = p.mk("xsum", 'mm')
        p.wait('v', xsum_ms)
        xs = poolf[0:16, PF_SH:PF_SH + 1024]
        p.v.tensor_copy(xs[:, 0:512], pb[4][0:16, :])
        i = p.v.tensor_copy(xs[:, 512:1024], pb[5][0:16, :])
        p.inc(i, 've')
        xs_ms = p.mk("xs", 've')
        tap("d_xs", xs, xs_ms)
        bank_free[4] = xs_ms
        bank_free[5] = xs_ms
        p.wait('v', xs_ms)
        p.wait('v', dt_ms)
        i = p.v.scalar_tensor_tensor(xs, dt, 1.0 / 256.0, xs, ALU.mult, ALU.mult)
        p.inc(i, 've')
        v_ms = p.mk("v", 've')

        p.wait('t', v_ms)
        for tile in range(2):
            for piece in range(2):
                bank = 6 if (tile * 2 + piece) % 2 == 0 else 2
                p.wait('t', bank_free[bank])
                i = p.t.matmul(pb[bank][:, :], c_r[0:16, tile * 128:tile * 128 + 128],
                               xs[:, piece * 512:piece * 512 + 512],
                               start=True, stop=True)
                p.inc(i, 'mm')
                grp = p.mk(f"vr{tile}{piece}", 'mm')
                tmp = inter[:, IT_XCS[0] + piece * 512:IT_XCS[0] + piece * 512 + 512]
                ems = evac_s(bank, grp, lambda s, tmp=tmp, bank=bank:
                             s.activation(tmp, pb[bank][:, :], AF.Copy), f"vre{tile}{piece}")
                p.wait('v', ems)
                boff = scan_s0 + SP_B[tile] + piece * 512
                i = p.v.tensor_mul(wreg[:, boff:boff + 512], wreg[:, boff:boff + 512], tmp)
                p.inc(i, 've')
                p.mk(f"u{tile}{piece}", 've')
        u_ms = p.mk("u", 've')
        p.wait('s', "consts_f")
        i = p.s.activation(c_acol[:, :], c_alog[:, :], AF.Exp)
        p.inc(i, 'se')
        acol_ms = p.mk("acol", 'se')
        p.wait('v', acol_ms)
        i = p.v.tensor_scalar(c_acol[:, :], c_acol[:, :], -1.0, None, ALU.mult)
        p.inc(i, 've')
        p.wait('v', p.mk("acoln", 've'))
        i = p.v.tensor_scalar(xs, dt, c_acol[:, 0:1], None, ALU.mult)
        p.inc(i, 've')
        loga_ms = p.mk("loga", 've')
        i = p.v.reduce_sum(ls16[:, :], xs, AX.X)
        p.inc(i, 've')
        lsum_ms = p.mk("lsum", 've')
        p.wait('s', lsum_ms)
        i = p.s.activation(d16[:, :], ls16[:, :], AF.Exp)
        p.inc(i, 'se')
        d16_ms = p.mk("d16", 'se')
        p.wait('s', slot_ms[0])
        p.wait('t', loga_ms)
        for tile in range(2):
            for piece in range(2):
                bank = 6 if (tile * 2 + piece) % 2 == 0 else 2
                p.wait('t', bank_free[bank])
                i = p.t.matmul(pb[bank][:, :], c_r[0:16, tile * 128:tile * 128 + 128],
                               xs[:, piece * 512:piece * 512 + 512],
                               start=True, stop=True)
                p.inc(i, 'mm')
                grp = p.mk(f"lr{tile}{piece}", 'mm')
                daoff = scan_s0 + SP_DA[tile] + piece * 512
                evac_s(bank, grp, lambda s, daoff=daoff, bank=bank:
                       s.activation(wreg[:, daoff:daoff + 512], pb[bank][:, :], AF.Exp),
                       f"dae{tile}{piece}")
        da_ms = p.mk("da", 'se')

        p.wait('v', da_ms)
        p.wait('v', u_ms)
        for tile in range(2):
            i = p.v.tensor_tensor_scan(
                wreg[:, scan_s0 + SP_H[tile]:scan_s0 + SP_H[tile] + 1024],
                wreg[:, scan_s0 + SP_DA[tile]:scan_s0 + SP_DA[tile] + 1024],
                wreg[:, scan_s0 + SP_B[tile]:scan_s0 + SP_B[tile] + 1024],
                0.0, ALU.mult, ALU.add)
            p.inc(i, 've')
        for tile in range(2):
            i = p.v.tensor_copy(fcol[:, tile:tile + 1],
                                wreg[:, scan_s0 + SP_H[tile] + 1023:
                                     scan_s0 + SP_H[tile] + 1024])
            p.inc(i, 've')
        f_ms = p.mk("f", 've')
        tap("d_scan", wreg[:, 0:8192], f_ms)
        p.wait('y', f_ms)
        p.wait('y', d16_ms)
        i = p.y.dma_start(cc2_in.ap()[0:1, 0:128].rearrange("o q -> q o"), fcol[:, 0:1])
        p.inc(i, 'sd', 16)
        i = p.y.dma_start(cc2_in.ap()[0:1, 128:256].rearrange("o q -> q o"), fcol[:, 1:2])
        p.inc(i, 'sd', 16)
        i = p.y.dma_start(cc2_in.ap()[0:1, 256:272].rearrange("o q -> q o"), d16[:, :])
        p.inc(i, 'sd', 16)
        cc2s_ms = p.mk("cc2s", 'sd')
        p.wait('g', cc2s_ms)
        i = p.g.collective_compute("AllGather", ALU.bypass,
                                   replica_groups=[list(range(8))],
                                   ins=[cc2_in.ap()], outs=[cc2_out.ap()])
        p.inc(i, 'cc')
        cc2_ms = p.mk("cc2", 'cc')
        p.wait('y', cc2_ms)
        for j in range(8):
            i = p.y.dma_start(fjall[:, 2 * j:2 * j + 1],
                              cc2_out.ap()[j:j + 1, 0:128].rearrange("o q -> q o"))
            p.inc(i, 'sd', 16)
            i = p.y.dma_start(fjall[:, 2 * j + 1:2 * j + 2],
                              cc2_out.ap()[j:j + 1, 128:256].rearrange("o q -> q o"))
            p.inc(i, 'sd', 16)
            i = p.y.dma_start(djall[:, j:j + 1],
                              cc2_out.ap()[j:j + 1, 256:272].rearrange("o q -> q o"))
            p.inc(i, 'sd', 16)
        gath_ms = p.mk("gath", 'sd')
        p.wait('v', gath_ms)
        i = p.v.memset(hcol[:, :], 0.0)
        p.inc(i, 've')
        p.mk("hz", 've')
        for j in range(8):
            p.wait('t', gath_ms)
            for tile in range(2):
                bank = 6 if tile == 0 else 2
                p.wait('t', bank_free[bank])
                i = p.t.matmul(pb[bank][:, 0:1], c_r[0:16, tile * 128:tile * 128 + 128],
                               djall[:, j:j + 1], start=True, stop=True)
                p.inc(i, 'mm')
                grp = p.mk(f"dr{j}{tile}", 'mm')
                evac_v(bank, grp, lambda v, tile=tile, bank=bank:
                       v.tensor_copy(djrep[:, tile:tile + 1], pb[bank][:, 0:1]),
                       f"dre{j}{tile}")
            p.wait('v', p.mk(f"drr{j}", 've'))
            for tile in range(2):
                p.v.scalar_tensor_tensor(djeff[:, tile:tile + 1], djrep[:, tile:tile + 1],
                                         c_sel[:, j:j + 1], c_isel[:, j:j + 1],
                                         ALU.mult, ALU.add)
                i = p.v.tensor_scalar(fjeff[:, tile:tile + 1], fjall[:, 2 * j + tile:2 * j + tile + 1],
                                      c_sel[:, j:j + 1], None, ALU.mult)
            p.inc(i, 've')
            p.wait('v', p.mk(f"cmb{j}", 've'))
            for tile in range(2):
                i = p.v.scalar_tensor_tensor(hcol[:, tile:tile + 1], hcol[:, tile:tile + 1],
                                             djeff[:, tile:tile + 1], fjeff[:, tile:tile + 1],
                                             ALU.mult, ALU.add)
            p.inc(i, 've')
            p.wait('v', p.mk(f"cmh{j}", 've'))
        tap("d_hc", hcol[:, :], p.mk("hcfin", 've'))
        i = p.v.tensor_copy(hb[:, :], hcol[:, :])
        p.inc(i, 've')
        p.wait('v', p.mk("hb", 've'))
        for tile in range(2):
            i = p.v.tensor_tensor_scan(
                wreg[:, scan_s0 + SP_H[tile]:scan_s0 + SP_H[tile] + 1024],
                wreg[:, scan_s0 + SP_DA[tile]:scan_s0 + SP_DA[tile] + 1024],
                wreg[:, scan_s0 + SP_B[tile]:scan_s0 + SP_B[tile] + 1024],
                hb[:, tile:tile + 1], ALU.mult, ALU.add)
            p.inc(i, 've')
        for tile in range(2):
            i = p.v.tensor_mul(wreg[:, scan_s0 + SP_C[tile]:scan_s0 + SP_C[tile] + 1024],
                               wreg[:, scan_s0 + SP_C[tile]:scan_s0 + SP_C[tile] + 1024],
                               wreg[:, scan_s0 + SP_H[tile]:scan_s0 + SP_H[tile] + 1024])
            p.inc(i, 've')
        ch_ms = p.mk("ch", 've')
        p.wait('t', ch_ms)
        ylast = None
        for piece in range(2):
            bank = 4 + piece
            p.wait('t', bank_free[bank])
            for tile in range(2):
                i = p.t.matmul(pb[bank][0:16, :], c_ind2[:, tile * 16:tile * 16 + 16],
                               wreg[:, scan_s0 + SP_C[tile] + piece * 512:
                                    scan_s0 + SP_C[tile] + piece * 512 + 512],
                               start=(tile == 0), stop=(tile == 1))
            p.inc(i, 'mm')
            ylast = p.mk(f"y{piece}", 'mm')
        p.wait('v', ylast)
        p.v.tensor_copy(yb16[:, 0:512], pb[4][0:16, :])
        i = p.v.tensor_copy(yb16[:, 512:1024], pb[5][0:16, :])
        p.inc(i, 've')
        yb_ms = p.mk("yb", 've')
        tap("d_yb", yb16[:, :], yb_ms)
        bank_free[4] = yb_ms
        bank_free[5] = yb_ms
        slot_ms[0] = ylast
        load_weight(48)

        wh2i_last = [None, None]
        yproj_last = [None, None]
        zsil_last = [None, None]
        group_add_ms = {}
        zidx = 0
        for gidx in range(8):
            slot = 1 + (gidx % 2)
            base = WSLOT[slot]
            dep = slot_ms[slot] if gidx < 2 else group_add_ms[gidx - 2]
            p.wait('g', dep)
            for q in range(4):
                ic = gidx * 4 + q
                i = p.g.dma_start(wreg[:, base + q * 2048:base + q * 2048 + 2048],
                                  wott_ap[ic])
                p.inc(i, 'wd', 16)
            wout_ms = p.mk(f"woutg{gidx}", 'wd')
            wsl = gidx % 2
            p.wait('g', wh2i_last[wsl])
            i = p.g.dma_start(inter[0:16, IT_WH2I[wsl]:IT_WH2I[wsl] + 512],
                              wh2it.ap()[0:16, gidx * 512:gidx * 512 + 512])
            p.inc(i, 'wd', 16)
            wh2i_ms = p.mk(f"wh2i{gidx}", 'wd')
            for q in range(4):
                zc = gidx * 4 + q
                widx = 37 + zidx
                win_tiles[widx] = load_wintile(zc, widx)
                zidx += 1
                grps = proj_groups(zc, widx, f"pz{zc}")
                zsl = zc % 2
                zbuf = inter[:, IT_XCS[zsl]:IT_XCS[zsl] + 1024]
                for piece, (bank, grp) in enumerate(grps):
                    p.wait('s', zsil_last[zsl])
                    evac_s(bank, grp, lambda s, zbuf=zbuf, piece=piece, bank=bank:
                           s.activation(zbuf[:, piece * 512:piece * 512 + 512],
                                        pb[bank][:, :], AF.Silu), f"zsil{zc}")
                zsil_ms = p.mk(f"zs{zc}", 'se')
                p.wait('t', wh2i_ms)
                p.wait('t', yb_ms)
                for piece in range(2):
                    bank = 4 + piece
                    p.wait('t', bank_free[bank])
                    i = p.t.matmul(pb[bank][:, :],
                                   inter[0:16, IT_WH2I[wsl] + q * 128:IT_WH2I[wsl] + q * 128 + 128],
                                   yb16[:, piece * 512:piece * 512 + 512],
                                   start=True, stop=True)
                    p.inc(i, 'mm')
                    grp = p.mk(f"yp{zc}{piece}", 'mm')
                    ypt = inter[:, IT_XI + piece * 512:IT_XI + piece * 512 + 512]
                    p.wait('s', yproj_last[piece])
                    ems = evac_s(bank, grp, lambda s, ypt=ypt, bank=bank:
                                 s.activation(ypt, pb[bank][:, :], AF.Copy),
                                 f"ype{zc}{piece}")
                    p.wait('v', ems)
                    p.wait('v', zsil_ms)
                    i = p.v.tensor_mul(ygr[:, q * 1024 + piece * 512:q * 1024 + piece * 512 + 512],
                                       zbuf[:, piece * 512:piece * 512 + 512], ypt)
                    p.inc(i, 've')
                    yproj_last[piece] = p.mk(f"ygm{zc}{piece}", 've')
                zsil_last[zsl] = yproj_last[1]
            yg_ms = yproj_last[1]
            p.wait('t', wout_ms)
            p.wait('t', yg_ms)
            last_add = None
            for piece in range(2):
                for oc in range(16):
                    bank = 2 + state["woping"][0]
                    state["woping"][0] ^= 1
                    mms = [(pb[bank][:, :],
                            wreg[:, base + q * 2048 + oc * 128:base + q * 2048 + oc * 128 + 128],
                            ygr[:, q * 1024 + piece * 512:q * 1024 + piece * 512 + 512])
                           for q in range(4)]
                    grp = mm_group(bank, mms, f"zwo{gidx}")
                    sl = slice(oc * T + piece * 512, oc * T + piece * 512 + 512)
                    last_add = evac_v(bank, grp, lambda v, sl=sl, bank=bank:
                                      v.tensor_add(resid[:, sl], resid[:, sl],
                                                   pb[bank][:, :]),
                                      f"zra{gidx}")
            group_add_ms[gidx] = last_add
            if gidx == 6:
                slot_ms[1] = last_add
                load_weight(49)
            if gidx == 7:
                slot_ms[2] = last_add
                load_weight(50)
        residB_ms = group_add_ms[7]
        tap("d_r2", resid[:, :], residB_ms)

        xnfree = p.mk("xnfree1", 'mm')
        resid_final = emit_ffn(1, residB_ms, 've', xnfree)

        p.wait('y', resid_final)
        i = p.y.dma_start(oxt.ap().rearrange("(c q) t -> q c t", q=128),
                          resid[:, :].rearrange("q (c t) -> q c t", t=T))
        p.inc(i, 'sd', 16)
        if p.emit == 'y':
            p.eng.wait_ge(p.sems['sd'], p.cnt['sd'])

    with nc.Block() as block:
        @block.sync
        def _(sync):
            program(Prog('y', sync, sems))

        @block.gpsimd
        def _(gp):
            program(Prog('g', gp, sems))

        @block.tensor
        def _(tn):
            program(Prog('t', tn, sems))

        @block.vector
        def _(vc):
            program(Prog('v', vc, sems))

        @block.scalar
        def _(sc):
            program(Prog('s', sc, sems))

    return nc


_NC_CACHE = {}


def _get_nc():
    if "nc" not in _NC_CACHE:
        _NC_CACHE["nc"] = build_program()
    return _NC_CACHE["nc"]


def kernel(L0_attn_x, L0_attn_w_norm, L0_attn_W_in, L0_attn_w_conv, L0_attn_dt_bias,
           L0_attn_A_log, L0_attn_W_h2i, L0_attn_W_out, L0_ffn_w_gate, L0_ffn_wi_0,
           L0_ffn_wi_1, L0_ffn_wo):
    f32 = np.float32
    x = np.asarray(L0_attn_x, f32)
    W_in = np.asarray(L0_attn_W_in, f32)
    w_conv = np.asarray(L0_attn_w_conv, f32)
    W_h2i = np.asarray(L0_attn_W_h2i, f32)
    W_out = np.asarray(L0_attn_W_out, f32)
    w_gate = np.asarray(L0_ffn_w_gate, f32)
    wi_0 = np.asarray(L0_ffn_wi_0, f32)
    wi_1 = np.asarray(L0_ffn_wi_1, f32)
    wo = np.asarray(L0_ffn_wo, f32)

    wi0h = wi_0.reshape(8, 16, 128, 2, 512).transpose(0, 3, 2, 1, 4)
    wi1h = wi_1.reshape(8, 16, 128, 2, 512).transpose(0, 3, 2, 1, 4)
    wih = np.ascontiguousarray(
        np.stack([wi0h[:, 0], wi0h[:, 1], wi1h[:, 0], wi1h[:, 1]], axis=1)
    ).reshape(32, 128, 16 * 512)
    woh = np.ascontiguousarray(
        wo.reshape(8, 8, 128, 2, 1024).transpose(0, 3, 2, 1, 4)
    ).reshape(16, 128, 8 * 1024)
    Wp = np.zeros((8832, 2048), f32)
    Wp[0:8192] = W_in[0:8192]
    Wp[8192:8208] = W_in[8192:8208]
    Wp[8320:8576] = W_in[8208:8464]
    Wp[8576:8832] = W_in[8464:8720]
    wint = np.ascontiguousarray(
        Wp.T.reshape(16, 128, 69, 128).transpose(2, 1, 0, 3)
    ).reshape(69, 128, 16 * 128)
    wott = np.ascontiguousarray(W_out.T).reshape(32, 128, 2048)
    wh2it = np.ascontiguousarray(W_h2i.T)
    wct = np.ascontiguousarray(
        w_conv.reshape(4096, 4).reshape(32, 128, 4).transpose(1, 0, 2)
    ).reshape(128, 128)
    wnt = np.ascontiguousarray(np.asarray(L0_attn_w_norm, f32).reshape(16, 128).T)
    wgt = np.ascontiguousarray(
        w_gate.reshape(16, 128, 8).transpose(1, 0, 2)
    ).reshape(128, 128)
    dtbv = np.asarray(L0_attn_dt_bias, f32).reshape(16, 1)
    alogv = np.asarray(L0_attn_A_log, f32).reshape(16, 1)
    f = np.arange(4096)
    indt = np.ascontiguousarray(
        (f[:, None] // 256 == np.arange(16)[None, :]).astype(f32)
        .reshape(32, 128, 16).transpose(1, 0, 2)
    ).reshape(128, 512)
    pidx = np.arange(128)
    ind2 = np.zeros((128, 32), f32)
    ind2[:, 0:16] = (pidx[:, None] // 16 == np.arange(16)[None, :])
    ind2[:, 16:32] = (pidx[:, None] // 16 + 8 == np.arange(16)[None, :])
    rrep = np.zeros((16, 256), f32)
    for t in range(2):
        rrep[:, t * 128:(t + 1) * 128] = (
            np.arange(16)[:, None] == (t * 8 + pidx[None, :] // 16))
    ones = np.ones((1, 128), f32)
    idm = np.eye(128, dtype=f32)

    shared = dict(wg=wgt, wih=wih, woh=woh, wint=wint, wott=wott, wh2it=wh2it,
                  wct=wct, wnt=wnt, dtb=dtbv, alog=alogv, indt=indt, ind2=ind2,
                  rrep=rrep, ones_f=ones, ones_b=ones, ones_c=np.ones((128, 1), f32),
                  epsr=np.array([[1e-5, -0.5, 1.0 + 1.1920929e-07,
                                  1e-5 * 1.1920929e-07, 0, 0, 0, 0]], f32),
                  id_f=idm, id_b=idm)

    in_maps = []
    for k in range(8):
        beta, pos = k // 4, k % 4
        xk = np.ascontiguousarray(x[beta, pos * T:(pos + 1) * T, :].T)
        selm = np.zeros((128, 8), f32)
        selprev = np.zeros((128, 8), f32)
        for j in range(8):
            if j // 4 == beta and j % 4 < pos:
                selm[:, j] = 1.0
        if pos > 0:
            selprev[:, k - 1] = 1.0
        m = dict(shared)
        m.update(xin=xk, selm=selm, invselm=np.ascontiguousarray(1.0 - selm),
                 selprev=selprev)
        in_maps.append(m)

    nc = _get_nc()
    res = run_bass_kernel_spmd(nc, in_maps, core_ids=list(range(8)))
    out = np.empty((2, 4096, 2048), f32)
    for k in range(8):
        beta, pos = k // 4, k % 4
        out[beta, pos * T:(pos + 1) * T, :] = res.results[k]["oxt"].T
    return out


# version 06
# speedup vs baseline: 1.0579x; 1.0179x over previous
import contextlib

import numpy as np
import concourse.bass as bass
import concourse.mybir as mybir
from concourse.bass_utils import run_bass_kernel_spmd

F32 = mybir.dt.float32
BF16 = mybir.dt.bfloat16
AF = mybir.ActivationFunctionType
ALU = mybir.AluOpType
AX = mybir.AxisListType

T = 1024
D = 2048
NKC = 16
NE = 8
H = 16
NZC = 32
EPS_LN = 1e-5
EPS_RMS = 1.1920929e-07

WSLOT = (0, 8192, 16384)


class _Dummy:
    def __getattr__(self, name):
        def f(*a, **k):
            return None
        return f


_DUM = _Dummy()


class Prog:

    def __init__(self, emit, eng, sems):
        self.emit = emit
        self.eng = eng
        self.sems = sems
        self.cnt = {k: 0 for k in sems}
        self.ms = {}
        self.uid = 0

    def E(self, name):
        return self.eng if self.emit == name else _DUM

    @property
    def g(self):
        return self.E('g')

    @property
    def v(self):
        return self.E('v')

    @property
    def s(self):
        return self.E('s')

    @property
    def t(self):
        return self.E('t')

    @property
    def y(self):
        return self.E('y')

    def inc(self, inst, sem, k=1):
        if inst is not None:
            inst.then_inc(self.sems[sem], k)
        self.cnt[sem] += k

    def mark(self, name, sem):
        assert name not in self.ms, name
        self.ms[name] = (sem, self.cnt[sem])
        return name

    def mkid(self, tag):
        self.uid += 1
        return f"{tag}_{self.uid}"

    def mk(self, tag, sem):
        return self.mark(self.mkid(tag), sem)

    def wait(self, eng_name, msname):
        if msname is None:
            return
        sem, val = self.ms[msname]
        if self.emit == eng_name and val > 0:
            self.eng.wait_ge(self.sems[sem], val)


def build_program(debug=False):
    nc = bass.Bass("TRN2", target_bir_lowering=False, debug=False)

    def inp(name, shape):
        return nc.dram_tensor(name, list(shape), F32, kind="ExternalInput")

    xin = inp("xin", (D, T))
    wg = inp("wg", (128, 128))
    wih = inp("wih", (NE * 4, 128, 16 * 512))
    woh = inp("woh", (NE * 2, 128, 8 * 1024))
    wint = inp("wint", (69, 128, 16 * 128))
    wott = inp("wott", (32, 128, 2048))
    wh2it = inp("wh2it", (16, 4096))
    wct = inp("wct", (128, 128))
    wnt = inp("wnt", (128, 16))
    dtb = inp("dtb", (16, 1))
    alog = inp("alog", (16, 1))
    indt = inp("indt", (128, 512))
    ind2 = inp("ind2", (128, 32))
    rrep = inp("rrep", (16, 256))
    ones_f = inp("ones_f", (1, 128))
    ones_b = inp("ones_b", (1, 128))
    ones_c = inp("ones_c", (128, 1))
    epsr = inp("epsr", (1, 8))
    id_f = inp("id_f", (128, 128))
    id_b = inp("id_b", (128, 128))
    selm = inp("selm", (128, 8))
    invselm = inp("invselm", (128, 8))
    selprev = inp("selprev", (128, 8))

    oxt = nc.dram_tensor("oxt", [D, T], F32, kind="ExternalOutput")

    dbg = {}
    if debug:
        dbg["d_xn"] = nc.dram_tensor("d_xn", [128, 16 * T], BF16, kind="ExternalOutput")
        dbg["d_g8"] = nc.dram_tensor("d_g8", [8, 1024], BF16, kind="ExternalOutput")
        dbg["d_r1"] = nc.dram_tensor("d_r1", [128, 16 * T], F32, kind="ExternalOutput")
        dbg["d_dt"] = nc.dram_tensor("d_dt", [16, T], F32, kind="ExternalOutput")
        dbg["d_xs"] = nc.dram_tensor("d_xs", [16, T], F32, kind="ExternalOutput")
        dbg["d_scan"] = nc.dram_tensor("d_scan", [128, 8192], BF16, kind="ExternalOutput")
        dbg["d_yb"] = nc.dram_tensor("d_yb", [16, T], BF16, kind="ExternalOutput")
        dbg["d_r2"] = nc.dram_tensor("d_r2", [128, 16 * T], F32, kind="ExternalOutput")
        dbg["d_hc"] = nc.dram_tensor("d_hc", [128, 2], F32, kind="ExternalOutput")
        dbg["d_xn2"] = nc.dram_tensor("d_xn2", [128, 16 * T], BF16, kind="ExternalOutput")
        dbg["d_gw"] = nc.dram_tensor("d_gw", [128, 64], F32, kind="ExternalOutput")
        dbg["d_gc"] = nc.dram_tensor("d_gc", [128, 56], F32, kind="ExternalOutput")
        dbg["d_msk"] = nc.dram_tensor("d_msk", [128, 64], F32, kind="ExternalOutput")
        dbg["d_exp"] = nc.dram_tensor("d_exp", [128, 64], F32, kind="ExternalOutput")
        dbg["d_lgf"] = nc.dram_tensor("d_lgf", [8, 1024], F32, kind="ExternalOutput")
        dbg["d_int"] = nc.dram_tensor("d_int", [128, 8192], BF16, kind="ExternalOutput")
    cc1_in = nc.dram_tensor("cc1_in", [1, 32 * 128 * 3], BF16)
    cc1_out = nc.dram_tensor("cc1_out", [8, 32 * 128 * 3], BF16)
    cc2_in = nc.dram_tensor("cc2_in", [1, 272], F32)
    cc2_out = nc.dram_tensor("cc2_out", [8, 272], F32)

    es = contextlib.ExitStack()
    sbuf = lambda name, shape, dt: es.enter_context(nc.sbuf_tensor(name, list(shape), dt))
    psum = lambda name, shape, dt: es.enter_context(nc.psum_tensor(name, list(shape), dt))

    resid = sbuf("resid", (128, 16 * T), F32)
    xnr = sbuf("xnr", (128, 16 * T), BF16)
    wreg = sbuf("wreg", (128, 3 * 8192), BF16)
    inter = sbuf("inter", (128, 8192), BF16)
    ygr = sbuf("ygr", (128, 4096), BF16)
    poolf = sbuf("poolf", (128, 4096), F32)
    c_wg = sbuf("c_wg", (128, 128), BF16)
    c_wc = sbuf("c_wc", (128, 128), F32)
    c_wn = sbuf("c_wn", (128, 16), F32)
    c_dtb = sbuf("c_dtb", (16, 1), F32)
    c_alog = sbuf("c_alog", (16, 1), F32)
    c_acol = sbuf("c_acol", (16, 1), F32)
    c_ind = sbuf("c_ind", (128, 512), BF16)
    c_ind2 = sbuf("c_ind2", (128, 32), BF16)
    c_r = sbuf("c_r", (16, 256), F32)
    c_1f = sbuf("c_1f", (1, 128), F32)
    c_1b = sbuf("c_1b", (1, 128), BF16)
    c_1bc = sbuf("c_1bc", (128, 1), BF16)
    c_eps = sbuf("c_eps", (1, 8), F32)
    c_idf = sbuf("c_idf", (128, 128), F32)
    c_idb = sbuf("c_idb", (128, 128), BF16)
    c_sel = sbuf("c_sel", (128, 8), F32)
    c_isel = sbuf("c_isel", (128, 8), F32)
    c_sprev = sbuf("c_sprev", (128, 8), F32)
    gw = sbuf("gw", (128, 64), F32)
    ge1 = sbuf("ge1", (128, 64), F32)
    glm = sbuf("glm", (128, 64), F32)
    gsub = sbuf("gsub", (128, 64), F32)
    gmsk = sbuf("gmsk", (128, 64), F32)
    gexp = sbuf("gexp", (128, 64), F32)
    gpp = sbuf("gpp", (128, 64), F32)
    gcols = sbuf("gcols", (128, 56), F32)
    gwb = sbuf("gwb", (128, 64), BF16)
    grow = sbuf("grow", (1, 1024), BF16)
    yb16 = sbuf("yb16", (16, T), BF16)
    halo_o = sbuf("halo_o", (128, 96), BF16)
    halo_i = sbuf("halo_i", (128, 96), BF16)
    xi_f3 = sbuf("xi_f3", (128, 96), BF16)
    comb6 = sbuf("comb6", (128, 192), BF16)
    xcfixf = sbuf("xcfixf", (128, 96), F32)
    xcfixb = sbuf("xcfixb", (128, 96), BF16)
    ccr = sbuf("ccr", (128, 192), BF16)
    fcol = sbuf("fcol", (128, 2), F32)
    hcol = sbuf("hcol", (128, 2), F32)
    hb = sbuf("hb", (128, 2), BF16)
    fjall = sbuf("fjall", (128, 16), F32)
    djall = sbuf("djall", (16, 8), F32)
    djrep = sbuf("djrep", (128, 2), F32)
    djeff = sbuf("djeff", (128, 2), F32)
    fjeff = sbuf("fjeff", (128, 2), F32)
    d16 = sbuf("d16", (16, 1), F32)
    ls16 = sbuf("ls16", (16, 1), F32)

    pb = [psum(f"pb{i}", (128, 512), F32) for i in range(7)]
    pbf = psum("pbf", (128, 1024), BF16)

    sems = {k: es.enter_context(nc.semaphore(k)) for k in
            ("wd", "sd", "mm", "ve", "se", "cc")}

    IT_XI = 0
    IT_WIN = (1024, 3072)
    IT_WH2I = (5120, 5632)
    IT_XCS = (6144, 7168)
    YG_GSB = (0, 1024)
    YG_G8 = 2048
    YG_H1T = 3072
    PF_XC = 0
    PF_DT = 1024
    PF_SH = 2048
    PF_LGF = 3072
    SP_B = (0, 1024)
    SP_C = (2048, 3072)
    SP_H = (4096, 5120)
    SP_DA = (6144, 7168)

    R_MEAN = poolf[0:1, 0:1024]
    R_VAR = poolf[0:1, 1024:2048]
    R_SCR = poolf[0:1, 2048:3072]
    R_A = poolf[0:1, 3072:4096]
    R_RMS = poolf[0:1, 2048:3072]

    wih_ap = wih.ap()
    woh_ap = woh.ap()
    wint_ap = wint.ap()
    wott_ap = wott.ap()

    def program(p: Prog):
        def tap(name, src_ap, ms, eng='v'):
            if not debug:
                return
            p.wait('y', ms)
            i = p.y.dma_start(dbg[name].ap(), src_ap)
            p.inc(i, 'sd', 16)
        bank_free = {i: None for i in range(8)}
        slot_ms = {0: None, 1: None, 2: None}
        wload = {}

        def mm_group(bank, mms, tag):
            p.wait('t', bank_free[bank])
            n = len(mms)
            last = None
            for i, (o, l, r) in enumerate(mms):
                last = p.t.matmul(o, l, r, start=(i == 0), stop=(i == n - 1))
            p.inc(last, 'mm')
            return p.mk(tag, 'mm')

        def evac_v(bank, grp, fn, tag):
            p.wait('v', grp)
            inst = fn(p.v)
            p.inc(inst, 've')
            name = p.mk(tag, 've')
            bank_free[bank] = name
            return name

        def evac_s(bank, grp, fn, tag):
            p.wait('s', grp)
            inst = fn(p.s)
            p.inc(inst, 'se')
            name = p.mk(tag, 'se')
            bank_free[bank] = name
            return name

        i = p.y.dma_start(resid[:, :].rearrange("q (c t) -> q c t", t=T),
                          xin.ap().rearrange("(c q) t -> q c t", q=128))
        p.inc(i, 'sd', 16)
        p.mark("x_loaded", 'sd')
        for dst, src in ((c_wc, wct), (c_wn, wnt), (c_dtb, dtb), (c_alog, alog),
                         (c_r, rrep), (c_1f, ones_f), (c_idf, id_f), (c_sel, selm),
                         (c_isel, invselm), (c_sprev, selprev), (c_eps, epsr)):
            i = p.y.dma_start(dst[:, :], src[:, :])
            p.inc(i, 'sd', 16)
        p.mark("consts_f", 'sd')
        for dst, src in ((c_wg, wg), (c_ind, indt), (c_ind2, ind2),
                         (c_1b, ones_b), (c_1bc, ones_c), (c_idb, id_b)):
            i = p.g.dma_start(dst[:, :], src[:, :])
            p.inc(i, 'wd', 16)
        p.mark("consts_b", 'wd')

        def load_weight(w):
            slot = w % 3
            e = (w % 48) // 6
            j = w % 6
            p.wait('g', slot_ms[slot])
            if j < 4:
                src = wih_ap[e * 4 + j]
            else:
                src = woh_ap[e * 2 + (j - 4)]
            i = p.g.dma_start(wreg[:, WSLOT[slot]:WSLOT[slot] + 8192], src)
            p.inc(i, 'wd', 16)
            wload[w] = (p.mk(f"wl{w}", 'wd'), slot)

        for w in (0, 1, 2):
            load_weight(w)

        def emit_ln(tag, with_wn, resid_ms, resid_sem_eng, xn_free_ms):
            XB = (5120, 6144)
            XQ = (7168, 0)
            p.wait('s', resid_ms)
            chunk_mm = [None, None]
            for c in range(NKC):
                xb = inter[:, XB[c % 2]:XB[c % 2] + 1024]
                xq = inter[:, XQ[c % 2]:XQ[c % 2] + 1024]
                p.wait('s', chunk_mm[c % 2])
                i = p.s.activation(xb, resid[:, c * T:(c + 1) * T], AF.Copy)
                p.inc(i, 'se')
                cast_ms = p.mk(f"{tag}cb", 'se')
                p.wait('v', cast_ms)
                i = p.v.tensor_mul(xq, xb, xb)
                p.inc(i, 've')
                sq_ms = p.mk(f"{tag}sq", 've')
                p.wait('t', sq_ms)
                if c == 0:
                    for b in range(4):
                        p.wait('t', bank_free[b])
                last = None
                for piece in range(2):
                    last = p.t.matmul(pb[piece][0:1, :], c_1bc[:, 0:1],
                                      xb[:, piece * 512:piece * 512 + 512],
                                      start=(c == 0), stop=(c == NKC - 1))
                    last = p.t.matmul(pb[2 + piece][0:1, :], c_1bc[:, 0:1],
                                      xq[:, piece * 512:piece * 512 + 512],
                                      start=(c == 0), stop=(c == NKC - 1))
                p.inc(last, 'mm')
                chunk_mm[c % 2] = p.mk(f"{tag}cm", 'mm')
            stat_grp = chunk_mm[(NKC - 1) % 2]
            p.wait('v', stat_grp)
            p.v.tensor_copy(R_MEAN[:, 0:512], pb[0][0:1, :])
            p.v.tensor_copy(R_MEAN[:, 512:1024], pb[1][0:1, :])
            p.v.tensor_copy(R_VAR[:, 0:512], pb[2][0:1, :])
            i = p.v.tensor_copy(R_VAR[:, 512:1024], pb[3][0:1, :])
            p.inc(i, 've')
            ev = p.mk(f"{tag}sev", 've')
            for b in range(4):
                bank_free[b] = ev
            p.wait('v', ev)
            p.v.tensor_scalar(R_MEAN, R_MEAN, 1.0 / D, None, ALU.mult)
            p.v.tensor_mul(R_SCR, R_MEAN, R_MEAN)
            p.v.tensor_scalar(R_VAR, R_VAR, 1.0 / D, None, ALU.mult)
            i = p.v.tensor_sub(R_VAR, R_VAR, R_SCR)
            p.inc(i, 've')
            var_ms = p.mk(f"{tag}var", 've')
            p.wait('s', var_ms)
            if with_wn:
                p.s.activation(R_A, R_VAR, AF.Ln,
                               scale=c_eps[0:1, 2:3], bias=c_eps[0:1, 3:4])
                p.s.activation(R_A, R_A, AF.Exp, scale=c_eps[0:1, 1:2])
                p.s.activation(R_SCR, R_VAR, AF.Ln, bias=c_eps[0:1, 0:1])
                i = p.s.activation(R_SCR, R_SCR, AF.Exp, scale=c_eps[0:1, 1:2])
            else:
                p.s.activation(R_A, R_VAR, AF.Ln, bias=c_eps[0:1, 0:1])
                i = p.s.activation(R_A, R_A, AF.Exp, scale=c_eps[0:1, 1:2])
            p.inc(i, 'se')
            rstd_ms = p.mk(f"{tag}rstd", 'se')
            p.wait('v', rstd_ms)
            if with_wn:
                p.v.tensor_mul(R_VAR, R_MEAN, R_A)
                p.v.tensor_scalar(R_VAR, R_VAR, -1.0, None, ALU.mult)
                p.v.tensor_mul(R_MEAN, R_MEAN, R_SCR)
                i = p.v.tensor_scalar(R_MEAN, R_MEAN, -1.0, None, ALU.mult)
            else:
                p.v.tensor_mul(R_VAR, R_MEAN, R_A)
                i = p.v.tensor_scalar(R_VAR, R_VAR, -1.0, None, ALU.mult)
            p.inc(i, 've')
            ab_ms = p.mk(f"{tag}ab", 've')
            p.wait('t', ab_ms)
            for piece in range(2):
                for bi0, row in ((4, R_A), (0, R_VAR)):
                    bi = bi0 + piece
                    p.wait('t', bank_free[bi])
                    i = p.t.matmul(pb[bi][:, :], c_1f[0:1, :],
                                   row[:, piece * 512:piece * 512 + 512],
                                   start=True, stop=True)
                    p.inc(i, 'mm')
            if with_wn:
                for piece in range(2):
                    p.wait('t', bank_free[2 + piece])
                    i = p.t.matmul(pb[2 + piece][:, :], c_1f[0:1, :],
                                   R_SCR[:, piece * 512:piece * 512 + 512],
                                   start=True, stop=True)
                    p.inc(i, 'mm')
                p.wait('t', bank_free[6])
                i = p.t.matmul(pb[6][:, :], c_1f[0:1, :], R_MEAN[:, 0:512],
                               start=True, stop=True)
                p.inc(i, 'mm')
            bc_ms = p.mk(f"{tag}bc", 'mm')
            p.wait('v', bc_ms)
            p.wait('v', xn_free_ms)
            scr = inter[:, 5120:5632]
            xn_p = [None, None]
            for piece in range(2):
                last = None
                for c in range(NKC):
                    sl = slice(c * T + piece * 512, c * T + piece * 512 + 512)
                    p.v.tensor_mul(scr, resid[:, sl], pb[4 + piece][:, :])
                    if with_wn:
                        p.v.tensor_add(scr, scr, pb[piece][:, :])
                        last = p.v.tensor_scalar(xnr[:, sl], scr,
                                                 c_wn[:, c:c + 1], None, ALU.mult)
                    else:
                        last = p.v.tensor_add(xnr[:, sl], scr, pb[piece][:, :])
                p.inc(last, 've')
                xn_p[piece] = p.mk(f"{tag}xnp{piece}", 've')
            xn_ms = xn_p[1]
            for b in (0, 1, 4, 5):
                bank_free[b] = xn_ms
            if tag == "ln0":
                tap("d_xn", xnr[:, :], xn_ms)
            if tag == "ln1m":
                tap("d_xn2", xnr[:, :], xn_ms)
            if not with_wn:
                return xn_ms, xn_p, None

            lag = {"items": [(piece, c) for piece in range(2) for c in range(NKC)],
                   "idx": 0, "scr2": inter[:, 5632:6144], "last": None}

            def lag_pump(n):
                while n > 0 and lag["idx"] < 32:
                    piece, c = lag["items"][lag["idx"]]
                    if lag["idx"] == 16:
                        pm = p.mk(f"{tag}lagp0", 've')
                        p.wait('t', pm)
                        i = p.t.matmul(pb[6][:, :], c_1f[0:1, :],
                                       R_MEAN[:, 512:1024], start=True, stop=True)
                        p.inc(i, 'mm')
                        rb = p.mk(f"{tag}lagrb", 'mm')
                        p.wait('v', rb)
                    sl = slice(c * T + piece * 512, c * T + piece * 512 + 512)
                    p.v.tensor_mul(lag["scr2"], resid[:, sl], pb[2 + piece][:, :])
                    p.v.tensor_add(lag["scr2"], lag["scr2"], pb[6][:, :])
                    i = p.v.tensor_add(resid[:, sl], resid[:, sl], lag["scr2"])
                    p.inc(i, 've')
                    lag["idx"] += 1
                    n -= 1
                if lag["idx"] == 32 and lag["last"] is None:
                    lag["last"] = p.mk(f"{tag}lagend", 've')
                    for b in (2, 3, 6):
                        bank_free[b] = lag["last"]

            return xn_ms, xn_p, lag_pump

        def emit_gate(call, xn_ms):
            tag = f"g{call}"
            grps = []
            for piece in range(2):
                p.wait('t', p.xn_p_cur[piece])
                mms = [(pb[2 + piece][0:8, :], c_wg[:, kc * 8:kc * 8 + 8],
                        xnr[:, kc * T + piece * 512:kc * T + piece * 512 + 512])
                       for kc in range(NKC)]
                grps.append(mm_group(2 + piece, mms, f"{tag}lg"))
            lgf = poolf[0:8, PF_LGF:PF_LGF + 1024]
            p.wait('v', grps[1])
            p.v.tensor_copy(lgf[:, 0:512], pb[2][0:8, :])
            i = p.v.tensor_copy(lgf[:, 512:1024], pb[3][0:8, :])
            p.inc(i, 've')
            lg_ms = p.mk(f"{tag}lgf", 've')
            bank_free[2] = lg_ms
            bank_free[3] = lg_ms
            p.wait('t', lg_ms)
            p.wait('t', bank_free[6])
            last = None
            for tc in range(8):
                last = p.t.transpose(pb[6][:, tc * 8:tc * 8 + 8],
                                     lgf[:, tc * 128:(tc + 1) * 128], c_idf[0:8, 0:8])
            p.inc(last, 'mm')
            tr_ms = p.mk(f"{tag}tr", 'mm')
            p.wait('v', tr_ms)
            i = p.v.tensor_copy(gw[:, :], pb[6][:, 0:64])
            p.inc(i, 've')
            bank_free[6] = p.mk(f"{tag}gw", 've')
            m1 = gcols[:, 0:8]
            m2 = gcols[:, 8:16]
            c1 = gcols[:, 16:24]
            tie = gcols[:, 24:32]
            m2e = gcols[:, 32:40]
            zz = gcols[:, 40:48]
            rz = gcols[:, 48:56]
            last = None
            for tc in range(8):
                last = p.v.reduce_max(m1[:, tc:tc + 1], gw[:, tc * 8:tc * 8 + 8], AX.X)
            p.inc(last, 've')
            p.wait('v', p.mk(f"{tag}s1", 've'))
            last = None
            for tc in range(8):
                p.v.tensor_scalar(ge1[:, tc * 8:tc * 8 + 8], gw[:, tc * 8:tc * 8 + 8],
                                  m1[:, tc:tc + 1], None, ALU.is_ge)
                last = p.v.tensor_scalar(gsub[:, tc * 8:tc * 8 + 8], gw[:, tc * 8:tc * 8 + 8],
                                         m1[:, tc:tc + 1], None, ALU.subtract)
            p.inc(last, 've')
            p.wait('v', p.mk(f"{tag}s2a", 've'))
            i = p.v.scalar_tensor_tensor(glm[:, :], ge1[:, :], -1.0e30, gw[:, :],
                                         ALU.mult, ALU.add)
            p.inc(i, 've')
            p.wait('v', p.mk(f"{tag}s2b", 've'))
            last = None
            for tc in range(8):
                p.v.reduce_max(m2[:, tc:tc + 1], glm[:, tc * 8:tc * 8 + 8], AX.X)
                last = p.v.reduce_sum(c1[:, tc:tc + 1], ge1[:, tc * 8:tc * 8 + 8], AX.X)
            p.inc(last, 've')
            sub_ms = p.mk(f"{tag}s2", 've')
            p.wait('v', sub_ms)
            i = p.v.tensor_scalar(tie, c1, 2.0, None, ALU.is_ge)
            p.inc(i, 've')
            p.wait('v', p.mk(f"{tag}s3a", 've'))
            i = p.v.tensor_sub(m2e, m1, m2)
            p.inc(i, 've')
            p.wait('v', p.mk(f"{tag}s3b", 've'))
            i = p.v.tensor_mul(m2e, m2e, tie)
            p.inc(i, 've')
            p.wait('v', p.mk(f"{tag}s3c", 've'))
            i = p.v.tensor_add(m2e, m2e, m2)
            p.inc(i, 've')
            p.wait('v', p.mk(f"{tag}s3", 've'))
            last = None
            for tc in range(8):
                last = p.v.tensor_scalar(gmsk[:, tc * 8:tc * 8 + 8],
                                         gw[:, tc * 8:tc * 8 + 8],
                                         m2e[:, tc:tc + 1], None, ALU.is_ge)
            p.inc(last, 've')
            msk_ms = p.mk(f"{tag}s4", 've')
            p.wait('v', msk_ms)
            p.wait('s', sub_ms)
            i = p.s.activation(gexp[:, :], gsub[:, :], AF.Exp)
            p.inc(i, 'se')
            exp_ms = p.mk(f"{tag}exp", 'se')
            p.wait('v', exp_ms)
            i = p.v.tensor_mul(gpp[:, :], gexp[:, :], gmsk[:, :])
            p.inc(i, 've')
            p.wait('v', p.mk(f"{tag}s5a", 've'))
            last = None
            for tc in range(8):
                last = p.v.reduce_sum(zz[:, tc:tc + 1], gpp[:, tc * 8:tc * 8 + 8], AX.X)
            p.inc(last, 've')
            p.wait('v', p.mk(f"{tag}s5", 've'))
            i = p.v.reciprocal(rz, zz)
            p.inc(i, 've')
            p.wait('v', p.mk(f"{tag}s6", 've'))
            last = None
            for tc in range(8):
                last = p.v.tensor_scalar(gwb[:, tc * 8:tc * 8 + 8],
                                         gpp[:, tc * 8:tc * 8 + 8],
                                         rz[:, tc:tc + 1], 0.5, ALU.mult, ALU.mult)
            p.inc(last, 've')
            w_ms = p.mk(f"{tag}w", 've')
            p.wait('t', w_ms)
            p.wait('t', bank_free[7])
            last = None
            for tc in range(8):
                last = p.t.transpose(pbf[0:8, tc * 128:(tc + 1) * 128],
                                     gwb[:, tc * 8:tc * 8 + 8], c_idb[:, :])
            p.inc(last, 'mm')
            gtr_ms = p.mk(f"{tag}gtr", 'mm')
            p.wait('v', gtr_ms)
            i = p.v.tensor_copy(ygr[0:8, YG_G8:YG_G8 + 1024], pbf[0:8, :])
            p.inc(i, 've')
            g8_ms = p.mk(f"{tag}g8", 've')
            bank_free[7] = g8_ms
            if call == 0:
                tap("d_g8", ygr[0:8, YG_G8:YG_G8 + 1024], g8_ms)
                tap("d_gw", gw[:, :], g8_ms)
                tap("d_gc", gcols[:, :], g8_ms)
                tap("d_msk", gmsk[:, :], g8_ms)
                tap("d_exp", gexp[:, :], g8_ms)
                tap("d_lgf", lgf[:, :], g8_ms)
            return g8_ms

        state = {"grow_pe": None, "h1t_last": [None, None], "hping": [0],
                 "woping": [0]}

        def gsb_make(call, e, g8_ms):
            p.wait('y', g8_ms)
            p.wait('y', state["grow_pe"])
            i = p.y.dma_start(grow[0:1, :], ygr[e:e + 1, YG_G8:YG_G8 + 1024])
            p.inc(i, 'sd', 16)
            grow_ms = p.mk(f"c{call}e{e}grow", 'sd')
            p.wait('t', grow_ms)
            last_pe = None
            for piece in range(2):
                p.wait('t', bank_free[6])
                i = p.t.matmul(pb[6][:, :], c_1b[0:1, :],
                               grow[0:1, piece * 512:piece * 512 + 512],
                               start=True, stop=True)
                p.inc(i, 'mm')
                grp = p.mk(f"c{call}e{e}gb", 'mm')
                if piece == 1:
                    last_pe = grp
                evac_v(6, grp, lambda v, piece=piece, e=e:
                       v.tensor_copy(ygr[:, YG_GSB[e % 2] + piece * 512:
                                          YG_GSB[e % 2] + piece * 512 + 512],
                                     pb[6][:, :]), f"c{call}e{e}ge")
            state["grow_pe"] = last_pe

        def ffn_h(call, e, w, which):
            wms, slot = wload[w]
            base = WSLOT[slot]
            p.wait('t', wms)
            gsb_off = YG_GSB[e % 2]
            last_grp = None
            for ml in range(4):
                m = (which % 2) * 4 + ml
                for piece in range(2):
                    bank = state["hping"][0]
                    state["hping"][0] ^= 1
                    mms = [(pb[bank][:, :],
                            wreg[:, base + kc * 512 + ml * 128:
                                 base + kc * 512 + ml * 128 + 128],
                            xnr[:, kc * T + piece * 512:kc * T + piece * 512 + 512])
                           for kc in range(NKC)]
                    grp = mm_group(bank, mms, f"c{call}e{e}h{which}")
                    last_grp = grp
                    s_sl = inter[:, m * 1024 + piece * 512:m * 1024 + piece * 512 + 512]
                    if which < 2:
                        evac_s(bank, grp, lambda s, s_sl=s_sl, bank=bank:
                               s.activation(s_sl, pb[bank][:, :], AF.Silu),
                               f"c{call}e{e}sil")
                    else:
                        h1t = ygr[:, YG_H1T + piece * 512:YG_H1T + piece * 512 + 512]
                        p.wait('s', state["h1t_last"][piece])
                        ems = evac_s(bank, grp, lambda s, h1t=h1t, bank=bank:
                                     s.activation(h1t, pb[bank][:, :], AF.Copy),
                                     f"c{call}e{e}h1c")
                        p.wait('v', ems)
                        gsb = ygr[:, gsb_off + piece * 512:gsb_off + piece * 512 + 512]
                        p.v.tensor_mul(s_sl, s_sl, gsb)
                        i = p.v.tensor_mul(s_sl, s_sl, h1t)
                        p.inc(i, 've')
                        state["h1t_last"][piece] = p.mk(f"c{call}e{e}im", 've')
            return last_grp

        def ffn_wo(call, e, w, half, inter_ms):
            wms, slot = wload[w]
            base = WSLOT[slot]
            p.wait('t', wms)
            p.wait('t', inter_ms)
            last_add = None
            for ocl in range(8):
                oc = half * 8 + ocl
                for piece in range(2):
                    bank = 2 + state["woping"][0]
                    state["woping"][0] ^= 1
                    mms = [(pb[bank][:, :],
                            wreg[:, base + ic * 1024 + ocl * 128:
                                 base + ic * 1024 + ocl * 128 + 128],
                            inter[:, ic * 1024 + piece * 512:ic * 1024 + piece * 512 + 512])
                           for ic in range(8)]
                    grp = mm_group(bank, mms, f"c{call}e{e}wo")
                    sl = slice(oc * T + piece * 512, oc * T + piece * 512 + 512)
                    last_add = evac_v(bank, grp, lambda v, sl=sl, bank=bank:
                                      v.tensor_add(resid[:, sl], resid[:, sl],
                                                   pb[bank][:, :]),
                                      f"c{call}e{e}ra")
            return last_add

        def emit_ffn(call, resid_ms, resid_eng, xn_free_ms):
            xn_ms, xn_p, _ = emit_ln(f"ln{call * 2}", False, resid_ms, resid_eng, xn_free_ms)
            p.xn_p_cur = xn_p
            g8_ms = emit_gate(call, xn_ms)
            wbase = call * 48
            last_resid = None
            for e in range(NE):
                for ph in range(6):
                    w = wbase + e * 6 + ph
                    if ph < 4:
                        done = ffn_h(call, e, w, ph)
                        if ph == 0:
                            gsb_make(call, e, g8_ms)
                        if ph == 3:
                            inter_ms = p.mk(f"c{call}e{e}int", 've')
                            state["last_inter_ms"] = inter_ms
                            if call == 0 and e == 0:
                                tap("d_int", inter[:, :], inter_ms)
                    else:
                        done = ffn_wo(call, e, w, ph - 4, inter_ms)
                        if ph == 5:
                            last_resid = done
                    slot_ms[w % 3] = done
                    if (w % 48) < 45:
                        load_weight(w + 3)
            return last_resid

        resid_ms = emit_ffn(0, "x_loaded", 'sd', None)
        tap("d_r1", resid[:, :], resid_ms)

        xnfree0 = p.mk("xnfree0", 'mm')
        xn2_ms, xn2_p, lag_pump = emit_ln("ln1m", True, resid_ms, 've', xnfree0)

        win_last = [xnfree0, xnfree0]

        def load_wintile(oc, idx):
            sl = idx % 2
            p.wait('g', win_last[sl])
            i = p.g.dma_start(inter[:, IT_WIN[sl]:IT_WIN[sl] + 2048], wint_ap[oc])
            p.inc(i, 'wd', 16)
            return p.mk(f"win{oc}", 'wd'), sl

        def proj_groups(oc, idx, tag):
            wms, sl = win_tiles[idx]
            p.wait('t', wms)
            grps = []
            for piece in range(2):
                p.wait('t', xn2_p[piece])
                bank = state["hping"][0]
                state["hping"][0] ^= 1
                mms = [(pb[bank][:, :],
                        inter[:, IT_WIN[sl] + kc * 128:IT_WIN[sl] + kc * 128 + 128],
                        xnr[:, kc * T + piece * 512:kc * T + piece * 512 + 512])
                       for kc in range(NKC)]
                grps.append((bank, mm_group(bank, mms, tag)))
            win_last[sl] = grps[1][1]
            return grps

        proj_ocs = [64, 65, 66, 67, 68] + list(range(32, 64))
        win_tiles = {}
        for idx, oc in enumerate(proj_ocs[:2]):
            win_tiles[idx] = load_wintile(oc, idx)

        grps = proj_groups(64, 0, "pdt")
        lag_pump(7)
        win_tiles[2] = load_wintile(proj_ocs[2], 2)
        for piece, (bank, grp) in enumerate(grps):
            evac_v(bank, grp, lambda v, piece=piece, bank=bank:
                   v.tensor_copy(poolf[0:16, PF_DT + piece * 512:PF_DT + piece * 512 + 512],
                                 pb[bank][0:16, :]), "dtev")
        dtraw_ms = p.mk("dtraw", 've')
        dt = poolf[0:16, PF_DT:PF_DT + 1024]
        sh = poolf[0:16, PF_SH:PF_SH + 1024]
        i = p.v.tensor_scalar(dt, dt, c_dtb[:, 0:1], None, ALU.add)
        p.inc(i, 've')
        dtx_ms = p.mk("dtx", 've')
        p.wait('s', dtx_ms)
        p.s.activation(sh, dt, AF.Abs)
        p.s.activation(sh, sh, AF.Exp, scale=-1.0)
        i = p.s.activation(sh, sh, AF.Ln, bias=1.0)
        p.inc(i, 'se')
        sp_ms = p.mk("sp", 'se')
        p.wait('v', sp_ms)
        p.v.tensor_scalar_max(dt, dt, 0.0)
        i = p.v.tensor_add(dt, dt, sh)
        p.inc(i, 've')
        dt_ms = p.mk("dt", 've')
        tap("d_dt", poolf[0:16, PF_DT:PF_DT + 1024], dt_ms)

        scan_s0 = WSLOT[0]
        p.wait('v', slot_ms[0])
        for idx, oc in enumerate(proj_ocs[1:5], start=1):
            grps = proj_groups(oc, idx, "pbc")
            lag_pump(7)
            if idx + 2 < len(proj_ocs):
                win_tiles[idx + 2] = load_wintile(proj_ocs[idx + 2], idx + 2)
            dsts = {65: SP_B[0], 66: SP_B[1], 67: SP_C[0], 68: SP_C[1]}
            off = dsts[oc]
            for piece, (bank, grp) in enumerate(grps):
                evac_v(bank, grp, lambda v, off=off, piece=piece, bank=bank:
                       v.tensor_copy(wreg[:, scan_s0 + off + piece * 512:
                                          scan_s0 + off + piece * 512 + 512],
                                     pb[bank][:, :]), "bcev")

        xsum_started = [False]
        xcs_last = [None, None]
        xc_sil_last = [None]
        pending_xsum = []

        def emit_xsum(c, sil_ms):
            p.wait('t', sil_ms)
            if not xsum_started[0]:
                p.wait('t', bank_free[4])
                p.wait('t', bank_free[5])
            xcs = inter[:, IT_XCS[c % 2]:IT_XCS[c % 2] + 1024]
            last = None
            for piece in range(2):
                last = p.t.matmul(pb[4 + piece][0:16, :], c_ind[:, c * 16:c * 16 + 16],
                                  xcs[:, piece * 512:piece * 512 + 512],
                                  start=not xsum_started[0], stop=False)
            xsum_started[0] = True
            p.inc(last, 'mm')
            xcs_last[c % 2] = p.mk(f"xsm{c}", 'mm')

        for idx0, oc in enumerate(proj_ocs[5:]):
            idx = idx0 + 5
            c = oc - 32
            if len(pending_xsum) >= 2:
                emit_xsum(*pending_xsum.pop(0))
            grps = proj_groups(oc, idx, "pxi")
            if idx + 2 < len(proj_ocs):
                win_tiles[idx + 2] = load_wintile(proj_ocs[idx + 2], idx + 2)
            xi = inter[:, IT_XI:IT_XI + 1024]
            for piece, (bank, grp) in enumerate(grps):
                evac_v(bank, grp, lambda v, piece=piece, bank=bank:
                       v.tensor_copy(xi[:, piece * 512:piece * 512 + 512],
                                     pb[bank][:, :]), "xiev")
            p.v.tensor_copy(xi_f3[:, c * 3:c * 3 + 3], xi[:, 0:3])
            i = p.v.tensor_copy(halo_o[:, c * 3:c * 3 + 3], xi[:, 1021:1024])
            p.inc(i, 've')
            p.mk(f"xis{c}", 've')
            p.wait('v', xc_sil_last[0])
            xc = poolf[:, PF_XC:PF_XC + 1024]
            p.v.tensor_scalar(xc[:, 3:1024], xi[:, 0:1021],
                              c_wc[:, c * 4:c * 4 + 1], None, ALU.mult)
            for k in (1, 2, 3):
                p.v.scalar_tensor_tensor(xc[:, 3:1024], xi[:, k:1021 + k],
                                         c_wc[:, c * 4 + k:c * 4 + k + 1],
                                         xc[:, 3:1024], ALU.mult, ALU.add)
            i = p.v.memset(xc[:, 0:3], 0.0)
            p.inc(i, 've')
            xc_ms = p.mk(f"xc{c}", 've')
            xcs = inter[:, IT_XCS[c % 2]:IT_XCS[c % 2] + 1024]
            p.wait('s', xc_ms)
            p.wait('s', xcs_last[c % 2])
            i = p.s.activation(xcs, xc, AF.Silu)
            p.inc(i, 'se')
            xc_sil_last[0] = p.mk(f"xsl{c}", 'se')
            pending_xsum.append((c, xc_sil_last[0]))
        while pending_xsum:
            emit_xsum(*pending_xsum.pop(0))

        p.wait('s', state["last_inter_ms"])
        for zc in range(4):
            widx = 37 + zc
            win_tiles[widx] = load_wintile(zc, widx)
            grps = proj_groups(zc, widx, f"zp{zc}")
            for piece, (bank, grp) in enumerate(grps):
                evac_s(bank, grp, lambda s, zc=zc, piece=piece, bank=bank:
                       s.activation(ygr[:, zc * 1024 + piece * 512:
                                         zc * 1024 + piece * 512 + 512],
                                    pb[bank][:, :], AF.Silu), f"zps{zc}")
        zpref_ms = p.mk("zpref", 'se')

        halo_ready = p.mk("haloo", 've')
        p.wait('y', halo_ready)
        i = p.y.dma_start(cc1_in.ap().rearrange("o (q x) -> (o q) x", q=128),
                          halo_o[:, :])
        p.inc(i, 'sd', 16)
        cc1s_ms = p.mk("cc1s", 'sd')
        p.wait('g', cc1s_ms)
        i = p.g.collective_compute("AllGather", ALU.bypass,
                                   replica_groups=[list(range(8))],
                                   ins=[cc1_in.ap()], outs=[cc1_out.ap()])
        p.inc(i, 'cc')
        cc1_ms = p.mk("cc1", 'cc')
        prev_use = [None, None]
        chain_ms = None
        for j in range(8):
            sl = j % 2
            p.wait('y', cc1_ms if j < 2 else prev_use[sl])
            i = p.y.dma_start(ccr[:, sl * 96:sl * 96 + 96],
                              cc1_out.ap()[j].rearrange("(q x) -> q x", q=128))
            p.inc(i, 'sd', 16)
            row_ms = p.mk(f"ccr{j}", 'sd')
            p.wait('v', row_ms)
            p.wait('v', chain_ms)
            if j == 0:
                i = p.v.tensor_scalar(halo_i[:, :], ccr[:, sl * 96:sl * 96 + 96],
                                      c_sprev[:, j:j + 1], None, ALU.mult)
            else:
                i = p.v.scalar_tensor_tensor(halo_i[:, :], ccr[:, sl * 96:sl * 96 + 96],
                                             c_sprev[:, j:j + 1], halo_i[:, :],
                                             ALU.mult, ALU.add)
            p.inc(i, 've')
            prev_use[sl] = p.mk(f"hacc{j}", 've')
            chain_ms = prev_use[sl]
        p.wait('v', chain_ms)
        comb3 = comb6[:, :].rearrange("q (c k) -> q c k", k=6)
        p.v.tensor_copy(comb3[:, :, 0:3],
                        halo_i[:, :].rearrange("q (c k) -> q c k", k=3))
        i = p.v.tensor_copy(comb3[:, :, 3:6],
                            xi_f3[:, :].rearrange("q (c k) -> q c k", k=3))
        p.inc(i, 've')
        p.wait('v', p.mk("comb", 've'))
        last = None
        for k in (0, 1, 2, 3):
            for c in range(NZC):
                if k == 0:
                    last = p.v.tensor_scalar(xcfixf[:, c * 3:c * 3 + 3],
                                             comb6[:, c * 6:c * 6 + 3],
                                             c_wc[:, c * 4:c * 4 + 1], None, ALU.mult)
                else:
                    last = p.v.scalar_tensor_tensor(xcfixf[:, c * 3:c * 3 + 3],
                                                    comb6[:, c * 6 + k:c * 6 + k + 3],
                                                    c_wc[:, c * 4 + k:c * 4 + k + 1],
                                                    xcfixf[:, c * 3:c * 3 + 3],
                                                    ALU.mult, ALU.add)
        p.inc(last, 've')
        fix_ms = p.mk("fixf", 've')
        p.wait('s', fix_ms)
        i = p.s.activation(xcfixb[:, :], xcfixf[:, :], AF.Silu)
        p.inc(i, 'se')
        fixb_ms = p.mk("fixb", 'se')
        p.wait('t', fixb_ms)
        last = None
        for c in range(NZC):
            last = p.t.matmul(pb[4][0:16, 0:3], c_ind[:, c * 16:c * 16 + 16],
                              xcfixb[:, c * 3:c * 3 + 3],
                              start=False, stop=(c == NZC - 1))
        p.inc(last, 'mm')
        xsum_ms = p.mk("xsum", 'mm')
        p.wait('v', xsum_ms)
        xs = poolf[0:16, PF_SH:PF_SH + 1024]
        p.v.tensor_copy(xs[:, 0:512], pb[4][0:16, :])
        i = p.v.tensor_copy(xs[:, 512:1024], pb[5][0:16, :])
        p.inc(i, 've')
        xs_ms = p.mk("xs", 've')
        tap("d_xs", xs, xs_ms)
        bank_free[4] = xs_ms
        bank_free[5] = xs_ms
        p.wait('v', xs_ms)
        p.wait('v', dt_ms)
        i = p.v.scalar_tensor_tensor(xs, dt, 1.0 / 256.0, xs, ALU.mult, ALU.mult)
        p.inc(i, 've')
        v_ms = p.mk("v", 've')

        p.wait('t', v_ms)
        for tile in range(2):
            for piece in range(2):
                bank = 6 if (tile * 2 + piece) % 2 == 0 else 2
                p.wait('t', bank_free[bank])
                i = p.t.matmul(pb[bank][:, :], c_r[0:16, tile * 128:tile * 128 + 128],
                               xs[:, piece * 512:piece * 512 + 512],
                               start=True, stop=True)
                p.inc(i, 'mm')
                grp = p.mk(f"vr{tile}{piece}", 'mm')
                tmp = inter[:, IT_XCS[0] + piece * 512:IT_XCS[0] + piece * 512 + 512]
                ems = evac_s(bank, grp, lambda s, tmp=tmp, bank=bank:
                             s.activation(tmp, pb[bank][:, :], AF.Copy), f"vre{tile}{piece}")
                p.wait('v', ems)
                boff = scan_s0 + SP_B[tile] + piece * 512
                i = p.v.tensor_mul(wreg[:, boff:boff + 512], wreg[:, boff:boff + 512], tmp)
                p.inc(i, 've')
                p.mk(f"u{tile}{piece}", 've')
        u_ms = p.mk("u", 've')
        p.wait('s', "consts_f")
        i = p.s.activation(c_acol[:, :], c_alog[:, :], AF.Exp)
        p.inc(i, 'se')
        acol_ms = p.mk("acol", 'se')
        p.wait('v', acol_ms)
        i = p.v.tensor_scalar(c_acol[:, :], c_acol[:, :], -1.0, None, ALU.mult)
        p.inc(i, 've')
        p.wait('v', p.mk("acoln", 've'))
        i = p.v.tensor_scalar(xs, dt, c_acol[:, 0:1], None, ALU.mult)
        p.inc(i, 've')
        loga_ms = p.mk("loga", 've')
        i = p.v.reduce_sum(ls16[:, :], xs, AX.X)
        p.inc(i, 've')
        lsum_ms = p.mk("lsum", 've')
        p.wait('s', lsum_ms)
        i = p.s.activation(d16[:, :], ls16[:, :], AF.Exp)
        p.inc(i, 'se')
        d16_ms = p.mk("d16", 'se')
        p.wait('s', slot_ms[0])
        p.wait('t', loga_ms)
        for tile in range(2):
            for piece in range(2):
                bank = 6 if (tile * 2 + piece) % 2 == 0 else 2
                p.wait('t', bank_free[bank])
                i = p.t.matmul(pb[bank][:, :], c_r[0:16, tile * 128:tile * 128 + 128],
                               xs[:, piece * 512:piece * 512 + 512],
                               start=True, stop=True)
                p.inc(i, 'mm')
                grp = p.mk(f"lr{tile}{piece}", 'mm')
                daoff = scan_s0 + SP_DA[tile] + piece * 512
                evac_s(bank, grp, lambda s, daoff=daoff, bank=bank:
                       s.activation(wreg[:, daoff:daoff + 512], pb[bank][:, :], AF.Exp),
                       f"dae{tile}{piece}")
        da_ms = p.mk("da", 'se')

        p.wait('v', da_ms)
        p.wait('v', u_ms)
        for tile in range(2):
            i = p.v.tensor_tensor_scan(
                wreg[:, scan_s0 + SP_H[tile]:scan_s0 + SP_H[tile] + 1024],
                wreg[:, scan_s0 + SP_DA[tile]:scan_s0 + SP_DA[tile] + 1024],
                wreg[:, scan_s0 + SP_B[tile]:scan_s0 + SP_B[tile] + 1024],
                0.0, ALU.mult, ALU.add)
            p.inc(i, 've')
        for tile in range(2):
            i = p.v.tensor_copy(fcol[:, tile:tile + 1],
                                wreg[:, scan_s0 + SP_H[tile] + 1023:
                                     scan_s0 + SP_H[tile] + 1024])
            p.inc(i, 've')
        f_ms = p.mk("f", 've')
        tap("d_scan", wreg[:, 0:8192], f_ms)
        p.wait('y', f_ms)
        p.wait('y', d16_ms)
        i = p.y.dma_start(cc2_in.ap()[0:1, 0:128].rearrange("o q -> q o"), fcol[:, 0:1])
        p.inc(i, 'sd', 16)
        i = p.y.dma_start(cc2_in.ap()[0:1, 128:256].rearrange("o q -> q o"), fcol[:, 1:2])
        p.inc(i, 'sd', 16)
        i = p.y.dma_start(cc2_in.ap()[0:1, 256:272].rearrange("o q -> q o"), d16[:, :])
        p.inc(i, 'sd', 16)
        cc2s_ms = p.mk("cc2s", 'sd')
        p.wait('g', cc2s_ms)
        i = p.g.collective_compute("AllGather", ALU.bypass,
                                   replica_groups=[list(range(8))],
                                   ins=[cc2_in.ap()], outs=[cc2_out.ap()])
        p.inc(i, 'cc')
        cc2_ms = p.mk("cc2", 'cc')
        p.wait('y', cc2_ms)
        for j in range(8):
            i = p.y.dma_start(fjall[:, 2 * j:2 * j + 1],
                              cc2_out.ap()[j:j + 1, 0:128].rearrange("o q -> q o"))
            p.inc(i, 'sd', 16)
            i = p.y.dma_start(fjall[:, 2 * j + 1:2 * j + 2],
                              cc2_out.ap()[j:j + 1, 128:256].rearrange("o q -> q o"))
            p.inc(i, 'sd', 16)
            i = p.y.dma_start(djall[:, j:j + 1],
                              cc2_out.ap()[j:j + 1, 256:272].rearrange("o q -> q o"))
            p.inc(i, 'sd', 16)
        gath_ms = p.mk("gath", 'sd')
        p.wait('v', gath_ms)
        i = p.v.memset(hcol[:, :], 0.0)
        p.inc(i, 've')
        p.mk("hz", 've')
        for j in range(8):
            p.wait('t', gath_ms)
            for tile in range(2):
                bank = 6 if tile == 0 else 2
                p.wait('t', bank_free[bank])
                i = p.t.matmul(pb[bank][:, 0:1], c_r[0:16, tile * 128:tile * 128 + 128],
                               djall[:, j:j + 1], start=True, stop=True)
                p.inc(i, 'mm')
                grp = p.mk(f"dr{j}{tile}", 'mm')
                evac_v(bank, grp, lambda v, tile=tile, bank=bank:
                       v.tensor_copy(djrep[:, tile:tile + 1], pb[bank][:, 0:1]),
                       f"dre{j}{tile}")
            p.wait('v', p.mk(f"drr{j}", 've'))
            for tile in range(2):
                p.v.scalar_tensor_tensor(djeff[:, tile:tile + 1], djrep[:, tile:tile + 1],
                                         c_sel[:, j:j + 1], c_isel[:, j:j + 1],
                                         ALU.mult, ALU.add)
                i = p.v.tensor_scalar(fjeff[:, tile:tile + 1], fjall[:, 2 * j + tile:2 * j + tile + 1],
                                      c_sel[:, j:j + 1], None, ALU.mult)
            p.inc(i, 've')
            p.wait('v', p.mk(f"cmb{j}", 've'))
            for tile in range(2):
                i = p.v.scalar_tensor_tensor(hcol[:, tile:tile + 1], hcol[:, tile:tile + 1],
                                             djeff[:, tile:tile + 1], fjeff[:, tile:tile + 1],
                                             ALU.mult, ALU.add)
            p.inc(i, 've')
            p.wait('v', p.mk(f"cmh{j}", 've'))
        tap("d_hc", hcol[:, :], p.mk("hcfin", 've'))
        i = p.v.tensor_copy(hb[:, :], hcol[:, :])
        p.inc(i, 've')
        p.wait('v', p.mk("hb", 've'))
        for tile in range(2):
            i = p.v.tensor_tensor_scan(
                wreg[:, scan_s0 + SP_H[tile]:scan_s0 + SP_H[tile] + 1024],
                wreg[:, scan_s0 + SP_DA[tile]:scan_s0 + SP_DA[tile] + 1024],
                wreg[:, scan_s0 + SP_B[tile]:scan_s0 + SP_B[tile] + 1024],
                hb[:, tile:tile + 1], ALU.mult, ALU.add)
            p.inc(i, 've')
        for tile in range(2):
            i = p.v.tensor_mul(wreg[:, scan_s0 + SP_C[tile]:scan_s0 + SP_C[tile] + 1024],
                               wreg[:, scan_s0 + SP_C[tile]:scan_s0 + SP_C[tile] + 1024],
                               wreg[:, scan_s0 + SP_H[tile]:scan_s0 + SP_H[tile] + 1024])
            p.inc(i, 've')
        ch_ms = p.mk("ch", 've')
        p.wait('t', ch_ms)
        ylast = None
        for piece in range(2):
            bank = 4 + piece
            p.wait('t', bank_free[bank])
            for tile in range(2):
                i = p.t.matmul(pb[bank][0:16, :], c_ind2[:, tile * 16:tile * 16 + 16],
                               wreg[:, scan_s0 + SP_C[tile] + piece * 512:
                                    scan_s0 + SP_C[tile] + piece * 512 + 512],
                               start=(tile == 0), stop=(tile == 1))
            p.inc(i, 'mm')
            ylast = p.mk(f"y{piece}", 'mm')
        p.wait('v', ylast)
        p.v.tensor_copy(yb16[:, 0:512], pb[4][0:16, :])
        i = p.v.tensor_copy(yb16[:, 512:1024], pb[5][0:16, :])
        p.inc(i, 've')
        yb_ms = p.mk("yb", 've')
        tap("d_yb", yb16[:, :], yb_ms)
        bank_free[4] = yb_ms
        bank_free[5] = yb_ms
        slot_ms[0] = ylast
        load_weight(48)

        wh2i_last = [None, None]
        yproj_last = [None, None]
        zsil_last = [None, None]
        group_add_ms = {}
        zidx = 4
        for gidx in range(8):
            slot = 1 + (gidx % 2)
            base = WSLOT[slot]
            dep = slot_ms[slot] if gidx < 2 else group_add_ms[gidx - 2]
            p.wait('g', dep)
            for q in range(4):
                ic = gidx * 4 + q
                i = p.g.dma_start(wreg[:, base + q * 2048:base + q * 2048 + 2048],
                                  wott_ap[ic])
                p.inc(i, 'wd', 16)
            wout_ms = p.mk(f"woutg{gidx}", 'wd')
            wsl = gidx % 2
            p.wait('g', wh2i_last[wsl])
            i = p.g.dma_start(inter[0:16, IT_WH2I[wsl]:IT_WH2I[wsl] + 512],
                              wh2it.ap()[0:16, gidx * 512:gidx * 512 + 512])
            p.inc(i, 'wd', 16)
            wh2i_ms = p.mk(f"wh2i{gidx}", 'wd')
            for q in range(4):
                zc = gidx * 4 + q
                if gidx == 0:
                    zbuf = ygr[:, zc * 1024:zc * 1024 + 1024]
                    zsil_ms = zpref_ms
                else:
                    widx = 37 + zidx
                    win_tiles[widx] = load_wintile(zc, widx)
                    zidx += 1
                    grps = proj_groups(zc, widx, f"pz{zc}")
                    zsl = zc % 2
                    zbuf = inter[:, IT_XCS[zsl]:IT_XCS[zsl] + 1024]
                    for piece, (bank, grp) in enumerate(grps):
                        p.wait('s', zsil_last[zsl])
                        evac_s(bank, grp, lambda s, zbuf=zbuf, piece=piece, bank=bank:
                               s.activation(zbuf[:, piece * 512:piece * 512 + 512],
                                            pb[bank][:, :], AF.Silu), f"zsil{zc}")
                    zsil_ms = p.mk(f"zs{zc}", 'se')
                p.wait('t', wh2i_ms)
                p.wait('t', yb_ms)
                for piece in range(2):
                    bank = 4 + piece
                    p.wait('t', bank_free[bank])
                    i = p.t.matmul(pb[bank][:, :],
                                   inter[0:16, IT_WH2I[wsl] + q * 128:IT_WH2I[wsl] + q * 128 + 128],
                                   yb16[:, piece * 512:piece * 512 + 512],
                                   start=True, stop=True)
                    p.inc(i, 'mm')
                    grp = p.mk(f"yp{zc}{piece}", 'mm')
                    ypt = inter[:, IT_XI + piece * 512:IT_XI + piece * 512 + 512]
                    p.wait('s', yproj_last[piece])
                    ems = evac_s(bank, grp, lambda s, ypt=ypt, bank=bank:
                                 s.activation(ypt, pb[bank][:, :], AF.Copy),
                                 f"ype{zc}{piece}")
                    p.wait('v', ems)
                    p.wait('v', zsil_ms)
                    i = p.v.tensor_mul(ygr[:, q * 1024 + piece * 512:q * 1024 + piece * 512 + 512],
                                       zbuf[:, piece * 512:piece * 512 + 512], ypt)
                    p.inc(i, 've')
                    yproj_last[piece] = p.mk(f"ygm{zc}{piece}", 've')
                if gidx != 0:
                    zsil_last[zsl] = yproj_last[1]
            yg_ms = yproj_last[1]
            p.wait('t', wout_ms)
            p.wait('t', yg_ms)
            last_add = None
            for piece in range(2):
                for oc in range(16):
                    bank = 2 + state["woping"][0]
                    state["woping"][0] ^= 1
                    mms = [(pb[bank][:, :],
                            wreg[:, base + q * 2048 + oc * 128:base + q * 2048 + oc * 128 + 128],
                            ygr[:, q * 1024 + piece * 512:q * 1024 + piece * 512 + 512])
                           for q in range(4)]
                    grp = mm_group(bank, mms, f"zwo{gidx}")
                    sl = slice(oc * T + piece * 512, oc * T + piece * 512 + 512)
                    last_add = evac_v(bank, grp, lambda v, sl=sl, bank=bank:
                                      v.tensor_add(resid[:, sl], resid[:, sl],
                                                   pb[bank][:, :]),
                                      f"zra{gidx}")
            group_add_ms[gidx] = last_add
            if gidx == 6:
                slot_ms[1] = last_add
                load_weight(49)
            if gidx == 7:
                slot_ms[2] = last_add
                load_weight(50)
        residB_ms = group_add_ms[7]
        tap("d_r2", resid[:, :], residB_ms)

        xnfree = p.mk("xnfree1", 'mm')
        resid_final = emit_ffn(1, residB_ms, 've', xnfree)

        p.wait('y', resid_final)
        i = p.y.dma_start(oxt.ap().rearrange("(c q) t -> q c t", q=128),
                          resid[:, :].rearrange("q (c t) -> q c t", t=T))
        p.inc(i, 'sd', 16)
        if p.emit == 'y':
            p.eng.wait_ge(p.sems['sd'], p.cnt['sd'])

    with nc.Block() as block:
        @block.sync
        def _(sync):
            program(Prog('y', sync, sems))

        @block.gpsimd
        def _(gp):
            program(Prog('g', gp, sems))

        @block.tensor
        def _(tn):
            program(Prog('t', tn, sems))

        @block.vector
        def _(vc):
            program(Prog('v', vc, sems))

        @block.scalar
        def _(sc):
            program(Prog('s', sc, sems))

    return nc


_NC_CACHE = {}


def _get_nc():
    if "nc" not in _NC_CACHE:
        _NC_CACHE["nc"] = build_program()
    return _NC_CACHE["nc"]


def kernel(L0_attn_x, L0_attn_w_norm, L0_attn_W_in, L0_attn_w_conv, L0_attn_dt_bias,
           L0_attn_A_log, L0_attn_W_h2i, L0_attn_W_out, L0_ffn_w_gate, L0_ffn_wi_0,
           L0_ffn_wi_1, L0_ffn_wo):
    f32 = np.float32
    x = np.asarray(L0_attn_x, f32)
    W_in = np.asarray(L0_attn_W_in, f32)
    w_conv = np.asarray(L0_attn_w_conv, f32)
    W_h2i = np.asarray(L0_attn_W_h2i, f32)
    W_out = np.asarray(L0_attn_W_out, f32)
    w_gate = np.asarray(L0_ffn_w_gate, f32)
    wi_0 = np.asarray(L0_ffn_wi_0, f32)
    wi_1 = np.asarray(L0_ffn_wi_1, f32)
    wo = np.asarray(L0_ffn_wo, f32)

    wi0h = wi_0.reshape(8, 16, 128, 2, 512).transpose(0, 3, 2, 1, 4)
    wi1h = wi_1.reshape(8, 16, 128, 2, 512).transpose(0, 3, 2, 1, 4)
    wih = np.ascontiguousarray(
        np.stack([wi0h[:, 0], wi0h[:, 1], wi1h[:, 0], wi1h[:, 1]], axis=1)
    ).reshape(32, 128, 16 * 512)
    woh = np.ascontiguousarray(
        wo.reshape(8, 8, 128, 2, 1024).transpose(0, 3, 2, 1, 4)
    ).reshape(16, 128, 8 * 1024)
    Wp = np.zeros((8832, 2048), f32)
    Wp[0:8192] = W_in[0:8192]
    Wp[8192:8208] = W_in[8192:8208]
    Wp[8320:8576] = W_in[8208:8464]
    Wp[8576:8832] = W_in[8464:8720]
    wint = np.ascontiguousarray(
        Wp.T.reshape(16, 128, 69, 128).transpose(2, 1, 0, 3)
    ).reshape(69, 128, 16 * 128)
    wott = np.ascontiguousarray(W_out.T).reshape(32, 128, 2048)
    wh2it = np.ascontiguousarray(W_h2i.T)
    wct = np.ascontiguousarray(
        w_conv.reshape(4096, 4).reshape(32, 128, 4).transpose(1, 0, 2)
    ).reshape(128, 128)
    wnt = np.ascontiguousarray(np.asarray(L0_attn_w_norm, f32).reshape(16, 128).T)
    wgt = np.ascontiguousarray(
        w_gate.reshape(16, 128, 8).transpose(1, 0, 2)
    ).reshape(128, 128)
    dtbv = np.asarray(L0_attn_dt_bias, f32).reshape(16, 1)
    alogv = np.asarray(L0_attn_A_log, f32).reshape(16, 1)
    f = np.arange(4096)
    indt = np.ascontiguousarray(
        (f[:, None] // 256 == np.arange(16)[None, :]).astype(f32)
        .reshape(32, 128, 16).transpose(1, 0, 2)
    ).reshape(128, 512)
    pidx = np.arange(128)
    ind2 = np.zeros((128, 32), f32)
    ind2[:, 0:16] = (pidx[:, None] // 16 == np.arange(16)[None, :])
    ind2[:, 16:32] = (pidx[:, None] // 16 + 8 == np.arange(16)[None, :])
    rrep = np.zeros((16, 256), f32)
    for t in range(2):
        rrep[:, t * 128:(t + 1) * 128] = (
            np.arange(16)[:, None] == (t * 8 + pidx[None, :] // 16))
    ones = np.ones((1, 128), f32)
    idm = np.eye(128, dtype=f32)

    shared = dict(wg=wgt, wih=wih, woh=woh, wint=wint, wott=wott, wh2it=wh2it,
                  wct=wct, wnt=wnt, dtb=dtbv, alog=alogv, indt=indt, ind2=ind2,
                  rrep=rrep, ones_f=ones, ones_b=ones, ones_c=np.ones((128, 1), f32),
                  epsr=np.array([[1e-5, -0.5, 1.0 + 1.1920929e-07,
                                  1e-5 * 1.1920929e-07, 0, 0, 0, 0]], f32),
                  id_f=idm, id_b=idm)

    in_maps = []
    for k in range(8):
        beta, pos = k // 4, k % 4
        xk = np.ascontiguousarray(x[beta, pos * T:(pos + 1) * T, :].T)
        selm = np.zeros((128, 8), f32)
        selprev = np.zeros((128, 8), f32)
        for j in range(8):
            if j // 4 == beta and j % 4 < pos:
                selm[:, j] = 1.0
        if pos > 0:
            selprev[:, k - 1] = 1.0
        m = dict(shared)
        m.update(xin=xk, selm=selm, invselm=np.ascontiguousarray(1.0 - selm),
                 selprev=selprev)
        in_maps.append(m)

    nc = _get_nc()
    res = run_bass_kernel_spmd(nc, in_maps, core_ids=list(range(8)))
    out = np.empty((2, 4096, 2048), f32)
    for k in range(8):
        beta, pos = k // 4, k % 4
        out[beta, pos * T:(pos + 1) * T, :] = res.results[k]["oxt"].T
    return out


# version 11
# speedup vs baseline: 1.0618x; 1.0036x over previous
import contextlib

import numpy as np
import concourse.bass as bass
import concourse.mybir as mybir
from concourse.bass_utils import run_bass_kernel_spmd

F32 = mybir.dt.float32
BF16 = mybir.dt.bfloat16
AF = mybir.ActivationFunctionType
ALU = mybir.AluOpType
AX = mybir.AxisListType

T = 1024
D = 2048
NKC = 16
NE = 8
H = 16
NZC = 32
EPS_LN = 1e-5
EPS_RMS = 1.1920929e-07

WSLOT = (0, 8192, 16384)


class _Dummy:
    def __getattr__(self, name):
        def f(*a, **k):
            return None
        return f


_DUM = _Dummy()


class Prog:

    def __init__(self, emit, eng, sems):
        self.emit = emit
        self.eng = eng
        self.sems = sems
        self.cnt = {k: 0 for k in sems}
        self.ms = {}
        self.uid = 0

    def E(self, name):
        return self.eng if self.emit == name else _DUM

    @property
    def g(self):
        return self.E('g')

    @property
    def v(self):
        return self.E('v')

    @property
    def s(self):
        return self.E('s')

    @property
    def t(self):
        return self.E('t')

    @property
    def y(self):
        return self.E('y')

    def inc(self, inst, sem, k=1):
        if inst is not None:
            inst.then_inc(self.sems[sem], k)
        self.cnt[sem] += k

    def mark(self, name, sem):
        assert name not in self.ms, name
        self.ms[name] = (sem, self.cnt[sem])
        return name

    def mkid(self, tag):
        self.uid += 1
        return f"{tag}_{self.uid}"

    def mk(self, tag, sem):
        return self.mark(self.mkid(tag), sem)

    def wait(self, eng_name, msname):
        if msname is None:
            return
        sem, val = self.ms[msname]
        if self.emit == eng_name and val > 0:
            self.eng.wait_ge(self.sems[sem], val)


def build_program(debug=False):
    nc = bass.Bass("TRN2", target_bir_lowering=False, debug=False)

    def inp(name, shape):
        return nc.dram_tensor(name, list(shape), F32, kind="ExternalInput")

    xin = inp("xin", (D, T))
    wg = inp("wg", (128, 128))
    wih = inp("wih", (NE * 4, 128, 16 * 512))
    woh = inp("woh", (NE * 2, 128, 8 * 1024))
    wint = inp("wint", (69, 128, 16 * 128))
    wott = inp("wott", (32, 128, 2048))
    wh2it = inp("wh2it", (16, 4096))
    wct = inp("wct", (128, 128))
    wnt = inp("wnt", (128, 16))
    dtb = inp("dtb", (16, 1))
    alog = inp("alog", (16, 1))
    indt = inp("indt", (128, 512))
    ind2 = inp("ind2", (128, 32))
    rrep = inp("rrep", (16, 256))
    ones_f = inp("ones_f", (1, 128))
    ones_b = inp("ones_b", (1, 128))
    ones_c = inp("ones_c", (128, 1))
    epsr = inp("epsr", (1, 8))
    id_f = inp("id_f", (128, 128))
    id_b = inp("id_b", (128, 128))
    selm = inp("selm", (128, 8))
    invselm = inp("invselm", (128, 8))
    selprev = inp("selprev", (128, 8))
    selm16 = inp("selm16", (128, 16))
    iselm16 = inp("iselm16", (128, 16))

    oxt = nc.dram_tensor("oxt", [D, T], F32, kind="ExternalOutput")

    dbg = {}
    if debug:
        dbg["d_xn"] = nc.dram_tensor("d_xn", [128, 16 * T], BF16, kind="ExternalOutput")
        dbg["d_g8"] = nc.dram_tensor("d_g8", [8, 1024], BF16, kind="ExternalOutput")
        dbg["d_r1"] = nc.dram_tensor("d_r1", [128, 16 * T], F32, kind="ExternalOutput")
        dbg["d_dt"] = nc.dram_tensor("d_dt", [16, T], F32, kind="ExternalOutput")
        dbg["d_xs"] = nc.dram_tensor("d_xs", [16, T], F32, kind="ExternalOutput")
        dbg["d_scan"] = nc.dram_tensor("d_scan", [128, 8192], BF16, kind="ExternalOutput")
        dbg["d_yb"] = nc.dram_tensor("d_yb", [16, T], BF16, kind="ExternalOutput")
        dbg["d_r2"] = nc.dram_tensor("d_r2", [128, 16 * T], F32, kind="ExternalOutput")
        dbg["d_hc"] = nc.dram_tensor("d_hc", [128, 2], F32, kind="ExternalOutput")
        dbg["d_xn2"] = nc.dram_tensor("d_xn2", [128, 16 * T], BF16, kind="ExternalOutput")
        dbg["d_gw"] = nc.dram_tensor("d_gw", [128, 64], F32, kind="ExternalOutput")
        dbg["d_gc"] = nc.dram_tensor("d_gc", [128, 56], F32, kind="ExternalOutput")
        dbg["d_msk"] = nc.dram_tensor("d_msk", [128, 64], F32, kind="ExternalOutput")
        dbg["d_exp"] = nc.dram_tensor("d_exp", [128, 64], F32, kind="ExternalOutput")
        dbg["d_lgf"] = nc.dram_tensor("d_lgf", [8, 1024], F32, kind="ExternalOutput")
        dbg["d_int"] = nc.dram_tensor("d_int", [128, 8192], BF16, kind="ExternalOutput")
    cc1_in = nc.dram_tensor("cc1_in", [1, 32 * 128 * 3], BF16)
    cc1_out = nc.dram_tensor("cc1_out", [8, 32 * 128 * 3], BF16)
    cc2_in = nc.dram_tensor("cc2_in", [1, 272], F32)
    cc2_out = nc.dram_tensor("cc2_out", [8, 272], F32)

    es = contextlib.ExitStack()
    sbuf = lambda name, shape, dt: es.enter_context(nc.sbuf_tensor(name, list(shape), dt))
    psum = lambda name, shape, dt: es.enter_context(nc.psum_tensor(name, list(shape), dt))

    resid = sbuf("resid", (128, 16 * T), F32)
    xnr = sbuf("xnr", (128, 16 * T), BF16)
    wreg = sbuf("wreg", (128, 3 * 8192), BF16)
    inter = sbuf("inter", (128, 8192), BF16)
    ygr = sbuf("ygr", (128, 4096), BF16)
    poolf = sbuf("poolf", (128, 4096), F32)
    c_wg = sbuf("c_wg", (128, 128), BF16)
    c_wc = sbuf("c_wc", (128, 128), F32)
    c_wn = sbuf("c_wn", (128, 16), F32)
    c_dtb = sbuf("c_dtb", (16, 1), F32)
    c_alog = sbuf("c_alog", (16, 1), F32)
    c_acol = sbuf("c_acol", (16, 1), F32)
    c_ind = sbuf("c_ind", (128, 512), BF16)
    c_ind2 = sbuf("c_ind2", (128, 32), BF16)
    c_r = sbuf("c_r", (16, 256), F32)
    c_1f = sbuf("c_1f", (1, 128), F32)
    c_1b = sbuf("c_1b", (1, 128), BF16)
    c_1bc = sbuf("c_1bc", (128, 1), BF16)
    c_eps = sbuf("c_eps", (1, 8), F32)
    c_idf = sbuf("c_idf", (128, 128), F32)
    c_idb = sbuf("c_idb", (128, 128), BF16)
    c_sel = sbuf("c_sel", (128, 8), F32)
    c_isel = sbuf("c_isel", (128, 8), F32)
    c_sprev = sbuf("c_sprev", (128, 8), F32)
    c_sel16 = sbuf("c_sel16", (128, 16), F32)
    c_isel16 = sbuf("c_isel16", (128, 16), F32)
    gw = sbuf("gw", (128, 64), F32)
    ge1 = sbuf("ge1", (128, 64), F32)
    glm = sbuf("glm", (128, 64), F32)
    gsub = sbuf("gsub", (128, 64), F32)
    gmsk = sbuf("gmsk", (128, 64), F32)
    gexp = sbuf("gexp", (128, 64), F32)
    gpp = sbuf("gpp", (128, 64), F32)
    gcols = sbuf("gcols", (128, 56), F32)
    gwb = sbuf("gwb", (128, 64), BF16)
    grow = sbuf("grow", (1, 1024), BF16)
    yb16 = sbuf("yb16", (16, T), BF16)
    halo_o = sbuf("halo_o", (128, 96), BF16)
    halo_i = sbuf("halo_i", (128, 96), BF16)
    xi_f3 = sbuf("xi_f3", (128, 96), BF16)
    comb6 = sbuf("comb6", (128, 192), BF16)
    xcfixf = sbuf("xcfixf", (128, 96), F32)
    xcfixb = sbuf("xcfixb", (128, 96), BF16)
    ccr = sbuf("ccr", (128, 192), BF16)
    fcol = sbuf("fcol", (128, 2), F32)
    hcol = sbuf("hcol", (128, 2), F32)
    hb = sbuf("hb", (128, 2), BF16)
    fjall = sbuf("fjall", (128, 16), F32)
    djall = sbuf("djall", (16, 8), F32)
    djrep = sbuf("djrep", (128, 16), F32)
    djeffa = sbuf("djeffa", (128, 16), F32)
    fjeffa = sbuf("fjeffa", (128, 16), F32)
    djeff = sbuf("djeff", (128, 2), F32)
    fjeff = sbuf("fjeff", (128, 2), F32)
    d16 = sbuf("d16", (16, 1), F32)
    ls16 = sbuf("ls16", (16, 1), F32)

    pb = [psum(f"pb{i}", (128, 512), F32) for i in range(7)]
    pbf = psum("pbf", (128, 1024), BF16)

    sems = {k: es.enter_context(nc.semaphore(k)) for k in
            ("wd", "sd", "mm", "ve", "se", "cc")}

    IT_XI = 0
    IT_WIN = (1024, 3072)
    IT_WH2I = (5120, 5632)
    IT_XCS = (6144, 7168)
    YG_GSB = (0, 1024)
    YG_G8 = 2048
    YG_H1T = 3072
    PF_XC = 0
    PF_DT = 1024
    PF_SH = 2048
    PF_LGF = 3072
    SP_B = (0, 1024)
    SP_C = (2048, 3072)
    SP_H = (4096, 5120)
    SP_DA = (6144, 7168)

    R_MEAN = poolf[0:1, 0:1024]
    R_VAR = poolf[0:1, 1024:2048]
    R_SCR = poolf[0:1, 2048:3072]
    R_A = poolf[0:1, 3072:4096]
    R_RMS = poolf[0:1, 2048:3072]

    wih_ap = wih.ap()
    woh_ap = woh.ap()
    wint_ap = wint.ap()
    wott_ap = wott.ap()

    def program(p: Prog):
        def tap(name, src_ap, ms, eng='v'):
            if not debug:
                return
            p.wait('y', ms)
            i = p.y.dma_start(dbg[name].ap(), src_ap)
            p.inc(i, 'sd', 16)
        bank_free = {i: None for i in range(8)}
        slot_ms = {0: None, 1: None, 2: None}
        wload = {}

        def mm_group(bank, mms, tag):
            p.wait('t', bank_free[bank])
            n = len(mms)
            last = None
            for i, (o, l, r) in enumerate(mms):
                last = p.t.matmul(o, l, r, start=(i == 0), stop=(i == n - 1))
            p.inc(last, 'mm')
            return p.mk(tag, 'mm')

        def evac_v(bank, grp, fn, tag):
            p.wait('v', grp)
            inst = fn(p.v)
            p.inc(inst, 've')
            name = p.mk(tag, 've')
            bank_free[bank] = name
            return name

        def evac_s(bank, grp, fn, tag):
            p.wait('s', grp)
            inst = fn(p.s)
            p.inc(inst, 'se')
            name = p.mk(tag, 'se')
            bank_free[bank] = name
            return name

        i = p.y.dma_start(resid[:, :].rearrange("q (c t) -> q c t", t=T),
                          xin.ap().rearrange("(c q) t -> q c t", q=128))
        p.inc(i, 'sd', 16)
        p.mark("x_loaded", 'sd')
        for dst, src in ((c_wc, wct), (c_wn, wnt), (c_dtb, dtb), (c_alog, alog),
                         (c_r, rrep), (c_1f, ones_f), (c_idf, id_f), (c_sel, selm),
                         (c_isel, invselm), (c_sprev, selprev), (c_eps, epsr),
                         (c_sel16, selm16), (c_isel16, iselm16)):
            i = p.y.dma_start(dst[:, :], src[:, :])
            p.inc(i, 'sd', 16)
        p.mark("consts_f", 'sd')
        for dst, src in ((c_wg, wg), (c_ind, indt), (c_ind2, ind2),
                         (c_1b, ones_b), (c_1bc, ones_c), (c_idb, id_b)):
            i = p.g.dma_start(dst[:, :], src[:, :])
            p.inc(i, 'wd', 16)
        p.mark("consts_b", 'wd')

        def load_weight(w):
            slot = w % 3
            e = (w % 48) // 6
            j = w % 6
            p.wait('g', slot_ms[slot])
            if j < 4:
                src = wih_ap[e * 4 + j]
            else:
                src = woh_ap[e * 2 + (j - 4)]
            i = p.g.dma_start(wreg[:, WSLOT[slot]:WSLOT[slot] + 8192], src)
            p.inc(i, 'wd', 16)
            wload[w] = (p.mk(f"wl{w}", 'wd'), slot)

        for w in (0, 1, 2):
            load_weight(w)

        def emit_ln(tag, with_wn, resid_ms, resid_sem_eng, xn_free_ms,
                    scratch="inter"):
            if scratch == "wreg":
                reg, XB, XQ, SCR, SCR2 = wreg, (0, 1024), (2048, 3072), 4096, 4608
            else:
                reg, XB, XQ, SCR, SCR2 = inter, (5120, 6144), (7168, 0), 5120, 5632
            if not isinstance(resid_ms, dict):
                p.wait('s', resid_ms)
            chunk_mm = [None, None]
            for c in range(NKC):
                xb = reg[:, XB[c % 2]:XB[c % 2] + 1024]
                xq = reg[:, XQ[c % 2]:XQ[c % 2] + 1024]
                if isinstance(resid_ms, dict):
                    p.wait('s', resid_ms[c])
                p.wait('s', chunk_mm[c % 2])
                i = p.s.activation(xb, resid[:, c * T:(c + 1) * T], AF.Copy)
                p.inc(i, 'se')
                cast_ms = p.mk(f"{tag}cb", 'se')
                p.wait('v', cast_ms)
                i = p.v.tensor_mul(xq, xb, xb)
                p.inc(i, 've')
                sq_ms = p.mk(f"{tag}sq", 've')
                p.wait('t', sq_ms)
                if c == 0:
                    for b in range(4):
                        p.wait('t', bank_free[b])
                last = None
                for piece in range(2):
                    last = p.t.matmul(pb[piece][0:1, :], c_1bc[:, 0:1],
                                      xb[:, piece * 512:piece * 512 + 512],
                                      start=(c == 0), stop=(c == NKC - 1))
                    last = p.t.matmul(pb[2 + piece][0:1, :], c_1bc[:, 0:1],
                                      xq[:, piece * 512:piece * 512 + 512],
                                      start=(c == 0), stop=(c == NKC - 1))
                p.inc(last, 'mm')
                chunk_mm[c % 2] = p.mk(f"{tag}cm", 'mm')
            stat_grp = chunk_mm[(NKC - 1) % 2]
            p.wait('v', stat_grp)
            p.v.tensor_copy(R_MEAN[:, 0:512], pb[0][0:1, :])
            p.v.tensor_copy(R_MEAN[:, 512:1024], pb[1][0:1, :])
            p.v.tensor_copy(R_VAR[:, 0:512], pb[2][0:1, :])
            i = p.v.tensor_copy(R_VAR[:, 512:1024], pb[3][0:1, :])
            p.inc(i, 've')
            ev = p.mk(f"{tag}sev", 've')
            for b in range(4):
                bank_free[b] = ev
            p.wait('v', ev)
            p.v.tensor_scalar(R_MEAN, R_MEAN, 1.0 / D, None, ALU.mult)
            p.v.tensor_mul(R_SCR, R_MEAN, R_MEAN)
            p.v.tensor_scalar(R_VAR, R_VAR, 1.0 / D, None, ALU.mult)
            i = p.v.tensor_sub(R_VAR, R_VAR, R_SCR)
            p.inc(i, 've')
            var_ms = p.mk(f"{tag}var", 've')
            p.wait('s', var_ms)
            if with_wn:
                p.s.activation(R_A, R_VAR, AF.Ln,
                               scale=c_eps[0:1, 2:3], bias=c_eps[0:1, 3:4])
                p.s.activation(R_A, R_A, AF.Exp, scale=c_eps[0:1, 1:2])
                p.s.activation(R_SCR, R_VAR, AF.Ln, bias=c_eps[0:1, 0:1])
                i = p.s.activation(R_SCR, R_SCR, AF.Exp, scale=c_eps[0:1, 1:2])
            else:
                p.s.activation(R_A, R_VAR, AF.Ln, bias=c_eps[0:1, 0:1])
                i = p.s.activation(R_A, R_A, AF.Exp, scale=c_eps[0:1, 1:2])
            p.inc(i, 'se')
            rstd_ms = p.mk(f"{tag}rstd", 'se')
            p.wait('v', rstd_ms)
            if with_wn:
                p.v.tensor_mul(R_VAR, R_MEAN, R_A)
                p.v.tensor_scalar(R_VAR, R_VAR, -1.0, None, ALU.mult)
                p.v.tensor_mul(R_MEAN, R_MEAN, R_SCR)
                i = p.v.tensor_scalar(R_MEAN, R_MEAN, -1.0, None, ALU.mult)
            else:
                p.v.tensor_mul(R_VAR, R_MEAN, R_A)
                i = p.v.tensor_scalar(R_VAR, R_VAR, -1.0, None, ALU.mult)
            p.inc(i, 've')
            ab_ms = p.mk(f"{tag}ab", 've')
            p.wait('t', ab_ms)
            for piece in range(2):
                for bi0, row in ((4, R_A), (0, R_VAR)):
                    bi = bi0 + piece
                    p.wait('t', bank_free[bi])
                    i = p.t.matmul(pb[bi][:, :], c_1f[0:1, :],
                                   row[:, piece * 512:piece * 512 + 512],
                                   start=True, stop=True)
                    p.inc(i, 'mm')
            if with_wn:
                for piece in range(2):
                    p.wait('t', bank_free[2 + piece])
                    i = p.t.matmul(pb[2 + piece][:, :], c_1f[0:1, :],
                                   R_SCR[:, piece * 512:piece * 512 + 512],
                                   start=True, stop=True)
                    p.inc(i, 'mm')
                p.wait('t', bank_free[6])
                i = p.t.matmul(pb[6][:, :], c_1f[0:1, :], R_MEAN[:, 0:512],
                               start=True, stop=True)
                p.inc(i, 'mm')
            bc_ms = p.mk(f"{tag}bc", 'mm')
            if with_wn:
                p.wait('v', bc_ms)
                i = p.v.tensor_copy(R_A, R_MEAN)
                p.inc(i, 've')
                p.mk(f"{tag}bsafe", 've')
            p.wait('v', bc_ms)
            p.wait('v', xn_free_ms)
            scr = reg[:, SCR:SCR + 512]
            xn_p = [None, None]
            for piece in range(2):
                last = None
                for c in range(NKC):
                    sl = slice(c * T + piece * 512, c * T + piece * 512 + 512)
                    p.v.tensor_mul(scr, resid[:, sl], pb[4 + piece][:, :])
                    if with_wn:
                        p.v.tensor_add(scr, scr, pb[piece][:, :])
                        last = p.v.tensor_scalar(xnr[:, sl], scr,
                                                 c_wn[:, c:c + 1], None, ALU.mult)
                    else:
                        last = p.v.tensor_add(xnr[:, sl], scr, pb[piece][:, :])
                p.inc(last, 've')
                xn_p[piece] = p.mk(f"{tag}xnp{piece}", 've')
            xn_ms = xn_p[1]
            for b in (0, 1, 4, 5):
                bank_free[b] = xn_ms
            if tag == "ln0":
                tap("d_xn", xnr[:, :], xn_ms)
            if tag == "ln1m":
                tap("d_xn2", xnr[:, :], xn_ms)
            if not with_wn:
                return xn_ms, xn_p, None

            lag = {"items": [(piece, c) for piece in range(2) for c in range(NKC)],
                   "idx": 0, "scr2": reg[:, SCR2:SCR2 + 512], "last": None}

            def lag_pump(n):
                while n > 0 and lag["idx"] < 32:
                    piece, c = lag["items"][lag["idx"]]
                    if lag["idx"] == 16:
                        pm = p.mk(f"{tag}lagp0", 've')
                        p.wait('t', pm)
                        i = p.t.matmul(pb[6][:, :], c_1f[0:1, :],
                                       R_A[:, 512:1024], start=True, stop=True)
                        p.inc(i, 'mm')
                        rb = p.mk(f"{tag}lagrb", 'mm')
                        p.wait('v', rb)
                    sl = slice(c * T + piece * 512, c * T + piece * 512 + 512)
                    p.v.tensor_mul(lag["scr2"], resid[:, sl], pb[2 + piece][:, :])
                    p.v.tensor_add(lag["scr2"], lag["scr2"], pb[6][:, :])
                    i = p.v.tensor_add(resid[:, sl], resid[:, sl], lag["scr2"])
                    p.inc(i, 've')
                    lag["idx"] += 1
                    n -= 1
                if lag["idx"] == 32 and lag["last"] is None:
                    lag["last"] = p.mk(f"{tag}lagend", 've')
                    for b in (2, 3, 6):
                        bank_free[b] = lag["last"]

            return xn_ms, xn_p, lag_pump

        def emit_gate(call, xn_ms):
            tag = f"g{call}"
            grps = []
            for piece in range(2):
                p.wait('t', p.xn_p_cur[piece])
                mms = [(pb[2 + piece][0:8, :], c_wg[:, kc * 8:kc * 8 + 8],
                        xnr[:, kc * T + piece * 512:kc * T + piece * 512 + 512])
                       for kc in range(NKC)]
                grps.append(mm_group(2 + piece, mms, f"{tag}lg"))
            lgf = poolf[0:8, PF_LGF:PF_LGF + 1024]
            p.wait('v', grps[1])
            p.v.tensor_copy(lgf[:, 0:512], pb[2][0:8, :])
            i = p.v.tensor_copy(lgf[:, 512:1024], pb[3][0:8, :])
            p.inc(i, 've')
            lg_ms = p.mk(f"{tag}lgf", 've')
            bank_free[2] = lg_ms
            bank_free[3] = lg_ms
            p.wait('t', lg_ms)
            p.wait('t', bank_free[6])
            last = None
            for tc in range(8):
                last = p.t.transpose(pb[6][:, tc * 8:tc * 8 + 8],
                                     lgf[:, tc * 128:(tc + 1) * 128], c_idf[0:8, 0:8])
            p.inc(last, 'mm')
            tr_ms = p.mk(f"{tag}tr", 'mm')
            p.wait('v', tr_ms)
            i = p.v.tensor_copy(gw[:, :], pb[6][:, 0:64])
            p.inc(i, 've')
            bank_free[6] = p.mk(f"{tag}gw", 've')
            m1 = gcols[:, 0:8]
            m2 = gcols[:, 8:16]
            c1 = gcols[:, 16:24]
            tie = gcols[:, 24:32]
            m2e = gcols[:, 32:40]
            zz = gcols[:, 40:48]
            rz = gcols[:, 48:56]
            last = None
            for tc in range(8):
                last = p.v.reduce_max(m1[:, tc:tc + 1], gw[:, tc * 8:tc * 8 + 8], AX.X)
            p.inc(last, 've')
            p.wait('v', p.mk(f"{tag}s1", 've'))
            last = None
            for tc in range(8):
                p.v.tensor_scalar(ge1[:, tc * 8:tc * 8 + 8], gw[:, tc * 8:tc * 8 + 8],
                                  m1[:, tc:tc + 1], None, ALU.is_ge)
                last = p.v.tensor_scalar(gsub[:, tc * 8:tc * 8 + 8], gw[:, tc * 8:tc * 8 + 8],
                                         m1[:, tc:tc + 1], None, ALU.subtract)
            p.inc(last, 've')
            p.wait('v', p.mk(f"{tag}s2a", 've'))
            i = p.v.scalar_tensor_tensor(glm[:, :], ge1[:, :], -1.0e30, gw[:, :],
                                         ALU.mult, ALU.add)
            p.inc(i, 've')
            p.wait('v', p.mk(f"{tag}s2b", 've'))
            last = None
            for tc in range(8):
                p.v.reduce_max(m2[:, tc:tc + 1], glm[:, tc * 8:tc * 8 + 8], AX.X)
                last = p.v.reduce_sum(c1[:, tc:tc + 1], ge1[:, tc * 8:tc * 8 + 8], AX.X)
            p.inc(last, 've')
            sub_ms = p.mk(f"{tag}s2", 've')
            p.wait('v', sub_ms)
            i = p.v.tensor_scalar(tie, c1, 2.0, None, ALU.is_ge)
            p.inc(i, 've')
            p.wait('v', p.mk(f"{tag}s3a", 've'))
            i = p.v.tensor_sub(m2e, m1, m2)
            p.inc(i, 've')
            p.wait('v', p.mk(f"{tag}s3b", 've'))
            i = p.v.tensor_mul(m2e, m2e, tie)
            p.inc(i, 've')
            p.wait('v', p.mk(f"{tag}s3c", 've'))
            i = p.v.tensor_add(m2e, m2e, m2)
            p.inc(i, 've')
            p.wait('v', p.mk(f"{tag}s3", 've'))
            last = None
            for tc in range(8):
                last = p.v.tensor_scalar(gmsk[:, tc * 8:tc * 8 + 8],
                                         gw[:, tc * 8:tc * 8 + 8],
                                         m2e[:, tc:tc + 1], None, ALU.is_ge)
            p.inc(last, 've')
            msk_ms = p.mk(f"{tag}s4", 've')
            p.wait('v', msk_ms)
            p.wait('s', sub_ms)
            i = p.s.activation(gexp[:, :], gsub[:, :], AF.Exp)
            p.inc(i, 'se')
            exp_ms = p.mk(f"{tag}exp", 'se')
            p.wait('v', exp_ms)
            i = p.v.tensor_mul(gpp[:, :], gexp[:, :], gmsk[:, :])
            p.inc(i, 've')
            p.wait('v', p.mk(f"{tag}s5a", 've'))
            last = None
            for tc in range(8):
                last = p.v.reduce_sum(zz[:, tc:tc + 1], gpp[:, tc * 8:tc * 8 + 8], AX.X)
            p.inc(last, 've')
            p.wait('v', p.mk(f"{tag}s5", 've'))
            i = p.v.reciprocal(rz, zz)
            p.inc(i, 've')
            p.wait('v', p.mk(f"{tag}s6", 've'))
            last = None
            for tc in range(8):
                last = p.v.tensor_scalar(gwb[:, tc * 8:tc * 8 + 8],
                                         gpp[:, tc * 8:tc * 8 + 8],
                                         rz[:, tc:tc + 1], 0.5, ALU.mult, ALU.mult)
            p.inc(last, 've')
            w_ms = p.mk(f"{tag}w", 've')
            p.wait('t', w_ms)
            p.wait('t', bank_free[7])
            last = None
            for tc in range(8):
                last = p.t.transpose(pbf[0:8, tc * 128:(tc + 1) * 128],
                                     gwb[:, tc * 8:tc * 8 + 8], c_idb[:, :])
            p.inc(last, 'mm')
            gtr_ms = p.mk(f"{tag}gtr", 'mm')
            p.wait('v', gtr_ms)
            i = p.v.tensor_copy(ygr[0:8, YG_G8:YG_G8 + 1024], pbf[0:8, :])
            p.inc(i, 've')
            g8_ms = p.mk(f"{tag}g8", 've')
            bank_free[7] = g8_ms
            if call == 0:
                tap("d_g8", ygr[0:8, YG_G8:YG_G8 + 1024], g8_ms)
                tap("d_gw", gw[:, :], g8_ms)
                tap("d_gc", gcols[:, :], g8_ms)
                tap("d_msk", gmsk[:, :], g8_ms)
                tap("d_exp", gexp[:, :], g8_ms)
                tap("d_lgf", lgf[:, :], g8_ms)
            return g8_ms

        state = {"grow_pe": None, "h1t_last": [None, None], "hping": [0],
                 "woping": [0], "resid_oc_ms": {}, "last_inter_ms": None,
                 "last_xn_use": None}

        def gsb_make(call, e, g8_ms):
            p.wait('y', g8_ms)
            p.wait('y', state["grow_pe"])
            i = p.y.dma_start(grow[0:1, :], ygr[e:e + 1, YG_G8:YG_G8 + 1024])
            p.inc(i, 'sd', 16)
            grow_ms = p.mk(f"c{call}e{e}grow", 'sd')
            p.wait('t', grow_ms)
            last_pe = None
            for piece in range(2):
                p.wait('t', bank_free[6])
                i = p.t.matmul(pb[6][:, :], c_1b[0:1, :],
                               grow[0:1, piece * 512:piece * 512 + 512],
                               start=True, stop=True)
                p.inc(i, 'mm')
                grp = p.mk(f"c{call}e{e}gb", 'mm')
                if piece == 1:
                    last_pe = grp
                evac_v(6, grp, lambda v, piece=piece, e=e:
                       v.tensor_copy(ygr[:, YG_GSB[e % 2] + piece * 512:
                                          YG_GSB[e % 2] + piece * 512 + 512],
                                     pb[6][:, :]), f"c{call}e{e}ge")
            state["grow_pe"] = last_pe

        def ffn_h(call, e, w, which):
            wms, slot = wload[w]
            base = WSLOT[slot]
            p.wait('t', wms)
            gsb_off = YG_GSB[e % 2]
            last_grp = None
            for ml in range(4):
                m = (which % 2) * 4 + ml
                for piece in range(2):
                    bank = state["hping"][0]
                    state["hping"][0] ^= 1
                    mms = [(pb[bank][:, :],
                            wreg[:, base + kc * 512 + ml * 128:
                                 base + kc * 512 + ml * 128 + 128],
                            xnr[:, kc * T + piece * 512:kc * T + piece * 512 + 512])
                           for kc in range(NKC)]
                    grp = mm_group(bank, mms, f"c{call}e{e}h{which}")
                    last_grp = grp
                    s_sl = inter[:, m * 1024 + piece * 512:m * 1024 + piece * 512 + 512]
                    if which < 2:
                        evac_s(bank, grp, lambda s, s_sl=s_sl, bank=bank:
                               s.activation(s_sl, pb[bank][:, :], AF.Silu),
                               f"c{call}e{e}sil")
                    else:
                        h1t = ygr[:, YG_H1T + piece * 512:YG_H1T + piece * 512 + 512]
                        p.wait('s', state["h1t_last"][piece])
                        ems = evac_s(bank, grp, lambda s, h1t=h1t, bank=bank:
                                     s.activation(h1t, pb[bank][:, :], AF.Copy),
                                     f"c{call}e{e}h1c")
                        p.wait('v', ems)
                        gsb = ygr[:, gsb_off + piece * 512:gsb_off + piece * 512 + 512]
                        p.v.tensor_mul(s_sl, s_sl, gsb)
                        i = p.v.tensor_mul(s_sl, s_sl, h1t)
                        p.inc(i, 've')
                        state["h1t_last"][piece] = p.mk(f"c{call}e{e}im", 've')
            return last_grp

        def ffn_wo(call, e, w, half, inter_ms):
            wms, slot = wload[w]
            base = WSLOT[slot]
            p.wait('t', wms)
            p.wait('t', inter_ms)
            last_add = None
            for ocl in range(8):
                oc = half * 8 + ocl
                for piece in range(2):
                    bank = 2 + state["woping"][0]
                    state["woping"][0] ^= 1
                    mms = [(pb[bank][:, :],
                            wreg[:, base + ic * 1024 + ocl * 128:
                                 base + ic * 1024 + ocl * 128 + 128],
                            inter[:, ic * 1024 + piece * 512:ic * 1024 + piece * 512 + 512])
                           for ic in range(8)]
                    grp = mm_group(bank, mms, f"c{call}e{e}wo")
                    sl = slice(oc * T + piece * 512, oc * T + piece * 512 + 512)
                    last_add = evac_v(bank, mms and grp, lambda v, sl=sl, bank=bank:
                                      v.tensor_add(resid[:, sl], resid[:, sl],
                                                   pb[bank][:, :]),
                                      f"c{call}e{e}ra")
                    if e == NE - 1 and piece == 1:
                        state["resid_oc_ms"][oc] = last_add
            return last_add

        def emit_ffn(call, resid_ms, resid_eng, xn_free_ms):
            xn_ms, xn_p, _ = emit_ln(f"ln{call * 2}", False, resid_ms, resid_eng, xn_free_ms)
            p.xn_p_cur = xn_p
            g8_ms = emit_gate(call, xn_ms)
            wbase = call * 48
            last_resid = None
            for e in range(NE):
                for ph in range(6):
                    w = wbase + e * 6 + ph
                    if ph < 4:
                        done = ffn_h(call, e, w, ph)
                        if ph == 0:
                            gsb_make(call, e, g8_ms)
                        if ph == 3:
                            inter_ms = p.mk(f"c{call}e{e}int", 've')
                            state["last_inter_ms"] = inter_ms
                            if e == NE - 1:
                                state["last_xn_use"] = done
                            if call == 0 and e == 0:
                                tap("d_int", inter[:, :], inter_ms)
                    else:
                        done = ffn_wo(call, e, w, ph - 4, inter_ms)
                        if ph == 5:
                            last_resid = done
                    slot_ms[w % 3] = done
                    if (w % 48) < 45:
                        load_weight(w + 3)
            return last_resid

        resid_ms = emit_ffn(0, "x_loaded", 'sd', None)
        tap("d_r1", resid[:, :], resid_ms)

        xnfree0 = p.mk("xnfree0", 'mm')
        p.wait('s', slot_ms[0])
        p.wait('v', slot_ms[0])
        xn2_ms, xn2_p, lag_pump = emit_ln("ln1m", True, dict(state["resid_oc_ms"]),
                                          've', state["last_xn_use"],
                                          scratch="wreg")
        state["resid_oc_ms"].clear()

        win_last = [xnfree0, xnfree0]

        def load_wintile(oc, idx):
            sl = idx % 2
            p.wait('g', win_last[sl])
            i = p.g.dma_start(inter[:, IT_WIN[sl]:IT_WIN[sl] + 2048], wint_ap[oc])
            p.inc(i, 'wd', 16)
            return p.mk(f"win{oc}", 'wd'), sl

        def proj_groups(oc, idx, tag):
            wms, sl = win_tiles[idx]
            p.wait('t', wms)
            grps = []
            for piece in range(2):
                p.wait('t', xn2_p[piece])
                bank = state["hping"][0]
                state["hping"][0] ^= 1
                mms = [(pb[bank][:, :],
                        inter[:, IT_WIN[sl] + kc * 128:IT_WIN[sl] + kc * 128 + 128],
                        xnr[:, kc * T + piece * 512:kc * T + piece * 512 + 512])
                       for kc in range(NKC)]
                grps.append((bank, mm_group(bank, mms, tag)))
            win_last[sl] = grps[1][1]
            return grps

        proj_ocs = list(range(32, 64)) + [64, 65, 66, 67, 68]
        win_tiles = {}
        for idx, oc in enumerate(proj_ocs[:2]):
            win_tiles[idx] = load_wintile(oc, idx)
        xsum_started = [False]
        xcs_last = [None, None]
        xc_sil_last = [None]
        pending_xsum = []

        def emit_xsum(c, sil_ms):
            p.wait('t', sil_ms)
            if not xsum_started[0]:
                p.wait('t', bank_free[4])
                p.wait('t', bank_free[5])
            xcs = inter[:, IT_XCS[c % 2]:IT_XCS[c % 2] + 1024]
            last = None
            for piece in range(2):
                last = p.t.matmul(pb[4 + piece][0:16, :], c_ind[:, c * 16:c * 16 + 16],
                                  xcs[:, piece * 512:piece * 512 + 512],
                                  start=not xsum_started[0], stop=False)
            xsum_started[0] = True
            p.inc(last, 'mm')
            xcs_last[c % 2] = p.mk(f"xsm{c}", 'mm')

        for idx0, oc in enumerate(proj_ocs[0:32]):
            idx = idx0
            c = oc - 32
            lag_pump(2)
            if len(pending_xsum) >= 2:
                emit_xsum(*pending_xsum.pop(0))
            grps = proj_groups(oc, idx, "pxi")
            if idx + 2 <= 31:
                win_tiles[idx + 2] = load_wintile(proj_ocs[idx + 2], idx + 2)
            xi = inter[:, IT_XI:IT_XI + 1024]
            for piece, (bank, grp) in enumerate(grps):
                evac_v(bank, grp, lambda v, piece=piece, bank=bank:
                       v.tensor_copy(xi[:, piece * 512:piece * 512 + 512],
                                     pb[bank][:, :]), "xiev")
            p.v.tensor_copy(xi_f3[:, c * 3:c * 3 + 3], xi[:, 0:3])
            i = p.v.tensor_copy(halo_o[:, c * 3:c * 3 + 3], xi[:, 1021:1024])
            p.inc(i, 've')
            p.mk(f"xis{c}", 've')
            p.wait('v', xc_sil_last[0])
            xc = poolf[:, PF_XC:PF_XC + 1024]
            p.v.tensor_scalar(xc[:, 3:1024], xi[:, 0:1021],
                              c_wc[:, c * 4:c * 4 + 1], None, ALU.mult)
            for k in (1, 2, 3):
                p.v.scalar_tensor_tensor(xc[:, 3:1024], xi[:, k:1021 + k],
                                         c_wc[:, c * 4 + k:c * 4 + k + 1],
                                         xc[:, 3:1024], ALU.mult, ALU.add)
            i = p.v.memset(xc[:, 0:3], 0.0)
            p.inc(i, 've')
            xc_ms = p.mk(f"xc{c}", 've')
            xcs = inter[:, IT_XCS[c % 2]:IT_XCS[c % 2] + 1024]
            p.wait('s', xc_ms)
            p.wait('s', xcs_last[c % 2])
            i = p.s.activation(xcs, xc, AF.Silu)
            p.inc(i, 'se')
            xc_sil_last[0] = p.mk(f"xsl{c}", 'se')
            pending_xsum.append((c, xc_sil_last[0]))
        while pending_xsum:
            emit_xsum(*pending_xsum.pop(0))

        halo_ready = p.mk("haloo", 've')
        p.wait('y', halo_ready)
        i = p.y.dma_start(cc1_in.ap().rearrange("o (q x) -> (o q) x", q=128),
                          halo_o[:, :])
        p.inc(i, 'sd', 16)
        cc1s_ms = p.mk("cc1s", 'sd')
        p.wait('g', cc1s_ms)
        i = p.g.collective_compute("AllGather", ALU.bypass,
                                   replica_groups=[list(range(8))],
                                   ins=[cc1_in.ap()], outs=[cc1_out.ap()])
        p.inc(i, 'cc')
        cc1_ms = p.mk("cc1", 'cc')
        p.wait('s', state["last_inter_ms"])
        for zc in range(4):
            widx = 37 + zc
            win_tiles[widx] = load_wintile(zc, widx)
            grps = proj_groups(zc, widx, f"zp{zc}")
            for piece, (bank, grp) in enumerate(grps):
                evac_s(bank, grp, lambda s, zc=zc, piece=piece, bank=bank:
                       s.activation(ygr[:, zc * 1024 + piece * 512:
                                         zc * 1024 + piece * 512 + 512],
                                    pb[bank][:, :], AF.Silu), f"zps{zc}")
        zpref_ms = p.mk("zpref", 'se')

        win_tiles[32] = load_wintile(proj_ocs[32], 32)
        win_tiles[33] = load_wintile(proj_ocs[33], 33)
        grps = proj_groups(64, 32, "pdt")
        win_tiles[34] = load_wintile(proj_ocs[34], 34)
        for piece, (bank, grp) in enumerate(grps):
            evac_v(bank, grp, lambda v, piece=piece, bank=bank:
                   v.tensor_copy(poolf[0:16, PF_DT + piece * 512:PF_DT + piece * 512 + 512],
                                 pb[bank][0:16, :]), "dtev")
        dtraw_ms = p.mk("dtraw", 've')
        dt = poolf[0:16, PF_DT:PF_DT + 1024]
        sh = poolf[0:16, PF_SH:PF_SH + 1024]
        i = p.v.tensor_scalar(dt, dt, c_dtb[:, 0:1], None, ALU.add)
        p.inc(i, 've')
        dtx_ms = p.mk("dtx", 've')
        p.wait('s', dtx_ms)
        p.s.activation(sh, dt, AF.Abs)
        p.s.activation(sh, sh, AF.Exp, scale=-1.0)
        i = p.s.activation(sh, sh, AF.Ln, bias=1.0)
        p.inc(i, 'se')
        sp_ms = p.mk("sp", 'se')
        p.wait('v', sp_ms)
        p.v.tensor_scalar_max(dt, dt, 0.0)
        i = p.v.tensor_add(dt, dt, sh)
        p.inc(i, 've')
        dt_ms = p.mk("dt", 've')
        tap("d_dt", poolf[0:16, PF_DT:PF_DT + 1024], dt_ms)

        scan_s0 = WSLOT[0]
        p.wait('v', slot_ms[0])
        for idx, oc in enumerate(proj_ocs[33:], start=33):
            grps = proj_groups(oc, idx, "pbc")
            if idx + 2 < len(proj_ocs):
                win_tiles[idx + 2] = load_wintile(proj_ocs[idx + 2], idx + 2)
            dsts = {65: SP_B[0], 66: SP_B[1], 67: SP_C[0], 68: SP_C[1]}
            off = dsts[oc]
            for piece, (bank, grp) in enumerate(grps):
                evac_v(bank, grp, lambda v, off=off, piece=piece, bank=bank:
                       v.tensor_copy(wreg[:, scan_s0 + off + piece * 512:
                                          scan_s0 + off + piece * 512 + 512],
                                     pb[bank][:, :]), "bcev")

        win_tiles[41] = load_wintile(4, 41)
        grps = proj_groups(4, 41, "zp4")
        for piece, (bank, grp) in enumerate(grps):
            p.wait('s', xcs_last[0])
            evac_s(bank, grp, lambda s, piece=piece, bank=bank:
                   s.activation(inter[:, IT_XCS[0] + piece * 512:
                                      IT_XCS[0] + piece * 512 + 512],
                                pb[bank][:, :], AF.Silu), "zp4s")
        zp4_ms = p.mk("zp4", 'se')

        prev_use = [None, None]
        chain_ms = None
        for j in range(8):
            sl = j % 2
            p.wait('y', cc1_ms if j < 2 else prev_use[sl])
            i = p.y.dma_start(ccr[:, sl * 96:sl * 96 + 96],
                              cc1_out.ap()[j].rearrange("(q x) -> q x", q=128))
            p.inc(i, 'sd', 16)
            row_ms = p.mk(f"ccr{j}", 'sd')
            p.wait('v', row_ms)
            p.wait('v', chain_ms)
            if j == 0:
                i = p.v.tensor_scalar(halo_i[:, :], ccr[:, sl * 96:sl * 96 + 96],
                                      c_sprev[:, j:j + 1], None, ALU.mult)
            else:
                i = p.v.scalar_tensor_tensor(halo_i[:, :], ccr[:, sl * 96:sl * 96 + 96],
                                             c_sprev[:, j:j + 1], halo_i[:, :],
                                             ALU.mult, ALU.add)
            p.inc(i, 've')
            prev_use[sl] = p.mk(f"hacc{j}", 've')
            chain_ms = prev_use[sl]
        p.wait('v', chain_ms)
        comb3 = comb6[:, :].rearrange("q (c k) -> q c k", k=6)
        p.v.tensor_copy(comb3[:, :, 0:3],
                        halo_i[:, :].rearrange("q (c k) -> q c k", k=3))
        i = p.v.tensor_copy(comb3[:, :, 3:6],
                            xi_f3[:, :].rearrange("q (c k) -> q c k", k=3))
        p.inc(i, 've')
        p.wait('v', p.mk("comb", 've'))
        last = None
        for k in (0, 1, 2, 3):
            for c in range(NZC):
                if k == 0:
                    last = p.v.tensor_scalar(xcfixf[:, c * 3:c * 3 + 3],
                                             comb6[:, c * 6:c * 6 + 3],
                                             c_wc[:, c * 4:c * 4 + 1], None, ALU.mult)
                else:
                    last = p.v.scalar_tensor_tensor(xcfixf[:, c * 3:c * 3 + 3],
                                                    comb6[:, c * 6 + k:c * 6 + k + 3],
                                                    c_wc[:, c * 4 + k:c * 4 + k + 1],
                                                    xcfixf[:, c * 3:c * 3 + 3],
                                                    ALU.mult, ALU.add)
        p.inc(last, 've')
        fix_ms = p.mk("fixf", 've')
        p.wait('s', fix_ms)
        i = p.s.activation(xcfixb[:, :], xcfixf[:, :], AF.Silu)
        p.inc(i, 'se')
        fixb_ms = p.mk("fixb", 'se')
        p.wait('t', fixb_ms)
        last = None
        for c in range(NZC):
            last = p.t.matmul(pb[4][0:16, 0:3], c_ind[:, c * 16:c * 16 + 16],
                              xcfixb[:, c * 3:c * 3 + 3],
                              start=False, stop=(c == NZC - 1))
        p.inc(last, 'mm')
        xsum_ms = p.mk("xsum", 'mm')
        p.wait('v', xsum_ms)
        xs = poolf[0:16, PF_SH:PF_SH + 1024]
        p.v.tensor_copy(xs[:, 0:512], pb[4][0:16, :])
        i = p.v.tensor_copy(xs[:, 512:1024], pb[5][0:16, :])
        p.inc(i, 've')
        xs_ms = p.mk("xs", 've')
        tap("d_xs", xs, xs_ms)
        bank_free[4] = xs_ms
        bank_free[5] = xs_ms
        p.wait('v', xs_ms)
        p.wait('v', dt_ms)
        i = p.v.scalar_tensor_tensor(xs, dt, 1.0 / 256.0, xs, ALU.mult, ALU.mult)
        p.inc(i, 've')
        v_ms = p.mk("v", 've')

        p.wait('t', v_ms)
        for tile in range(2):
            for piece in range(2):
                bank = 6 if (tile * 2 + piece) % 2 == 0 else 2
                p.wait('t', bank_free[bank])
                i = p.t.matmul(pb[bank][:, :], c_r[0:16, tile * 128:tile * 128 + 128],
                               xs[:, piece * 512:piece * 512 + 512],
                               start=True, stop=True)
                p.inc(i, 'mm')
                grp = p.mk(f"vr{tile}{piece}", 'mm')
                tmp = inter[:, IT_XCS[0] + piece * 512:IT_XCS[0] + piece * 512 + 512]
                ems = evac_s(bank, grp, lambda s, tmp=tmp, bank=bank:
                             s.activation(tmp, pb[bank][:, :], AF.Copy), f"vre{tile}{piece}")
                p.wait('v', ems)
                boff = scan_s0 + SP_B[tile] + piece * 512
                i = p.v.tensor_mul(wreg[:, boff:boff + 512], wreg[:, boff:boff + 512], tmp)
                p.inc(i, 've')
                p.mk(f"u{tile}{piece}", 've')
        u_ms = p.mk("u", 've')
        p.wait('s', "consts_f")
        i = p.s.activation(c_acol[:, :], c_alog[:, :], AF.Exp)
        p.inc(i, 'se')
        acol_ms = p.mk("acol", 'se')
        p.wait('v', acol_ms)
        i = p.v.tensor_scalar(c_acol[:, :], c_acol[:, :], -1.0, None, ALU.mult)
        p.inc(i, 've')
        p.wait('v', p.mk("acoln", 've'))
        i = p.v.tensor_scalar(xs, dt, c_acol[:, 0:1], None, ALU.mult)
        p.inc(i, 've')
        loga_ms = p.mk("loga", 've')
        i = p.v.reduce_sum(ls16[:, :], xs, AX.X)
        p.inc(i, 've')
        lsum_ms = p.mk("lsum", 've')
        p.wait('s', lsum_ms)
        i = p.s.activation(d16[:, :], ls16[:, :], AF.Exp)
        p.inc(i, 'se')
        d16_ms = p.mk("d16", 'se')
        p.wait('s', slot_ms[0])
        p.wait('t', loga_ms)
        for tile in range(2):
            for piece in range(2):
                bank = 6 if (tile * 2 + piece) % 2 == 0 else 2
                p.wait('t', bank_free[bank])
                i = p.t.matmul(pb[bank][:, :], c_r[0:16, tile * 128:tile * 128 + 128],
                               xs[:, piece * 512:piece * 512 + 512],
                               start=True, stop=True)
                p.inc(i, 'mm')
                grp = p.mk(f"lr{tile}{piece}", 'mm')
                daoff = scan_s0 + SP_DA[tile] + piece * 512
                evac_s(bank, grp, lambda s, daoff=daoff, bank=bank:
                       s.activation(wreg[:, daoff:daoff + 512], pb[bank][:, :], AF.Exp),
                       f"dae{tile}{piece}")
        da_ms = p.mk("da", 'se')
        win_tiles[42] = load_wintile(5, 42)
        grps = proj_groups(5, 42, "zp5")
        for piece, (bank, grp) in enumerate(grps):
            p.wait('s', xcs_last[1])
            evac_s(bank, grp, lambda s, piece=piece, bank=bank:
                   s.activation(inter[:, IT_XCS[1] + piece * 512:
                                      IT_XCS[1] + piece * 512 + 512],
                                pb[bank][:, :], AF.Silu), "zp5s")
        zp5_ms = p.mk("zp5", 'se')

        p.wait('v', da_ms)
        p.wait('v', u_ms)
        for tile in range(2):
            i = p.v.tensor_tensor_scan(
                wreg[:, scan_s0 + SP_H[tile]:scan_s0 + SP_H[tile] + 1024],
                wreg[:, scan_s0 + SP_DA[tile]:scan_s0 + SP_DA[tile] + 1024],
                wreg[:, scan_s0 + SP_B[tile]:scan_s0 + SP_B[tile] + 1024],
                0.0, ALU.mult, ALU.add)
            p.inc(i, 've')
        for tile in range(2):
            i = p.v.tensor_copy(fcol[:, tile:tile + 1],
                                wreg[:, scan_s0 + SP_H[tile] + 1023:
                                     scan_s0 + SP_H[tile] + 1024])
            p.inc(i, 've')
        f_ms = p.mk("f", 've')
        tap("d_scan", wreg[:, 0:8192], f_ms)
        p.wait('y', f_ms)
        p.wait('y', d16_ms)
        i = p.y.dma_start(cc2_in.ap()[0:1, 0:128].rearrange("o q -> q o"), fcol[:, 0:1])
        p.inc(i, 'sd', 16)
        i = p.y.dma_start(cc2_in.ap()[0:1, 128:256].rearrange("o q -> q o"), fcol[:, 1:2])
        p.inc(i, 'sd', 16)
        i = p.y.dma_start(cc2_in.ap()[0:1, 256:272].rearrange("o q -> q o"), d16[:, :])
        p.inc(i, 'sd', 16)
        cc2s_ms = p.mk("cc2s", 'sd')
        p.wait('g', cc2s_ms)
        i = p.g.collective_compute("AllGather", ALU.bypass,
                                   replica_groups=[list(range(8))],
                                   ins=[cc2_in.ap()], outs=[cc2_out.ap()])
        p.inc(i, 'cc')
        cc2_ms = p.mk("cc2", 'cc')
        p.wait('y', cc2_ms)
        for j in range(8):
            i = p.y.dma_start(fjall[:, 2 * j:2 * j + 1],
                              cc2_out.ap()[j:j + 1, 0:128].rearrange("o q -> q o"))
            p.inc(i, 'sd', 16)
            i = p.y.dma_start(fjall[:, 2 * j + 1:2 * j + 2],
                              cc2_out.ap()[j:j + 1, 128:256].rearrange("o q -> q o"))
            p.inc(i, 'sd', 16)
            i = p.y.dma_start(djall[:, j:j + 1],
                              cc2_out.ap()[j:j + 1, 256:272].rearrange("o q -> q o"))
            p.inc(i, 'sd', 16)
        gath_ms = p.mk("gath", 'sd')
        p.wait('v', gath_ms)
        i = p.v.memset(hcol[:, :], 0.0)
        p.inc(i, 've')
        p.mk("hz", 've')
        p.wait('t', gath_ms)
        p.wait('t', bank_free[6])
        last = None
        for j in range(8):
            for tile in range(2):
                last = p.t.matmul(pb[6][:, 2 * j + tile:2 * j + tile + 1],
                                  c_r[0:16, tile * 128:tile * 128 + 128],
                                  djall[:, j:j + 1], start=True, stop=True)
        p.inc(last, 'mm')
        drg = p.mk("drall", 'mm')
        p.wait('v', drg)
        i = p.v.tensor_copy(djrep[:, :], pb[6][:, 0:16])
        p.inc(i, 've')
        dre = p.mk("drev", 've')
        bank_free[6] = dre
        p.wait('v', dre)
        p.v.tensor_mul(djeffa[:, :], djrep[:, :], c_sel16[:, :])
        p.v.tensor_add(djeffa[:, :], djeffa[:, :], c_isel16[:, :])
        i = p.v.tensor_mul(fjeffa[:, :], fjall[:, :], c_sel16[:, :])
        p.inc(i, 've')
        p.wait('v', p.mk("effs", 've'))
        for j in range(8):
            for tile in range(2):
                i = p.v.scalar_tensor_tensor(hcol[:, tile:tile + 1], hcol[:, tile:tile + 1],
                                             djeffa[:, 2 * j + tile:2 * j + tile + 1],
                                             fjeffa[:, 2 * j + tile:2 * j + tile + 1],
                                             ALU.mult, ALU.add)
            p.inc(i, 've')
            p.wait('v', p.mk(f"cmh{j}", 've'))
        i = p.v.tensor_copy(hb[:, :], hcol[:, :])
        p.inc(i, 've')
        p.wait('v', p.mk("hb", 've'))
        for tile in range(2):
            i = p.v.tensor_tensor_scan(
                wreg[:, scan_s0 + SP_H[tile]:scan_s0 + SP_H[tile] + 1024],
                wreg[:, scan_s0 + SP_DA[tile]:scan_s0 + SP_DA[tile] + 1024],
                wreg[:, scan_s0 + SP_B[tile]:scan_s0 + SP_B[tile] + 1024],
                hb[:, tile:tile + 1], ALU.mult, ALU.add)
            p.inc(i, 've')
        for tile in range(2):
            i = p.v.tensor_mul(wreg[:, scan_s0 + SP_C[tile]:scan_s0 + SP_C[tile] + 1024],
                               wreg[:, scan_s0 + SP_C[tile]:scan_s0 + SP_C[tile] + 1024],
                               wreg[:, scan_s0 + SP_H[tile]:scan_s0 + SP_H[tile] + 1024])
            p.inc(i, 've')
        ch_ms = p.mk("ch", 've')
        p.wait('t', ch_ms)
        ylast = None
        for piece in range(2):
            bank = 4 + piece
            p.wait('t', bank_free[bank])
            for tile in range(2):
                i = p.t.matmul(pb[bank][0:16, :], c_ind2[:, tile * 16:tile * 16 + 16],
                               wreg[:, scan_s0 + SP_C[tile] + piece * 512:
                                    scan_s0 + SP_C[tile] + piece * 512 + 512],
                               start=(tile == 0), stop=(tile == 1))
            p.inc(i, 'mm')
            ylast = p.mk(f"y{piece}", 'mm')
        p.wait('v', ylast)
        p.v.tensor_copy(yb16[:, 0:512], pb[4][0:16, :])
        i = p.v.tensor_copy(yb16[:, 512:1024], pb[5][0:16, :])
        p.inc(i, 've')
        yb_ms = p.mk("yb", 've')
        tap("d_yb", yb16[:, :], yb_ms)
        bank_free[4] = yb_ms
        bank_free[5] = yb_ms
        slot_ms[0] = ylast
        load_weight(48)

        wh2i_last = [None, None]
        yproj_last = [None, None]
        zsil_last = [None, None]
        group_add_ms = {}
        zidx = 6
        for gidx in range(8):
            slot = 1 + (gidx % 2)
            base = WSLOT[slot]
            dep = slot_ms[slot] if gidx < 2 else group_add_ms[gidx - 2]
            p.wait('g', dep)
            for q in range(4):
                ic = gidx * 4 + q
                i = p.g.dma_start(wreg[:, base + q * 2048:base + q * 2048 + 2048],
                                  wott_ap[ic])
                p.inc(i, 'wd', 16)
            wout_ms = p.mk(f"woutg{gidx}", 'wd')
            wsl = gidx % 2
            p.wait('g', wh2i_last[wsl])
            i = p.g.dma_start(inter[0:16, IT_WH2I[wsl]:IT_WH2I[wsl] + 512],
                              wh2it.ap()[0:16, gidx * 512:gidx * 512 + 512])
            p.inc(i, 'wd', 16)
            wh2i_ms = p.mk(f"wh2i{gidx}", 'wd')
            for q in range(4):
                zc = gidx * 4 + q
                if gidx == 0:
                    zbuf = ygr[:, zc * 1024:zc * 1024 + 1024]
                    zsil_ms = zpref_ms
                elif zc in (4, 5):
                    zsl = zc % 2
                    zbuf = inter[:, IT_XCS[zsl]:IT_XCS[zsl] + 1024]
                    zsil_ms = zp4_ms if zc == 4 else zp5_ms
                else:
                    widx = 37 + zidx
                    win_tiles[widx] = load_wintile(zc, widx)
                    zidx += 1
                    grps = proj_groups(zc, widx, f"pz{zc}")
                    zsl = zc % 2
                    zbuf = inter[:, IT_XCS[zsl]:IT_XCS[zsl] + 1024]
                    for piece, (bank, grp) in enumerate(grps):
                        p.wait('s', zsil_last[zsl])
                        evac_s(bank, grp, lambda s, zbuf=zbuf, piece=piece, bank=bank:
                               s.activation(zbuf[:, piece * 512:piece * 512 + 512],
                                            pb[bank][:, :], AF.Silu), f"zsil{zc}")
                    zsil_ms = p.mk(f"zs{zc}", 'se')
                p.wait('t', wh2i_ms)
                p.wait('t', yb_ms)
                for piece in range(2):
                    bank = 4 + piece
                    p.wait('t', bank_free[bank])
                    i = p.t.matmul(pb[bank][:, :],
                                   inter[0:16, IT_WH2I[wsl] + q * 128:IT_WH2I[wsl] + q * 128 + 128],
                                   yb16[:, piece * 512:piece * 512 + 512],
                                   start=True, stop=True)
                    p.inc(i, 'mm')
                    grp = p.mk(f"yp{zc}{piece}", 'mm')
                    ypt = inter[:, IT_XI + piece * 512:IT_XI + piece * 512 + 512]
                    p.wait('s', yproj_last[piece])
                    ems = evac_s(bank, grp, lambda s, ypt=ypt, bank=bank:
                                 s.activation(ypt, pb[bank][:, :], AF.Copy),
                                 f"ype{zc}{piece}")
                    p.wait('v', ems)
                    p.wait('v', zsil_ms)
                    i = p.v.tensor_mul(ygr[:, q * 1024 + piece * 512:q * 1024 + piece * 512 + 512],
                                       zbuf[:, piece * 512:piece * 512 + 512], ypt)
                    p.inc(i, 've')
                    yproj_last[piece] = p.mk(f"ygm{zc}{piece}", 've')
                if gidx != 0:
                    zsil_last[zsl] = yproj_last[1]
            yg_ms = yproj_last[1]
            p.wait('t', wout_ms)
            p.wait('t', yg_ms)
            last_add = None
            for piece in range(2):
                for oc in range(16):
                    bank = 2 + state["woping"][0]
                    state["woping"][0] ^= 1
                    mms = [(pb[bank][:, :],
                            wreg[:, base + q * 2048 + oc * 128:base + q * 2048 + oc * 128 + 128],
                            ygr[:, q * 1024 + piece * 512:q * 1024 + piece * 512 + 512])
                           for q in range(4)]
                    grp = mm_group(bank, mms, f"zwo{gidx}")
                    sl = slice(oc * T + piece * 512, oc * T + piece * 512 + 512)
                    last_add = evac_v(bank, grp, lambda v, sl=sl, bank=bank:
                                      v.tensor_add(resid[:, sl], resid[:, sl],
                                                   pb[bank][:, :]),
                                      f"zra{gidx}")
            group_add_ms[gidx] = last_add
            if gidx == 6:
                slot_ms[1] = last_add
                load_weight(49)
            if gidx == 7:
                slot_ms[2] = last_add
                load_weight(50)
        residB_ms = group_add_ms[7]
        tap("d_r2", resid[:, :], residB_ms)

        xnfree = p.mk("xnfree1", 'mm')
        resid_final = emit_ffn(1, residB_ms, 've', xnfree)

        p.wait('y', resid_final)
        i = p.y.dma_start(oxt.ap().rearrange("(c q) t -> q c t", q=128),
                          resid[:, :].rearrange("q (c t) -> q c t", t=T))
        p.inc(i, 'sd', 16)
        if p.emit == 'y':
            p.eng.wait_ge(p.sems['sd'], p.cnt['sd'])

    with nc.Block() as block:
        @block.sync
        def _(sync):
            program(Prog('y', sync, sems))

        @block.gpsimd
        def _(gp):
            program(Prog('g', gp, sems))

        @block.tensor
        def _(tn):
            program(Prog('t', tn, sems))

        @block.vector
        def _(vc):
            program(Prog('v', vc, sems))

        @block.scalar
        def _(sc):
            program(Prog('s', sc, sems))

    return nc


_NC_CACHE = {}


def _get_nc():
    if "nc" not in _NC_CACHE:
        _NC_CACHE["nc"] = build_program()
    return _NC_CACHE["nc"]


def kernel(L0_attn_x, L0_attn_w_norm, L0_attn_W_in, L0_attn_w_conv, L0_attn_dt_bias,
           L0_attn_A_log, L0_attn_W_h2i, L0_attn_W_out, L0_ffn_w_gate, L0_ffn_wi_0,
           L0_ffn_wi_1, L0_ffn_wo):
    f32 = np.float32
    x = np.asarray(L0_attn_x, f32)
    W_in = np.asarray(L0_attn_W_in, f32)
    w_conv = np.asarray(L0_attn_w_conv, f32)
    W_h2i = np.asarray(L0_attn_W_h2i, f32)
    W_out = np.asarray(L0_attn_W_out, f32)
    w_gate = np.asarray(L0_ffn_w_gate, f32)
    wi_0 = np.asarray(L0_ffn_wi_0, f32)
    wi_1 = np.asarray(L0_ffn_wi_1, f32)
    wo = np.asarray(L0_ffn_wo, f32)

    wi0h = wi_0.reshape(8, 16, 128, 2, 512).transpose(0, 3, 2, 1, 4)
    wi1h = wi_1.reshape(8, 16, 128, 2, 512).transpose(0, 3, 2, 1, 4)
    wih = np.ascontiguousarray(
        np.stack([wi0h[:, 0], wi0h[:, 1], wi1h[:, 0], wi1h[:, 1]], axis=1)
    ).reshape(32, 128, 16 * 512)
    woh = np.ascontiguousarray(
        wo.reshape(8, 8, 128, 2, 1024).transpose(0, 3, 2, 1, 4)
    ).reshape(16, 128, 8 * 1024)
    Wp = np.zeros((8832, 2048), f32)
    Wp[0:8192] = W_in[0:8192]
    Wp[8192:8208] = W_in[8192:8208]
    Wp[8320:8576] = W_in[8208:8464]
    Wp[8576:8832] = W_in[8464:8720]
    wint = np.ascontiguousarray(
        Wp.T.reshape(16, 128, 69, 128).transpose(2, 1, 0, 3)
    ).reshape(69, 128, 16 * 128)
    wott = np.ascontiguousarray(W_out.T).reshape(32, 128, 2048)
    wh2it = np.ascontiguousarray(W_h2i.T)
    wct = np.ascontiguousarray(
        w_conv.reshape(4096, 4).reshape(32, 128, 4).transpose(1, 0, 2)
    ).reshape(128, 128)
    wnt = np.ascontiguousarray(np.asarray(L0_attn_w_norm, f32).reshape(16, 128).T)
    wgt = np.ascontiguousarray(
        w_gate.reshape(16, 128, 8).transpose(1, 0, 2)
    ).reshape(128, 128)
    dtbv = np.asarray(L0_attn_dt_bias, f32).reshape(16, 1)
    alogv = np.asarray(L0_attn_A_log, f32).reshape(16, 1)
    f = np.arange(4096)
    indt = np.ascontiguousarray(
        (f[:, None] // 256 == np.arange(16)[None, :]).astype(f32)
        .reshape(32, 128, 16).transpose(1, 0, 2)
    ).reshape(128, 512)
    pidx = np.arange(128)
    ind2 = np.zeros((128, 32), f32)
    ind2[:, 0:16] = (pidx[:, None] // 16 == np.arange(16)[None, :])
    ind2[:, 16:32] = (pidx[:, None] // 16 + 8 == np.arange(16)[None, :])
    rrep = np.zeros((16, 256), f32)
    for t in range(2):
        rrep[:, t * 128:(t + 1) * 128] = (
            np.arange(16)[:, None] == (t * 8 + pidx[None, :] // 16))
    ones = np.ones((1, 128), f32)
    idm = np.eye(128, dtype=f32)

    shared = dict(wg=wgt, wih=wih, woh=woh, wint=wint, wott=wott, wh2it=wh2it,
                  wct=wct, wnt=wnt, dtb=dtbv, alog=alogv, indt=indt, ind2=ind2,
                  rrep=rrep, ones_f=ones, ones_b=ones, ones_c=np.ones((128, 1), f32),
                  epsr=np.array([[1e-5, -0.5, 1.0 + 1.1920929e-07,
                                  1e-5 * 1.1920929e-07, 0, 0, 0, 0]], f32),
                  id_f=idm, id_b=idm)

    in_maps = []
    for k in range(8):
        beta, pos = k // 4, k % 4
        xk = np.ascontiguousarray(x[beta, pos * T:(pos + 1) * T, :].T)
        selm = np.zeros((128, 8), f32)
        selprev = np.zeros((128, 8), f32)
        for j in range(8):
            if j // 4 == beta and j % 4 < pos:
                selm[:, j] = 1.0
        if pos > 0:
            selprev[:, k - 1] = 1.0
        selm16 = np.repeat(selm, 2, axis=1)
        m = dict(shared)
        m.update(xin=xk, selm=selm, invselm=np.ascontiguousarray(1.0 - selm),
                 selprev=selprev, selm16=selm16,
                 iselm16=np.ascontiguousarray(1.0 - selm16))
        in_maps.append(m)

    nc = _get_nc()
    res = run_bass_kernel_spmd(nc, in_maps, core_ids=list(range(8)))
    out = np.empty((2, 4096, 2048), f32)
    for k in range(8):
        beta, pos = k // 4, k % 4
        out[beta, pos * T:(pos + 1) * T, :] = res.results[k]["oxt"].T
    return out
